# version 03
# baseline (speedup 1.0000x reference)
from contextlib import ExitStack
import numpy as np
import ml_dtypes
import concourse.bass as bass
import concourse.mybir as mybir
from concourse.bass_utils import run_bass_kernel_spmd

F32 = mybir.dt.float32
BF16 = mybir.dt.bfloat16
F16 = mybir.dt.float16
I32 = mybir.dt.int32
U8 = mybir.dt.uint8
AF = mybir.ActivationFunctionType
ALU = mybir.AluOpType
BF = ml_dtypes.bfloat16

NC = 8
S = 8192
TCK = 1024
HALO = 4
TH = TCK + HALO
DM = 2048
DI = 4096
DSTATE = 16
EPS5 = 1e-5
EPS6 = 1e-6

NHEADS = 64
HEADDIM = 64
INTER = 4096
NGROUPS = 8
NSTATE = 128
CONVD = 6144
I2 = 5632

NCP_TH = [(0, 512), (512, 512), (1024, TH - 1024)]
NCP_TC = [(0, 512), (512, 512)]



class _Unit:
    __slots__ = ("eng", "fn", "deps", "dma", "n_dma", "flagged", "count_after",
                 "sem_idx", "sem_target")

    def __init__(self, eng, fn, deps, dma, n_dma):
        self.eng = eng
        self.fn = fn
        self.deps = [d for d in deps if d is not None]
        self.dma = dma
        self.n_dma = n_dma
        self.flagged = False
        self.count_after = None
        self.sem_idx = None
        self.sem_target = None

    def dep(self, *units):
        for u in units:
            if u is not None:
                self.deps.append(u)
        return self


class Emitter:
    ENGS = ("sync", "gpsimd", "vector", "scalar", "tensor")
    NPOOL = 12

    def __init__(self, nc):
        self.nc = nc
        self.units = {e: [] for e in self.ENGS}
        self.pool_i = 0
        self.pool_totals = [0] * self.NPOOL
        self.pool_last_unit = [None] * self.NPOOL

    def unit(self, eng, fn, deps=(), dma=False, n_dma=1):
        u = _Unit(eng, fn, deps, dma, n_dma)
        if dma:
            i = self.pool_i % self.NPOOL
            self.pool_i += 1
            u.sem_idx = i
            prev = self.pool_last_unit[i]
            if prev is not None:
                u.deps.append(prev)
            self.pool_totals[i] += 16 * n_dma
            u.sem_target = self.pool_totals[i]
            self.pool_last_unit[i] = u
        self.units[eng].append(u)
        return u

    def sync_dma(self, fn, deps=(), n=1):
        return self.unit("sync", fn, deps, dma=True, n_dma=n)

    def gp_dma(self, fn, deps=(), n=1):
        return self.unit("gpsimd", fn, deps, dma=True, n_dma=n)

    def v(self, fn, deps=()):
        return self.unit("vector", fn, deps)

    def s(self, fn, deps=()):
        return self.unit("scalar", fn, deps)

    def t(self, fn, deps=()):
        return self.unit("tensor", fn, deps)

    def g(self, fn, deps=()):
        return self.unit("gpsimd", fn, deps)

    def realize(self, block, sems, dma_sems):
        for e in self.ENGS:
            for u in self.units[e]:
                for d in u.deps:
                    if d.dma or d.eng != u.eng:
                        d.flagged = True
        for e in self.ENGS:
            c = 0
            for u in self.units[e]:
                if u.flagged and not u.dma:
                    c += 1
                u.count_after = c
        em = self

        def mk(engname):
            def body(eng):
                waited = {}
                for u in em.units[engname]:
                    for d in u.deps:
                        if d.dma:
                            key = ("dma", d.sem_idx)
                            tgt = d.sem_target
                            semh = dma_sems[d.sem_idx]
                        else:
                            if d.eng == engname:
                                continue
                            key = d.eng
                            tgt = d.count_after
                            semh = sems[d.eng]
                        if waited.get(key, -1) >= tgt:
                            continue
                        eng.wait_ge(semh, tgt)
                        waited[key] = tgt
                    ins = u.fn(eng)
                    if u.dma:
                        if not isinstance(ins, (list, tuple)):
                            ins = [ins]
                        assert len(ins) == u.n_dma, (
                            f"{engname} dma unit: expected {u.n_dma} dmas, got {len(ins)}")
                        for i_ in ins:
                            i_.then_inc(dma_sems[u.sem_idx], 16)
                    elif u.flagged:
                        if isinstance(ins, (list, tuple)):
                            ins = ins[-1]
                        if ins is None:
                            ins = eng.nop()
                        ins.then_inc(sems[u.eng], 1)
            return body

        block.sync(mk("sync"))
        block.gpsimd(mk("gpsimd"))
        block.vector(mk("vector"))
        block.scalar(mk("scalar"))
        block.tensor(mk("tensor"))


class Slots:

    def __init__(self, tiles):
        self.tiles = tiles
        self.i = 0
        self.reader = [None] * len(tiles)
        self.writer = [None] * len(tiles)

    def next(self):
        s = self.i % len(self.tiles)
        self.i += 1
        return s

    def tile(self, s):
        return self.tiles[s]



def lhsT_tiles(w):
    K, M = w.shape
    assert K % 128 == 0 and M % 128 == 0
    return np.ascontiguousarray(
        w.reshape(K // 128, 128, M // 128, 128).transpose(2, 0, 1, 3))


def percol(v):
    n = v.shape[0]
    return np.ascontiguousarray(v.reshape(n // 128, 128).T)


def tok_slices(x_t, halo):
    D = x_t.shape[0]
    out = []
    for c in range(NC):
        lo = c * TCK - halo
        sl = np.zeros((D, TCK + halo), x_t.dtype)
        if lo < 0:
            sl[:, halo:] = x_t[:, c * TCK:(c + 1) * TCK]
        else:
            sl[:, :] = x_t[:, lo:(c + 1) * TCK]
        out.append(np.ascontiguousarray(sl))
    return out



def build_A():
    nc = bass.Bass("TRN2", target_bir_lowering=False, debug=False)
    xT_h = nc.dram_tensor("xT", [16, 128, TH], F32, kind="ExternalInput")
    wn_h = nc.dram_tensor("wn", [128, 16], F32, kind="ExternalInput")
    cwt_h = nc.dram_tensor("cwt", [128, 128], F32, kind="ExternalInput")
    cb_h = nc.dram_tensor("cb", [128, 32], F32, kind="ExternalInput")
    bdt_h = nc.dram_tensor("bdt", [128, 32], F32, kind="ExternalInput")
    onesc_h = nc.dram_tensor("onesc", [128, 1], F32, kind="ExternalInput")
    onesr_h = nc.dram_tensor("onesr", [1, 128], F32, kind="ExternalInput")
    win_h = nc.dram_tensor("win_t", [64, 16, 128, 128], F32, kind="ExternalInput")
    wxp_h = nc.dram_tensor("wxp_t", [2, 32, 128, 128], F32, kind="ExternalInput")
    wdt_h = nc.dram_tensor("wdt_t", [32, 1, 128, 128], F32, kind="ExternalInput")

    xs_h = nc.dram_tensor("xs16", [32, 128, TCK], BF16, kind="ExternalOutput")
    res_h = nc.dram_tensor("res16", [32, 128, TCK], BF16, kind="ExternalOutput")
    delta_h = nc.dram_tensor("delta", [32, 128, TCK], F32, kind="ExternalOutput")
    bc_h = nc.dram_tensor("bc", [32, TCK], F32, kind="ExternalOutput")

    st = ExitStack()
    with st:
        def SB(name, shape, dt):
            return st.enter_context(nc.sbuf_tensor(name, shape, dt))

        def PSUM(name, shape):
            return st.enter_context(nc.psum_tensor(name, shape, F32))

        c_wn = SB("c_wn", [128, 16], F32)
        c_cwt = SB("c_cwt", [128, 128], F32)
        c_cb = SB("c_cb", [128, 32], F32)
        c_bdt = SB("c_bdt", [128, 32], F32)
        c_onesc = SB("c_onesc", [128, 1], F32)
        c_onesr = SB("c_onesr", [1, 128], F32)
        xbuf0 = SB("xbuf0", [128, TH], F32)
        xbuf1 = SB("xbuf1", [128, TH], F32)
        sq = SB("sq", [128, TH], F32)
        rrow = SB("rrow", [1, TH], F32)
        rstd_b = SB("rstd_b", [128, TH], F32)
        xn16 = SB("xn16", [128, 16, TH], BF16)
        wbuf0 = SB("wbuf0", [128, 2048], BF16)
        wbuf1 = SB("wbuf1", [128, 2048], BF16)
        xsraw = SB("xsraw", [128, TH], F32)
        cacc = SB("cacc", [128, TCK], F32)
        res0 = SB("res0", [128, TCK], BF16)
        res1 = SB("res1", [128, TCK], BF16)
        xs16s = SB("xs16s", [128, 32, TCK], BF16)
        wxp16 = SB("wxp16", [128, 2 * 32 * 128], BF16)
        wdt16 = SB("wdt16", [128, 32 * 128], BF16)
        dtr16 = SB("dtr16", [128, TCK], BF16)
        bcbuf = SB("bcbuf", [32, TCK], F32)
        sp1 = SB("sp1", [128, TCK], F32)
        sp2 = SB("sp2", [128, TCK], F32)
        delt0 = SB("delt0", [128, TCK], F32)
        delt1 = SB("delt1", [128, TCK], F32)
        ps_s0 = PSUM("ps_s0", [1, 512])
        ps_s1 = PSUM("ps_s1", [1, 512])
        ps_s2 = PSUM("ps_s2", [1, 512])
        ps_m0 = PSUM("ps_m0", [128, 512])
        ps_m1 = PSUM("ps_m1", [128, 512])
        ps_m2 = PSUM("ps_m2", [128, 512])
        ps_m3 = PSUM("ps_m3", [128, 512])
        sems = {e: st.enter_context(nc.semaphore("sem_" + e)) for e in Emitter.ENGS}
        dma_sems = [st.enter_context(nc.semaphore(f"dsem{i}")) for i in range(Emitter.NPOOL)]
        block = st.enter_context(nc.Block())
        em = Emitter(nc)
        psum_stats = [ps_s0, ps_s1, ps_s2]
        xbufs = [xbuf0, xbuf1]
        wbufs = [wbuf0, wbuf1]
        ps = Slots([ps_m0, ps_m1, ps_m2, ps_m3])
        resb = [res0, res1]

        consts = em.sync_dma(lambda sy: [
            sy.dma_start(out=c_wn[:], in_=wn_h.ap()),
            sy.dma_start(out=c_cwt[:], in_=cwt_h.ap()),
            sy.dma_start(out=c_cb[:], in_=cb_h.ap()),
            sy.dma_start(out=c_bdt[:], in_=bdt_h.ap()),
            sy.dma_start(out=c_onesc[:], in_=onesc_h.ap()),
            sy.dma_start(out=c_onesr[:], in_=onesr_h.ap()),
        ], n=6)
        ldwx = em.gp_dma(lambda g: g.dma_start(
            out=wxp16[:].rearrange("p (g j) -> p g j", j=128),
            in_=wxp_h.ap().rearrange("m kc p j -> p (m kc) j")))
        ldwd = em.gp_dma(lambda g: g.dma_start(
            out=wdt16[:].rearrange("p (dc j) -> p dc j", j=128),
            in_=wdt_h.ap().rearrange("dc one p j -> p (dc one) j")))

        sq_done = [None] * 16
        stat_done = [None] * 16
        for kc in range(16):
            ld = em.sync_dma(
                (lambda kc: lambda sy: sy.dma_start(
                    out=xbufs[kc % 2][:], in_=xT_h.ap()[kc]))(kc),
                deps=[sq_done[kc - 2] if kc >= 2 else None])
            sq_done[kc] = em.s(
                (lambda kc: lambda sc: sc.activation(
                    out=sq[:], in_=xbufs[kc % 2][:], func=AF.Square))(kc),
                deps=[ld, stat_done[kc - 1] if kc >= 1 else None])
            stat_done[kc] = em.t(
                (lambda kc: lambda t: [
                    t.matmul(psum_stats[i][:, 0:w], c_onesc[:], sq[:, o:o + w],
                             start=(kc == 0), stop=(kc == 15))
                    for i, (o, w) in enumerate(NCP_TH)])(kc),
                deps=[sq_done[kc], consts])

        r1c = em.s(lambda sc: [
            sc.copy(out=rrow[:, o:o + w], in_=psum_stats[i][:, 0:w])
            for i, (o, w) in enumerate(NCP_TH)],
            deps=[stat_done[15]])
        r1 = em.v(lambda v: v.tensor_scalar(out=rrow[:], in0=rrow[:], scalar1=1.0 / DM,
                                            scalar2=EPS5, op0=ALU.mult, op1=ALU.add),
                  deps=[r1c])
        r2 = em.s(lambda sc: sc.activation(out=rrow[:], in_=rrow[:], func=AF.Sqrt),
                  deps=[r1])
        r3 = em.v(lambda v: v.reciprocal(out=rrow[:], in_=rrow[:]), deps=[r2])
        bc_slots = [ps.next() for _ in range(3)]
        r4 = em.t(lambda t: [
            t.matmul(ps.tile(bc_slots[i])[:, 0:w], c_onesr[:], rrow[:, o:o + w],
                     start=True, stop=True)
            for i, (o, w) in enumerate(NCP_TH)], deps=[r3, consts])
        r5 = em.s(lambda sc: [
            sc.copy(out=rstd_b[:, o:o + w], in_=ps.tile(bc_slots[i])[:, 0:w])
            for i, (o, w) in enumerate(NCP_TH)], deps=[r4])
        for s_ in bc_slots:
            ps.reader[s_] = r5

        xn_done = [None] * 16
        for kc in range(16):
            ld = em.sync_dma(
                (lambda kc: lambda sy: sy.dma_start(
                    out=xbufs[kc % 2][:], in_=xT_h.ap()[kc]))(kc),
                deps=[xn_done[kc - 2] if kc >= 2 else sq_done[15 - (1 - kc)]])
            xn_done[kc] = em.v(
                (lambda kc: lambda v: v.scalar_tensor_tensor(
                    out=xn16[:, kc, :], in0=xbufs[kc % 2][:],
                    scalar=c_wn[:, kc:kc + 1], in1=rstd_b[:],
                    op0=ALU.mult, op1=ALU.mult))(kc),
                deps=[ld, r5, consts])

        ldw_units = [None] * 64
        conv_done = [None] * 32
        resdma = [None] * 32
        last_mm_user_of_wbuf = [None, None]
        for mc in range(64):
            ldw_units[mc] = em.gp_dma(
                (lambda mc: lambda g: g.dma_start(
                    out=wbufs[mc % 2][:].rearrange("p (kc j) -> p kc j", j=128),
                    in_=win_h.ap()[mc].rearrange("kc p j -> p kc j")))(mc),
                deps=[last_mm_user_of_wbuf[mc % 2]])
            evacs = []
            for i, (o, w) in enumerate(NCP_TH):
                s_ = ps.next()
                mm = em.t(
                    (lambda mc, o, w, s_: lambda t: [
                        t.matmul(ps.tile(s_)[:, 0:w],
                                 wbufs[mc % 2][:, kc * 128:(kc + 1) * 128],
                                 xn16[:, kc, o:o + w],
                                 start=(kc == 0), stop=(kc == 15))
                        for kc in range(16)])(mc, o, w, s_),
                    deps=[ldw_units[mc], xn_done[15], ps.reader[s_]])
                if mc < 32:
                    ev = em.s(
                        (lambda o, w, s_: lambda sc: sc.copy(
                            out=xsraw[:, o:o + w], in_=ps.tile(s_)[:, 0:w]))(o, w, s_),
                        deps=[mm, conv_done[mc - 1] if (i == 0 and mc >= 1) else None])
                else:
                    def resf(o, w, s_, mc):
                        def fn(sc):
                            if o == 0:
                                return sc.activation(out=resb[mc % 2][:, 0:w - HALO],
                                                     in_=ps.tile(s_)[:, HALO:w],
                                                     func=AF.Silu)
                            return sc.activation(
                                out=resb[mc % 2][:, o - HALO:o - HALO + w],
                                in_=ps.tile(s_)[:, 0:w], func=AF.Silu)
                        return fn
                    ev = em.s(resf(o, w, s_, mc),
                              deps=[mm, resdma[mc - 2 - 32] if (i == 0 and mc >= 34) else None])
                ps.reader[s_] = ev
                evacs.append(ev)
            last_mm_user_of_wbuf[mc % 2] = mm
            if mc < 32:
                cd = em.v(
                    (lambda mc: lambda v: [
                        v.tensor_scalar(out=cacc[:], in0=xsraw[:, 1:1 + TCK],
                                        scalar1=c_cwt[:, mc * 4:mc * 4 + 1],
                                        scalar2=c_cb[:, mc:mc + 1],
                                        op0=ALU.mult, op1=ALU.add),
                        *[v.scalar_tensor_tensor(
                            out=cacc[:], in0=xsraw[:, 1 + wt:1 + wt + TCK],
                            scalar=c_cwt[:, mc * 4 + wt:mc * 4 + wt + 1],
                            in1=cacc[:], op0=ALU.mult, op1=ALU.add)
                          for wt in range(1, 4)]])(mc),
                    deps=evacs + [conv_done[mc - 1] if mc >= 1 else None])
                conv_done[mc] = em.s(
                    (lambda mc: lambda sc: sc.activation(
                        out=xs16s[:, mc, :], in_=cacc[:], func=AF.Silu))(mc),
                    deps=[cd])
            else:
                resdma[mc - 32] = em.sync_dma(
                    (lambda mc: lambda sy: sy.dma_start(
                        out=res_h.ap()[mc - 32], in_=resb[mc % 2][:]))(mc),
                    deps=evacs)

        xs_out = em.sync_dma(lambda sy: [
            sy.dma_start(out=xs_h.ap()[mcc], in_=xs16s[:, mcc, :])
            for mcc in range(32)], deps=list(conv_done), n=32)

        xdbc_ev = {}
        for m2 in range(2):
            for i, (o, w) in enumerate(NCP_TC):
                s_ = ps.next()
                mm = em.t(
                    (lambda m2, o, w, s_: lambda t: [
                        t.matmul(ps.tile(s_)[:, 0:w],
                                 wxp16[:, (m2 * 32 + kc) * 128:(m2 * 32 + kc + 1) * 128],
                                 xs16s[:, kc, o:o + w],
                                 start=(kc == 0), stop=(kc == 31))
                        for kc in range(32)])(m2, o, w, s_),
                    deps=[ldwx, ps.reader[s_]] + list(conv_done))
                if m2 == 0:
                    ev = em.s(
                        (lambda o, w, s_: lambda sc: sc.copy(
                            out=dtr16[:, o:o + w], in_=ps.tile(s_)[:, 0:w]))(o, w, s_),
                        deps=[mm])
                else:
                    ev = em.v(
                        (lambda o, w, s_: lambda v: v.tensor_copy(
                            out=bcbuf[:, o:o + w], in_=ps.tile(s_)[0:32, 0:w]))(o, w, s_),
                        deps=[mm])
                ps.reader[s_] = ev
                xdbc_ev[(m2, i)] = ev
        bc_out = em.sync_dma(
            lambda sy: sy.dma_start(out=bc_h.ap(), in_=bcbuf[:]),
            deps=[xdbc_ev[(1, 0)], xdbc_ev[(1, 1)]])

        deltb = [delt0, delt1]
        add_done = [None] * 32
        ddma = [None] * 32
        for dc in range(32):
            acts = []
            for i, (o, w) in enumerate(NCP_TC):
                s_ = ps.next()
                mm = em.t(
                    (lambda dc, o, w, s_: lambda t: t.matmul(
                        ps.tile(s_)[:, 0:w], wdt16[:, dc * 128:(dc + 1) * 128],
                        dtr16[:, o:o + w], start=True, stop=True))(dc, o, w, s_),
                    deps=[ldwd, xdbc_ev[(0, 0)], xdbc_ev[(0, 1)], ps.reader[s_]])
                act = em.s(
                    (lambda dc, o, w, s_: lambda sc: [
                        sc.activation(out=sp1[:, o:o + w], in_=ps.tile(s_)[:, 0:w],
                                      func=AF.Abs, bias=c_bdt[:, dc:dc + 1]),
                        sc.activation(out=sp1[:, o:o + w], in_=sp1[:, o:o + w],
                                      func=AF.Exp, scale=-1.0),
                        sc.activation(out=sp1[:, o:o + w], in_=sp1[:, o:o + w],
                                      func=AF.Ln, bias=1.0),
                        sc.activation(out=sp2[:, o:o + w], in_=ps.tile(s_)[:, 0:w],
                                      func=AF.Relu, bias=c_bdt[:, dc:dc + 1]),
                    ][-1])(dc, o, w, s_),
                    deps=[mm, add_done[dc - 1] if (i == 0 and dc >= 1) else None])
                ps.reader[s_] = act
                acts.append(act)
            add_done[dc] = em.v(
                (lambda dc: lambda v: v.tensor_tensor(
                    out=deltb[dc % 2][:], in0=sp1[:], in1=sp2[:], op=ALU.add))(dc),
                deps=acts + [ddma[dc - 2] if dc >= 2 else None])
            ddma[dc] = em.sync_dma(
                (lambda dc: lambda sy: sy.dma_start(
                    out=delta_h.ap()[dc], in_=deltb[dc % 2][:]))(dc),
                deps=[add_done[dc]])

        em.g(lambda g: g.engine_nop(),
             deps=[xs_out, bc_out] + ddma + resdma)

        em.realize(block, sems, dma_sems)
    return nc


def host_inputs_A(inputs):
    x = np.asarray(inputs["L0_attn_x"])[0]
    xT = np.ascontiguousarray(x.T)
    xs = tok_slices(xT, HALO)
    w_in = np.asarray(inputs["L0_attn_W_in"])
    w_xp = np.asarray(inputs["L0_attn_W_xp"])
    w_xp_pad = np.zeros((DI, 256), np.float32)
    w_xp_pad[:, :160] = w_xp
    w_dt = np.asarray(inputs["L0_attn_W_dt"])
    conv_w = np.asarray(inputs["L0_attn_conv_w"])
    cwt = np.ascontiguousarray(
        conv_w.T.reshape(32, 128, 4).transpose(1, 0, 2).reshape(128, 128))
    shared = {
        "wn": percol(np.asarray(inputs["L0_attn_w_norm"])),
        "cwt": cwt,
        "cb": percol(np.asarray(inputs["L0_attn_conv_b"])),
        "bdt": percol(np.asarray(inputs["L0_attn_b_dt"])),
        "onesc": np.ones((128, 1), np.float32),
        "onesr": np.ones((1, 128), np.float32),
        "win_t": lhsT_tiles(w_in),
        "wxp_t": lhsT_tiles(w_xp_pad),
        "wdt_t": lhsT_tiles(w_dt),
    }
    maps = []
    for c in range(NC):
        m = dict(shared)
        m["xT"] = xs[c].reshape(16, 128, TH)
        maps.append(m)
    return maps


def run_A(inputs):
    nc = build_A()
    maps = host_inputs_A(inputs)
    res = run_bass_kernel_spmd(nc, maps, core_ids=list(range(NC)))
    return res.results


def kernel(**inputs):
    resG = run_G(inputs)
    resA = run_A(inputs)
    resB = run_B(resA, inputs)
    resC1 = run_C1(resA, resB, inputs)
    resC2 = run_C2(resC1, inputs)
    resD = run_D(resC2, inputs)
    resE = run_E(resD, inputs)
    resF1 = run_F1(resD, resE, resC2, inputs)
    resF2 = run_F2(resF1, resG, inputs)
    out = np.concatenate(
        [np.asarray(rr["outp"]).reshape(DM, TCK) for rr in resF2], axis=1).T[None]
    return np.ascontiguousarray(out.astype(np.float32))



def build_B():
    nc = bass.Bass("TRN2", target_bir_lowering=False, debug=False)
    delta_h = nc.dram_tensor("delta_s", [4, 128, S], F32, kind="ExternalInput")
    xs_h = nc.dram_tensor("xs_s", [4, 128, S], BF16, kind="ExternalInput")
    bf_h = nc.dram_tensor("bfull", [16, S], F32, kind="ExternalInput")
    cf_h = nc.dram_tensor("cfull", [16, S], F32, kind="ExternalInput")
    alog_h = nc.dram_tensor("alog_s", [128, 64], F32, kind="ExternalInput")
    d_h = nc.dram_tensor("d_s", [128, 4], F32, kind="ExternalInput")
    g_h = nc.dram_tensor("y16", [4, 128, S], BF16, kind="ExternalOutput")

    st = ExitStack()
    with st:
        def SB(name, shape, dt):
            return st.enter_context(nc.sbuf_tensor(name, shape, dt))
        deltab = SB("deltab", [128, S], F32)
        xsb = SB("xsb", [128, S], BF16)
        dub = SB("dub", [128, S], BF16)
        yb = SB("yb", [128, S], F32)
        dab = SB("dab", [128, S], F32)
        dbub = SB("dbub", [128, S], BF16)
        hb = SB("hb", [128, S], BF16)
        bbc = SB("bbc", [128, S], BF16)
        cbc = SB("cbc", [128, S], BF16)
        asb = SB("asb", [128, 64], F32)
        dsb = SB("dsb", [128, 4], F32)
        sems = {e: st.enter_context(nc.semaphore("sem_" + e)) for e in Emitter.ENGS}
        dma_sems = [st.enter_context(nc.semaphore(f"dsem{i}")) for i in range(Emitter.NPOOL)]
        block = st.enter_context(nc.Block())
        em = Emitter(nc)

        lda = em.sync_dma(lambda sy: [
            sy.dma_start(out=asb[:], in_=alog_h.ap()),
            sy.dma_start(out=dsb[:], in_=d_h.ap())], n=2)
        a1 = em.s(lambda sc: sc.activation(out=asb[:], in_=asb[:], func=AF.Exp),
                  deps=[lda])
        a2 = em.v(lambda v: v.tensor_scalar(out=asb[:], in0=asb[:], scalar1=-1.0,
                                            scalar2=None, op0=ALU.mult), deps=[a1])

        gdma = [None] * 4
        for dc in range(4):
            ldd = em.sync_dma(
                (lambda dc: lambda sy: [
                    sy.dma_start(out=deltab[:], in_=delta_h.ap()[dc]),
                    sy.dma_start(out=xsb[:], in_=xs_h.ap()[dc])])(dc),
                deps=[gdma[dc - 1] if dc >= 1 else None], n=2)
            du = em.v((lambda: lambda v: v.tensor_tensor(
                out=dub[:], in0=deltab[:], in1=xsb[:], op=ALU.mult))(),
                deps=[ldd])
            yinit = em.v((lambda dc: lambda v: v.tensor_scalar(
                out=yb[:], in0=xsb[:], scalar1=dsb[:, dc:dc + 1], scalar2=None,
                op0=ALU.mult))(dc), deps=[ldd, lda])
            prev_acc = yinit
            prev_dbu = None
            prev_prod = None
            prev_scan = None
            for n in range(DSTATE):
                ldb = em.gp_dma(
                    (lambda n: lambda g: [
                        g.dma_start(out=bbc[:], in_=bf_h.ap()[n:n + 1, :]
                                    .broadcast_to([128, S])),
                        g.dma_start(out=cbc[:], in_=cf_h.ap()[n:n + 1, :]
                                    .broadcast_to([128, S]))])(n),
                    deps=[prev_dbu, prev_prod], n=2)
                da = em.s(
                    (lambda dc, n: lambda sc: sc.activation(
                        out=dab[:], in_=deltab[:], func=AF.Exp,
                        scale=asb[:, dc * 16 + n:dc * 16 + n + 1]))(dc, n),
                    deps=[ldd, a2, prev_scan])
                dbu = em.v(
                    lambda v: v.tensor_tensor(out=dbub[:], in0=dub[:], in1=bbc[:],
                                              op=ALU.mult),
                    deps=[du, ldb, prev_scan])
                hsc = em.v(
                    lambda v: v.tensor_tensor_scan(out=hb[:], data0=dab[:],
                                                   data1=dbub[:], initial=0.0,
                                                   op0=ALU.mult, op1=ALU.add),
                    deps=[da, dbu, prev_prod])
                prod = em.g(
                    lambda g: g.tensor_tensor(out=hb[:], in0=hb[:], in1=cbc[:],
                                              op=ALU.mult),
                    deps=[hsc, ldb])
                prev_acc = em.g(
                    lambda g: g.tensor_tensor(out=yb[:], in0=yb[:], in1=hb[:],
                                              op=ALU.add),
                    deps=[prod, prev_acc])
                prev_dbu = dbu
                prev_prod = prev_acc
                prev_scan = hsc
            gat = em.v(
                lambda v: v.tensor_copy(out=dub[:], in_=yb[:]),
                deps=[prev_acc, du])
            gdma[dc] = em.sync_dma(
                (lambda dc: lambda sy: sy.dma_start(out=g_h.ap()[dc], in_=dub[:]))(dc),
                deps=[gat])
        em.g(lambda g: g.engine_nop(), deps=gdma)
        em.realize(block, sems, dma_sems)
    return nc


def host_inputs_B(resA, inputs):
    delta_all = np.concatenate(
        [np.asarray(r["delta"]).reshape(DI, TCK) for r in resA], axis=1)
    xs_all = np.concatenate(
        [np.asarray(r["xs16"]).view(BF).reshape(DI, TCK) for r in resA], axis=1)
    res_all = np.concatenate(
        [np.asarray(r["res16"]).view(BF).reshape(DI, TCK) for r in resA], axis=1)
    bc_all = np.concatenate(
        [np.asarray(r["bc"]).reshape(32, TCK) for r in resA], axis=1)
    alog = np.asarray(inputs["L0_attn_A_log"])
    dvec = np.asarray(inputs["L0_attn_D"])
    maps = []
    for c in range(NC):
        sl = slice(c * 512, (c + 1) * 512)
        m = {
            "delta_s": np.ascontiguousarray(delta_all[sl]).reshape(4, 128, S),
            "xs_s": np.ascontiguousarray(xs_all[sl]).reshape(4, 128, S),
            "bfull": np.ascontiguousarray(bc_all[0:16]),
            "cfull": np.ascontiguousarray(bc_all[16:32]),
            "alog_s": np.ascontiguousarray(
                alog[sl].reshape(4, 128, 16).transpose(1, 0, 2).reshape(128, 64)),
            "d_s": np.ascontiguousarray(dvec[sl].reshape(4, 128).T),
        }
        maps.append(m)
    return maps


def run_B(resA, inputs):
    nc = build_B()
    maps = host_inputs_B(resA, inputs)
    res = run_bass_kernel_spmd(nc, maps, core_ids=list(range(NC)))
    return res.results



def build_C1():
    nc = bass.Bass("TRN2", target_bir_lowering=False, debug=False)
    y_h = nc.dram_tensor("y16", [32, 128, TCK], BF16, kind="ExternalInput")
    res_h = nc.dram_tensor("res16", [32, 128, TCK], BF16, kind="ExternalInput")
    xT_h = nc.dram_tensor("xT", [16, 128, TCK], F32, kind="ExternalInput")
    wout_h = nc.dram_tensor("wout_t", [16, 32, 128, 128], F32, kind="ExternalInput")
    onesc_h = nc.dram_tensor("onesc", [128, 1], F32, kind="ExternalInput")
    onesr_h = nc.dram_tensor("onesr", [1, 128], F32, kind="ExternalInput")
    x1_h = nc.dram_tensor("x1", [16, 128, TCK], F32, kind="ExternalOutput")
    xn1_h = nc.dram_tensor("xn1", [16, 128, TCK], BF16, kind="ExternalOutput")

    st = ExitStack()
    with st:
        def SB(name, shape, dt):
            return st.enter_context(nc.sbuf_tensor(name, shape, dt))

        def PSUM(name, shape, dt=F32):
            return st.enter_context(nc.psum_tensor(name, shape, dt))
        c_onesc = SB("c_onesc", [128, 1], F32)
        c_onesr = SB("c_onesr", [1, 128], F32)
        ybuf = [SB("ybuf0", [128, TCK], BF16), SB("ybuf1", [128, TCK], BF16)]
        rbuf = [SB("rbuf0", [128, TCK], BF16), SB("rbuf1", [128, TCK], BF16)]
        gated = SB("gated", [128, 32, TCK], BF16)
        xs_ = SB("x1pre", [128, 16, TCK], F32)
        wbuf = [SB("wbuf0", [128, 32 * 128], BF16), SB("wbuf1", [128, 32 * 128], BF16)]
        sq = SB("sq", [128, TCK], F32)
        rows = SB("rows", [1, 4 * TCK], F32)
        mean_b = SB("mean_b", [128, TCK], F32)
        rstd_b = SB("rstd_b", [128, TCK], F32)
        xn1b = SB("xn1b", [128, TCK], BF16)
        psmm = Slots([PSUM(f"ps_m{i}", [128, 512]) for i in range(4)])
        ps_st = [PSUM(f"ps_s{i}", [1, 512]) for i in range(4)]
        sems = {e: st.enter_context(nc.semaphore("sem_" + e)) for e in Emitter.ENGS}
        dma_sems = [st.enter_context(nc.semaphore(f"dsem{i}")) for i in range(Emitter.NPOOL)]
        block = st.enter_context(nc.Block())
        em = Emitter(nc)

        consts = em.sync_dma(lambda sy: [
            sy.dma_start(out=c_onesc[:], in_=onesc_h.ap()),
            sy.dma_start(out=c_onesr[:], in_=onesr_h.ap())], n=2)
        ldx = em.sync_dma(lambda sy: sy.dma_start(
            out=xs_[:], in_=xT_h.ap().rearrange("kc p t -> p kc t")))

        gat = [None] * 32
        for dc in range(32):
            ld = em.sync_dma(
                (lambda dc: lambda sy: [
                    sy.dma_start(out=ybuf[dc % 2][:], in_=y_h.ap()[dc]),
                    sy.dma_start(out=rbuf[dc % 2][:], in_=res_h.ap()[dc])])(dc),
                deps=[gat[dc - 2] if dc >= 2 else None], n=2)
            gat[dc] = em.v(
                (lambda dc: lambda v: v.tensor_tensor(
                    out=gated[:, dc, :], in0=ybuf[dc % 2][:], in1=rbuf[dc % 2][:],
                    op=ALU.mult))(dc),
                deps=[ld])

        ldw = [None] * 16
        stat_units = []
        ev_units = []
        for mc in range(16):
            ldw[mc] = em.gp_dma(
                (lambda mc: lambda g: g.dma_start(
                    out=wbuf[mc % 2][:].rearrange("p (kc j) -> p kc j", j=128),
                    in_=wout_h.ap()[mc].rearrange("kc p j -> p kc j")))(mc),
                deps=[ev_units[-3] if mc >= 2 else None])
            for i, (o, w) in enumerate(NCP_TC):
                s_ = psmm.next()
                mm = em.t(
                    (lambda mc, o, w, s_: lambda t: [
                        t.matmul(psmm.tile(s_)[:, 0:w],
                                 wbuf[mc % 2][:, kc * 128:(kc + 1) * 128],
                                 gated[:, kc, o:o + w],
                                 start=(kc == 0), stop=(kc == 31))
                        for kc in range(32)])(mc, o, w, s_),
                    deps=[ldw[mc], psmm.reader[s_]] + ([gat[31]] if mc == 0 else []))
                ev = em.v(
                    (lambda mc, o, w, s_: lambda v: v.scalar_tensor_tensor(
                        out=xs_[:, mc, o:o + w], in0=xs_[:, mc, o:o + w],
                        scalar=2.0, in1=psmm.tile(s_)[:, 0:w],
                        op0=ALU.mult, op1=ALU.add))(mc, o, w, s_),
                    deps=[mm, ldx])
                psmm.reader[s_] = ev
                ev_units.append(ev)
            sqd = em.s(
                (lambda mc: lambda sc: sc.activation(
                    out=sq[:], in_=xs_[:, mc, :], func=AF.Square))(mc),
                deps=[ev_units[-1], ev_units[-2],
                      stat_units[-1] if mc >= 1 else None])
            stat_units.append(em.t(
                (lambda mc: lambda t: [
                    t.matmul(ps_st[0][:, 0:512], c_onesc[:], xs_[:, mc, 0:512],
                             start=(mc == 0), stop=(mc == 15)),
                    t.matmul(ps_st[1][:, 0:512], c_onesc[:], xs_[:, mc, 512:1024],
                             start=(mc == 0), stop=(mc == 15)),
                    t.matmul(ps_st[2][:, 0:512], c_onesc[:], sq[:, 0:512],
                             start=(mc == 0), stop=(mc == 15)),
                    t.matmul(ps_st[3][:, 0:512], c_onesc[:], sq[:, 512:1024],
                             start=(mc == 0), stop=(mc == 15))])(mc),
                deps=[sqd, consts]))

        rrc = em.s(lambda sc: [
            sc.copy(out=rows[:, 0:512], in_=ps_st[0][:, 0:512]),
            sc.copy(out=rows[:, 512:1024], in_=ps_st[1][:, 0:512]),
            sc.copy(out=rows[:, 1024:1536], in_=ps_st[2][:, 0:512]),
            sc.copy(out=rows[:, 1536:2048], in_=ps_st[3][:, 0:512])],
            deps=[stat_units[15]])
        rr = em.v(lambda v: [
            v.tensor_scalar(out=rows[:, 2048:3072], in0=rows[:, 0:1024],
                            scalar1=1.0 / DM, scalar2=None, op0=ALU.mult),
            v.tensor_scalar(out=rows[:, 1024:2048], in0=rows[:, 1024:2048],
                            scalar1=1.0 / DM, scalar2=None, op0=ALU.mult),
            v.tensor_tensor(out=rows[:, 0:1024], in0=rows[:, 2048:3072],
                            in1=rows[:, 2048:3072], op=ALU.mult),
            v.tensor_tensor(out=rows[:, 3072:4096], in0=rows[:, 1024:2048],
                            in1=rows[:, 0:1024], op=ALU.subtract),
            v.tensor_scalar(out=rows[:, 3072:4096], in0=rows[:, 3072:4096],
                            scalar1=EPS5, scalar2=None, op0=ALU.add)],
            deps=[rrc])
        r2 = em.s(lambda sc: sc.activation(out=rows[:, 3072:4096],
                                           in_=rows[:, 3072:4096], func=AF.Sqrt),
                  deps=[rr])
        r3 = em.v(lambda v: v.reciprocal(out=rows[:, 3072:4096],
                                         in_=rows[:, 3072:4096]), deps=[r2])
        bcs = [psmm.next() for _ in range(4)]
        r4 = em.t(lambda t: [
            t.matmul(psmm.tile(bcs[0])[:, 0:512], c_onesr[:], rows[:, 2048:2560],
                     start=True, stop=True),
            t.matmul(psmm.tile(bcs[1])[:, 0:512], c_onesr[:], rows[:, 2560:3072],
                     start=True, stop=True),
            t.matmul(psmm.tile(bcs[2])[:, 0:512], c_onesr[:], rows[:, 3072:3584],
                     start=True, stop=True),
            t.matmul(psmm.tile(bcs[3])[:, 0:512], c_onesr[:], rows[:, 3584:4096],
                     start=True, stop=True)],
            deps=[r3] + [psmm.reader[s_] for s_ in bcs])
        r5 = em.s(lambda sc: [
            sc.copy(out=mean_b[:, 0:512], in_=psmm.tile(bcs[0])[:, 0:512]),
            sc.copy(out=mean_b[:, 512:1024], in_=psmm.tile(bcs[1])[:, 0:512]),
            sc.copy(out=rstd_b[:, 0:512], in_=psmm.tile(bcs[2])[:, 0:512]),
            sc.copy(out=rstd_b[:, 512:1024], in_=psmm.tile(bcs[3])[:, 0:512])],
            deps=[r4])
        for s_ in bcs:
            psmm.reader[s_] = r5

        outs = []
        for mc in range(16):
            ap1 = em.v(
                (lambda mc: lambda v: [
                    v.tensor_tensor(out=xs_[:, mc, :], in0=xs_[:, mc, :],
                                    in1=mean_b[:], op=ALU.subtract),
                    v.tensor_tensor(out=xs_[:, mc, :], in0=xs_[:, mc, :],
                                    in1=rstd_b[:], op=ALU.mult)])(mc),
                deps=[r5, stat_units[15]])
            cp = em.s(
                (lambda mc: lambda sc: sc.copy(out=xn1b[:], in_=xs_[:, mc, :]))(mc),
                deps=[ap1, outs[-1] if mc >= 1 else None])
            o1 = em.sync_dma(
                (lambda mc: lambda sy: [
                    sy.dma_start(out=x1_h.ap()[mc], in_=xs_[:, mc, :]),
                    sy.dma_start(out=xn1_h.ap()[mc], in_=xn1b[:])])(mc),
                deps=[cp], n=2)
            outs.append(o1)
        em.g(lambda g: g.engine_nop(), deps=outs)
        em.realize(block, sems, dma_sems)
    return nc


def host_inputs_C1(resA, resB, inputs):
    y_all = np.concatenate(
        [np.asarray(r["y16"]).view(BF).reshape(512, S) for r in resB], axis=0)
    x = np.asarray(inputs["L0_attn_x"])[0]
    xT = np.ascontiguousarray(x.T)
    wout = lhsT_tiles(np.asarray(inputs["L0_attn_W_out"]))
    maps = []
    for c in range(NC):
        cols = slice(c * TCK, (c + 1) * TCK)
        maps.append({
            "y16": np.ascontiguousarray(y_all[:, cols]).reshape(32, 128, TCK),
            "res16": np.asarray(resA[c]["res16"]).view(BF).reshape(32, 128, TCK),
            "xT": np.ascontiguousarray(xT[:, cols]).reshape(16, 128, TCK),
            "wout_t": wout,
            "onesc": np.ones((128, 1), np.float32),
            "onesr": np.ones((1, 128), np.float32),
        })
    return maps


def run_C1(resA, resB, inputs):
    nc = build_C1()
    maps = host_inputs_C1(resA, resB, inputs)
    res = run_bass_kernel_spmd(nc, maps, core_ids=list(range(NC)))
    return res.results



def build_C2():
    nc = bass.Bass("TRN2", target_bir_lowering=False, debug=False)
    xn1_h = nc.dram_tensor("xn1", [16, 128, TCK], BF16, kind="ExternalInput")
    x1_h = nc.dram_tensor("x1", [16, 128, TCK], F32, kind="ExternalInput")
    wg_h = nc.dram_tensor("wg", [DM, 8192], F32, kind="ExternalInput")
    wu_h = nc.dram_tensor("wu", [DM, 8192], F32, kind="ExternalInput")
    wdn_h = nc.dram_tensor("wdn_t", [16, 64, 128, 128], F32, kind="ExternalInput")
    id_h = nc.dram_tensor("ident16", [128, 128], BF16, kind="ExternalInput")
    onesc_h = nc.dram_tensor("onesc", [128, 1], F32, kind="ExternalInput")
    onesr_h = nc.dram_tensor("onesr", [1, 128], F32, kind="ExternalInput")
    x2_h = nc.dram_tensor("x2", [16, 128, TCK], F32, kind="ExternalOutput")

    TT = 512
    st = ExitStack()
    with st:
        def SB(name, shape, dt):
            return st.enter_context(nc.sbuf_tensor(name, shape, dt))

        def PSUM(name, shape, dt=F32):
            return st.enter_context(nc.psum_tensor(name, shape, dt))
        c_id = SB("c_id", [128, 128], BF16)
        c_onesc = SB("c_onesc", [128, 1], F32)
        c_onesr = SB("c_onesr", [1, 128], F32)
        xn1 = SB("xn1s", [128, 16, TCK], BF16)
        hT = SB("hT", [128, 64, TT], BF16)
        wgb = SB("wgb", [128, 16, 512], BF16)
        wub = SB("wub", [128, 16, 512], BF16)
        wdb = [SB("wdb0", [128, 64, 128], BF16)]
        gs = SB("gs", [128, 4, 512], BF16)
        habs = SB("habs", [128, 2048], BF16)
        hsp = habs
        q = SB("q", [128, 4, 512], BF16)
        msk = SB("msk", [128, 2048], U8)
        zero16 = SB("zero16", [128, 512], BF16)
        x1h = SB("x1h", [128, 16, TT], F32)
        sq = SB("sq", [128, TT], F32)
        rows = SB("rows", [1, 4 * TT], F32)
        mean_b = SB("mean_b", [128, TT], F32)
        rstd_b = SB("rstd_b", [128, TT], F32)
        psg = Slots([PSUM(f"psg{i}", [128, 512]) for i in range(4)])
        ps_tr = [PSUM("pstr0", [128, 1024], BF16), PSUM("pstr1", [128, 1024], BF16)]
        ps_st = [PSUM("ps_su", [1, 512]), PSUM("ps_sq", [1, 512])]
        sems = {e: st.enter_context(nc.semaphore("sem_" + e)) for e in Emitter.ENGS}
        dma_sems = [st.enter_context(nc.semaphore(f"dsem{i}")) for i in range(Emitter.NPOOL)]
        block = st.enter_context(nc.Block())
        em = Emitter(nc)

        consts = em.sync_dma(lambda sy: [
            sy.dma_start(out=c_id[:], in_=id_h.ap()),
            sy.dma_start(out=c_onesc[:], in_=onesc_h.ap()),
            sy.dma_start(out=c_onesr[:], in_=onesr_h.ap())], n=3)
        ldxn = em.sync_dma(lambda sy: sy.dma_start(
            out=xn1[:], in_=xn1_h.ap().rearrange("kc p t -> p kc t")))
        z0 = em.v(lambda v: v.memset(zero16[:], 0.0))

        x2outs = []
        prev_ldw = {}
        prev_half = {}
        for th in range(2):
            tsl = slice(th * TT, (th + 1) * TT)
            ldx1 = em.sync_dma(
                (lambda th: lambda sy: sy.dma_start(
                    out=x1h[:],
                    in_=x1_h.ap().rearrange("kc p t -> p kc t")[:, :, th * TT:(th + 1) * TT]))(th),
                deps=prev_half.get('x2dma_all', []))
            tr_units = []
            for ncc in range(16):
                ldwg = em.gp_dma(
                    (lambda ncc: lambda g: g.dma_start(
                        out=wgb[:],
                        in_=wg_h.ap().rearrange("(kc p) n -> p kc n", p=128)
                        [:, :, ncc * 512:(ncc + 1) * 512]))(ncc),
                    deps=[prev_ldw.get('g')])
                gmm = []
                for tc in range(4):
                    s_ = psg.next()
                    mm = em.t(
                        (lambda ncc, tc, th, s_: lambda t: [
                            t.matmul(psg.tile(s_)[:, 0:512],
                                     xn1[:, kc, th * TT + tc * 128: th * TT + (tc + 1) * 128],
                                     wgb[:, kc, :],
                                     start=(kc == 0), stop=(kc == 15))
                            for kc in range(16)])(ncc, tc, th, s_),
                        deps=[ldwg, ldxn, psg.reader[s_]])
                    gmm.append((mm, s_))
                gevac = em.s(
                    (lambda gmm: lambda sc: [
                        sc.activation(out=gs[:, tc, :], in_=psg.tile(s_)[:, 0:512],
                                      func=AF.Silu)
                        for tc, (mm, s_) in enumerate(gmm)])(gmm),
                    deps=[mm for mm, _ in gmm])
                for _, s_ in gmm:
                    psg.reader[s_] = gevac
                prev_ldw['g'] = gmm[-1][0]
                ldwu = em.gp_dma(
                    (lambda ncc: lambda g: g.dma_start(
                        out=wub[:],
                        in_=wu_h.ap().rearrange("(kc p) n -> p kc n", p=128)
                        [:, :, ncc * 512:(ncc + 1) * 512]))(ncc),
                    deps=[prev_ldw.get('u')])
                umm = []
                for tc in range(4):
                    s_ = psg.next()
                    mm = em.t(
                        (lambda ncc, tc, th, s_: lambda t: [
                            t.matmul(psg.tile(s_)[:, 0:512],
                                     xn1[:, kc, th * TT + tc * 128: th * TT + (tc + 1) * 128],
                                     wub[:, kc, :],
                                     start=(kc == 0), stop=(kc == 15))
                            for kc in range(16)])(ncc, tc, th, s_),
                        deps=[ldwu, ldxn, psg.reader[s_]])
                    umm.append((mm, s_))
                prev_ldw['u'] = umm[-1][0]
                hvs = []
                for tc in range(4):
                    mm, s_ = umm[tc]
                    hv = em.v(
                        (lambda tc, s_: lambda v: v.tensor_tensor(
                            out=gs[:, tc, :], in0=gs[:, tc, :],
                            in1=psg.tile(s_)[:, 0:512], op=ALU.mult))(tc, s_),
                        deps=[mm, gevac])
                    psg.reader[s_] = hv
                    hvs.append(hv)
                gsf = gs[:].rearrange("p tc t -> p (tc t)")
                ab = em.s(
                    lambda sc: sc.activation(out=habs[:], in_=gsf, func=AF.Abs),
                    deps=hvs + [tr_units[-1] if tr_units else None])
                h4 = habs[:].rearrange("p (g4 j) -> p g4 j", j=4)
                spar = em.v(
                    (lambda h4, gsf: lambda v: [
                        v.tensor_tensor(out=q[:, 0, :], in0=h4[:, :, 0], in1=h4[:, :, 1], op=ALU.max),
                        v.tensor_tensor(out=q[:, 1, :], in0=h4[:, :, 0], in1=h4[:, :, 1], op=ALU.min),
                        v.tensor_tensor(out=q[:, 2, :], in0=h4[:, :, 2], in1=h4[:, :, 3], op=ALU.max),
                        v.tensor_tensor(out=q[:, 3, :], in0=h4[:, :, 2], in1=h4[:, :, 3], op=ALU.min),
                        v.tensor_tensor(out=q[:, 0, :], in0=q[:, 0, :], in1=q[:, 2, :], op=ALU.min),
                        v.tensor_tensor(out=q[:, 1, :], in0=q[:, 1, :], in1=q[:, 3, :], op=ALU.max),
                        v.tensor_tensor(out=q[:, 2, :], in0=q[:, 0, :], in1=q[:, 1, :], op=ALU.max),
                        v.tensor_tensor(out=msk[:].rearrange("p (g4 j) -> p g4 j", j=4),
                                        in0=h4,
                                        in1=q[:, 2, :].broadcast_to([128, 512, 4]),
                                        op=ALU.is_ge),
                        v.select(out=hsp[:].rearrange("p (g4 j) -> p g4 j", j=4),
                                 mask=msk[:].rearrange("p (g4 j) -> p g4 j", j=4),
                                 on_true=gsf.rearrange("p (g4 j) -> p g4 j", j=4),
                                 on_false=zero16[:].broadcast_to([128, 512, 4]))])(h4, gsf),
                    deps=[ab, z0])
                for half16 in range(2):
                    trs = half16
                    trm = em.t(
                        (lambda half16, trs: lambda t: [
                            t.transpose(ps_tr[trs][:, gg * 128:(gg + 1) * 128],
                                        hsp[:, half16 * 1024 + gg * 128:
                                            half16 * 1024 + (gg + 1) * 128], c_id[:])
                            for gg in range(8)])(half16, trs),
                        deps=[spar, consts,
                              tr_units[-2] if len(tr_units) >= 2 else None])
                    trev = em.v(
                        (lambda ncc, half16, trs: lambda v: [
                            v.tensor_copy(
                                out=hT[:, ncc * 4 + ((half16 * 1024 + gg * 128) % 512) // 128,
                                       ((half16 * 1024 + gg * 128) // 512) * 128:
                                       ((half16 * 1024 + gg * 128) // 512) * 128 + 128],
                                in_=ps_tr[trs][:, gg * 128:(gg + 1) * 128])
                            for gg in range(8)])(ncc, half16, trs),
                        deps=[trm, prev_half.get('down_last')])
                    tr_units.append(trev)
            stat_units = []
            ev_units = []
            for mc in range(16):
                ldwd = em.gp_dma(
                    (lambda mc: lambda g: g.dma_start(
                        out=wdb[0][:],
                        in_=wdn_h.ap()[mc].rearrange("kc p j -> p kc j")))(mc),
                    deps=[prev_ldw.get('d')])
                for i in range(1):
                    s_ = psg.next()
                    mm = em.t(
                        (lambda mc, s_: lambda t: [
                            t.matmul(psg.tile(s_)[:, 0:512], wdb[0][:, kc, :],
                                     hT[:, kc, :],
                                     start=(kc == 0), stop=(kc == 63))
                            for kc in range(64)])(mc, s_),
                        deps=[ldwd, psg.reader[s_]] + tr_units[-2:])
                    prev_ldw['d'] = mm
                    ev = em.v(
                        (lambda mc, s_: lambda v: v.tensor_tensor(
                            out=x1h[:, mc, :], in0=x1h[:, mc, :],
                            in1=psg.tile(s_)[:, 0:512], op=ALU.add))(mc, s_),
                        deps=[mm, ldx1])
                    psg.reader[s_] = ev
                    ev_units.append(ev)
                sqd = em.s(
                    (lambda mc: lambda sc: sc.activation(
                        out=sq[:], in_=x1h[:, mc, :], func=AF.Square))(mc),
                    deps=[ev_units[-1], stat_units[-1] if stat_units else None])
                stat_units.append(em.t(
                    (lambda mc: lambda t: [
                        t.matmul(ps_st[0][:, 0:512], c_onesc[:], x1h[:, mc, :],
                                 start=(mc == 0), stop=(mc == 15)),
                        t.matmul(ps_st[1][:, 0:512], c_onesc[:], sq[:],
                                 start=(mc == 0), stop=(mc == 15))])(mc),
                    deps=[sqd, consts]))
            rrc = em.s(
                lambda sc: [
                    sc.copy(out=rows[:, 0:512], in_=ps_st[0][:, 0:512]),
                    sc.copy(out=rows[:, 512:1024], in_=ps_st[1][:, 0:512])],
                deps=[stat_units[15]])
            rr = em.v(
                lambda v: [
                    v.tensor_scalar(out=rows[:, 1024:1536], in0=rows[:, 0:512],
                                    scalar1=1.0 / DM, scalar2=None, op0=ALU.mult),
                    v.tensor_scalar(out=rows[:, 512:1024], in0=rows[:, 512:1024],
                                    scalar1=1.0 / DM, scalar2=None, op0=ALU.mult),
                    v.tensor_tensor(out=rows[:, 0:512], in0=rows[:, 1024:1536],
                                    in1=rows[:, 1024:1536], op=ALU.mult),
                    v.tensor_tensor(out=rows[:, 1536:2048], in0=rows[:, 512:1024],
                                    in1=rows[:, 0:512], op=ALU.subtract),
                    v.tensor_scalar(out=rows[:, 1536:2048], in0=rows[:, 1536:2048],
                                    scalar1=EPS5, scalar2=None, op0=ALU.add)],
                deps=[rrc])
            r2 = em.s(lambda sc: sc.activation(out=rows[:, 1536:2048],
                                               in_=rows[:, 1536:2048], func=AF.Sqrt),
                      deps=[rr])
            r3 = em.v(lambda v: v.reciprocal(out=rows[:, 1536:2048],
                                             in_=rows[:, 1536:2048]), deps=[r2])
            bcs = [psg.next(), psg.next()]
            r4 = em.t(lambda t: [
                t.matmul(psg.tile(bcs[0])[:, 0:512], c_onesr[:], rows[:, 1024:1536],
                         start=True, stop=True),
                t.matmul(psg.tile(bcs[1])[:, 0:512], c_onesr[:], rows[:, 1536:2048],
                         start=True, stop=True)],
                deps=[r3] + [psg.reader[s_] for s_ in bcs])
            r5 = em.s(lambda sc: [
                sc.copy(out=mean_b[:], in_=psg.tile(bcs[0])[:, 0:512]),
                sc.copy(out=rstd_b[:], in_=psg.tile(bcs[1])[:, 0:512])],
                deps=[r4])
            for s_ in bcs:
                psg.reader[s_] = r5
            outs_th = []
            for mc in range(16):
                ap1 = em.v(
                    (lambda mc: lambda v: [
                        v.tensor_tensor(out=x1h[:, mc, :], in0=x1h[:, mc, :],
                                        in1=mean_b[:], op=ALU.subtract),
                        v.tensor_tensor(out=x1h[:, mc, :], in0=x1h[:, mc, :],
                                        in1=rstd_b[:], op=ALU.mult)])(mc),
                    deps=[r5])
                o1 = em.sync_dma(
                    (lambda mc, th: lambda sy: sy.dma_start(
                        out=x2_h.ap()[mc][:, th * TT:(th + 1) * TT],
                        in_=x1h[:, mc, :]))(mc, th),
                    deps=[ap1])
                outs_th.append(o1)
            x2outs += outs_th
            prev_half['x2dma_all'] = list(outs_th)
            prev_half['down_last'] = prev_ldw.get('d')
        em.g(lambda g: g.engine_nop(), deps=x2outs)
        em.realize(block, sems, dma_sems)
    return nc


def host_inputs_C2(resC1, inputs):
    wdn = lhsT_tiles(np.asarray(inputs["L0_ffn_w_down"]))
    ident = np.eye(128, dtype=np.float32).astype(BF)
    maps = []
    for c in range(NC):
        maps.append({
            "xn1": np.asarray(resC1[c]["xn1"]).view(BF).reshape(16, 128, TCK),
            "x1": np.asarray(resC1[c]["x1"]).reshape(16, 128, TCK),
            "wg": np.asarray(inputs["L0_ffn_w_gate"]),
            "wu": np.asarray(inputs["L0_ffn_w_up"]),
            "wdn_t": wdn,
            "ident16": ident,
            "onesc": np.ones((128, 1), np.float32),
            "onesr": np.ones((1, 128), np.float32),
        })
    return maps


def run_C2(resC1, inputs):
    nc = build_C2()
    maps = host_inputs_C2(resC1, inputs)
    res = run_bass_kernel_spmd(nc, maps, core_ids=list(range(NC)))
    return res.results



def build_D():
    nc = bass.Bass("TRN2", target_bir_lowering=False, debug=False)
    x2_h = nc.dram_tensor("x2T", [16, 128, TH], F32, kind="ExternalInput")
    wl_h = nc.dram_tensor("wl_t", [81, 16, 128, 128], F32, kind="ExternalInput")
    cwt_h = nc.dram_tensor("cwt2", [128, 192], F32, kind="ExternalInput")
    cb_h = nc.dram_tensor("cb2", [128, 48], F32, kind="ExternalInput")
    dtb_h = nc.dram_tensor("dtb", [64, 1], F32, kind="ExternalInput")
    alog_h = nc.dram_tensor("alog2", [64, 1], F32, kind="ExternalInput")

    gate_h = nc.dram_tensor("gate16", [32, 128, TCK], BF16, kind="ExternalOutput")
    xbc_h = nc.dram_tensor("xbc16", [48, 128, TCK], BF16, kind="ExternalOutput")
    dt2_h = nc.dram_tensor("dt2", [64, TCK], F32, kind="ExternalOutput")
    a2_h = nc.dram_tensor("a2", [64, TCK], F32, kind="ExternalOutput")

    st = ExitStack()
    with st:
        def SB(name, shape, dt):
            return st.enter_context(nc.sbuf_tensor(name, shape, dt))

        def PSUM(name, shape, dt=F32):
            return st.enter_context(nc.psum_tensor(name, shape, dt))
        c_cwt = SB("c_cwt", [128, 192], F32)
        c_cb = SB("c_cb", [128, 48], F32)
        c_dtb = SB("c_dtb", [64, 1], F32)
        c_al = SB("c_al", [64, 1], F32)
        x2n = SB("x2n", [128, 16, TH], BF16)
        wbuf = [SB("wbuf0", [128, 2048], BF16), SB("wbuf1", [128, 2048], BF16)]
        xraw = SB("xraw", [128, TH], F32)
        cacc = SB("cacc", [128, TCK], F32)
        ob = [SB("ob0", [128, TCK], BF16), SB("ob1", [128, TCK], BF16)]
        sp1 = SB("sp1", [128, TCK], F32)
        sp2 = SB("sp2", [128, TCK], F32)
        dtb_ = SB("dtbuf", [128, TCK], F32)
        a2b = SB("a2buf", [128, TCK], F32)
        psmm = Slots([PSUM(f"ps{i}", [128, 512]) for i in range(4)])
        sems = {e: st.enter_context(nc.semaphore("sem_" + e)) for e in Emitter.ENGS}
        dma_sems = [st.enter_context(nc.semaphore(f"dsem{i}")) for i in range(Emitter.NPOOL)]
        block = st.enter_context(nc.Block())
        em = Emitter(nc)

        consts = em.sync_dma(lambda sy: [
            sy.dma_start(out=c_cwt[:], in_=cwt_h.ap()),
            sy.dma_start(out=c_cb[:], in_=cb_h.ap()),
            sy.dma_start(out=c_dtb[:], in_=dtb_h.ap()),
            sy.dma_start(out=c_al[:], in_=alog_h.ap())], n=4)
        ldx = em.gp_dma(lambda g: g.dma_start(
            out=x2n[:], in_=x2_h.ap().rearrange("kc p t -> p kc t")))
        a1u = em.s(lambda sc: sc.activation(out=c_al[:], in_=c_al[:], func=AF.Exp),
                   deps=[consts])
        a2u = em.v(lambda v: v.tensor_scalar(out=c_al[:], in0=c_al[:], scalar1=-1.0,
                                             scalar2=None, op0=ALU.mult), deps=[a1u])

        ldw = [None] * 81
        out_units = []
        prev_gate_dma = {}
        prev_conv = [None] * 81
        last_wbuf_mm = [None, None]
        for mc in range(81):
            ldw[mc] = em.gp_dma(
                (lambda mc: lambda g: g.dma_start(
                    out=wbuf[mc % 2][:].rearrange("p (kc j) -> p kc j", j=128),
                    in_=wl_h.ap()[mc].rearrange("kc p j -> p kc j")))(mc),
                deps=[last_wbuf_mm[mc % 2]])
            evacs = []
            mm = None
            for i, (o, w) in enumerate(NCP_TH):
                s_ = psmm.next()
                mm = em.t(
                    (lambda mc, o, w, s_: lambda t: [
                        t.matmul(psmm.tile(s_)[:, 0:w],
                                 wbuf[mc % 2][:, kc * 128:(kc + 1) * 128],
                                 x2n[:, kc, o:o + w],
                                 start=(kc == 0), stop=(kc == 15))
                        for kc in range(16)])(mc, o, w, s_),
                    deps=[ldw[mc], ldx, psmm.reader[s_]])
                if mc < 32:
                    def gfn(o, w, s_, mc):
                        def fn(sc):
                            if o == 0:
                                return sc.activation(out=ob[mc % 2][:, 0:w - HALO],
                                                     in_=psmm.tile(s_)[:, HALO:w],
                                                     func=AF.Silu)
                            return sc.activation(out=ob[mc % 2][:, o - HALO:o - HALO + w],
                                                 in_=psmm.tile(s_)[:, 0:w], func=AF.Silu)
                        return fn
                    ev = em.s(gfn(o, w, s_, mc),
                              deps=[mm, prev_gate_dma.get(mc % 2)])
                elif mc < 80:
                    ev = em.s(
                        (lambda o, w, s_: lambda sc: sc.copy(
                            out=xraw[:, o:o + w], in_=psmm.tile(s_)[:, 0:w]))(o, w, s_),
                        deps=[mm, prev_conv[mc - 1] if i == 0 else None])
                else:
                    def dfn(o, w, s_):
                        def fn(sc):
                            if o == 0:
                                src = psmm.tile(s_)[0:64, HALO:w]
                                dsl = slice(0, w - HALO)
                            else:
                                src = psmm.tile(s_)[0:64, 0:w]
                                dsl = slice(o - HALO, o - HALO + w)
                            sc.activation(out=sp1[0:64, dsl], in_=src,
                                          func=AF.Abs, bias=c_dtb[:, 0:1])
                            sc.activation(out=sp1[0:64, dsl], in_=sp1[0:64, dsl],
                                          func=AF.Exp, scale=-1.0)
                            sc.activation(out=sp1[0:64, dsl], in_=sp1[0:64, dsl],
                                          func=AF.Ln, bias=1.0)
                            return sc.activation(out=sp2[0:64, dsl], in_=src,
                                                 func=AF.Relu, bias=c_dtb[:, 0:1])
                        return fn
                    ev = em.s(dfn(o, w, s_), deps=[mm])
                psmm.reader[s_] = ev
                evacs.append(ev)
            last_wbuf_mm[mc % 2] = mm
            if mc < 32:
                d1 = em.sync_dma(
                    (lambda mc: lambda sy: sy.dma_start(
                        out=gate_h.ap()[mc], in_=ob[mc % 2][:]))(mc),
                    deps=evacs)
                prev_gate_dma[mc % 2] = d1
                out_units.append(d1)
                prev_conv[mc] = evacs[-1]
            elif mc < 80:
                dc2 = mc - 32
                cd = em.v(
                    (lambda dc2: lambda v: [
                        v.tensor_scalar(out=cacc[:], in0=xraw[:, 1:1 + TCK],
                                        scalar1=c_cwt[:, dc2 * 4:dc2 * 4 + 1],
                                        scalar2=c_cb[:, dc2:dc2 + 1],
                                        op0=ALU.mult, op1=ALU.add),
                        *[v.scalar_tensor_tensor(
                            out=cacc[:], in0=xraw[:, 1 + wt:1 + wt + TCK],
                            scalar=c_cwt[:, dc2 * 4 + wt:dc2 * 4 + wt + 1],
                            in1=cacc[:], op0=ALU.mult, op1=ALU.add)
                          for wt in range(1, 4)]])(dc2),
                    deps=evacs + [prev_conv[mc - 1], consts])
                sil = em.s(
                    (lambda mc: lambda sc: sc.activation(
                        out=ob[mc % 2][:], in_=cacc[:], func=AF.Silu))(mc),
                    deps=[cd, prev_gate_dma.get(mc % 2)])
                prev_conv[mc] = sil
                d1 = em.sync_dma(
                    (lambda mc: lambda sy: sy.dma_start(
                        out=xbc_h.ap()[mc - 32], in_=ob[mc % 2][:]))(mc),
                    deps=[sil])
                prev_gate_dma[mc % 2] = d1
                out_units.append(d1)
            else:
                add = em.v(lambda v: v.tensor_tensor(
                    out=dtb_[0:64, :], in0=sp1[0:64, :], in1=sp2[0:64, :],
                    op=ALU.add), deps=evacs)
                a2c = em.v(lambda v: v.tensor_scalar(
                    out=a2b[0:64, :], in0=dtb_[0:64, :], scalar1=c_al[:, 0:1],
                    scalar2=None, op0=ALU.mult), deps=[add, a2u])
                d1 = em.sync_dma(lambda sy: [
                    sy.dma_start(out=dt2_h.ap(), in_=dtb_[0:64, :]),
                    sy.dma_start(out=a2_h.ap(), in_=a2b[0:64, :])],
                    deps=[add, a2c], n=2)
                out_units.append(d1)
        em.g(lambda g: g.engine_nop(), deps=out_units)
        em.realize(block, sems, dma_sems)
    return nc


def host_inputs_D(resC2, inputs):
    x2_all = np.concatenate(
        [np.asarray(r["x2"]).reshape(DM, TCK) for r in resC2], axis=1)
    wdt_pad = np.zeros((DM, 128), np.float32)
    wdt_pad[:, :64] = np.asarray(inputs["L1_attn_Wdt"])
    wl = np.concatenate([np.asarray(inputs["L1_attn_Wz"]),
                         np.asarray(inputs["L1_attn_Wx"]),
                         np.asarray(inputs["L1_attn_WB"]),
                         np.asarray(inputs["L1_attn_WC"]),
                         wdt_pad], axis=1)
    wl_t = lhsT_tiles(wl)
    conv_w = np.asarray(inputs["L1_attn_conv_w"])
    cwt2 = np.ascontiguousarray(
        conv_w.reshape(48, 128, 4).transpose(1, 0, 2).reshape(128, 192))
    cb2 = percol(np.asarray(inputs["L1_attn_conv_b"]))
    x2slices = tok_slices(x2_all, HALO)
    maps = []
    for c in range(NC):
        maps.append({
            "x2T": x2slices[c].reshape(16, 128, TH),
            "wl_t": wl_t,
            "cwt2": cwt2,
            "cb2": cb2,
            "dtb": np.asarray(inputs["L1_attn_dt_bias"]).reshape(64, 1),
            "alog2": np.asarray(inputs["L1_attn_A_log"]).reshape(64, 1),
        })
    return maps


def run_D(resC2, inputs):
    nc = build_D()
    maps = host_inputs_D(resC2, inputs)
    res = run_bass_kernel_spmd(nc, maps, core_ids=list(range(NC)))
    return res.results



def build_E():
    nc = bass.Bass("TRN2", target_bir_lowering=False, debug=False)
    xg_h = nc.dram_tensor("xg16", [8, 64, S], BF16, kind="ExternalInput")
    bg_h = nc.dram_tensor("bg16", [128, S], BF16, kind="ExternalInput")
    cg_h = nc.dram_tensor("cg16", [128, S], BF16, kind="ExternalInput")
    ag_h = nc.dram_tensor("ag8", [8, S], F32, kind="ExternalInput")
    dt_h = nc.dram_tensor("dt8", [8, S], F32, kind="ExternalInput")
    sel_h = nc.dram_tensor("sel8", [8, 8 * 128], F32, kind="ExternalInput")
    i8_h = nc.dram_tensor("i8", [8, 8], F32, kind="ExternalInput")
    ones8_h = nc.dram_tensor("ones8", [8, 128], F32, kind="ExternalInput")
    tri_h = nc.dram_tensor("tri", [128, 128], F32, kind="ExternalInput")
    dgb_h = nc.dram_tensor("dgb", [64, 8], F32, kind="ExternalInput")
    id_h = nc.dram_tensor("ident16", [128, 128], BF16, kind="ExternalInput")
    yg_h = nc.dram_tensor("yg16", [8, 64, S], BF16, kind="ExternalOutput")

    Q = 128
    NQTR = 4
    TTE = S // NQTR
    NCH = TTE // Q
    st = ExitStack()
    with st:
        def SB(name, shape, dt):
            return st.enter_context(nc.sbuf_tensor(name, shape, dt))

        def PSUM(name, shape, dt=F32):
            return st.enter_context(nc.psum_tensor(name, shape, dt))
        xg = SB("xg", [64, 8, TTE], BF16)
        bg = SB("bg", [128, TTE], BF16)
        cg = SB("cg", [128, TTE], BF16)
        yg = SB("yg", [64, 8, TTE], BF16)
        ag8 = SB("ag8s", [8, TTE], F32)
        dt8 = SB("dt8s", [8, TTE], F32)
        zr8 = SB("zr8", [8, TTE], BF16)
        cumx = SB("cumx", [8, TTE + 1], F32)
        carry = SB("carry", [8, 1], F32)
        c_sel = SB("c_sel", [8, 8 * 128], F32)
        c_i8 = SB("c_i8", [8, 8], F32)
        c_on8 = SB("c_on8", [8, 128], F32)
        c_tri = SB("c_tri", [128, 128], F32)
        c_dgb = SB("c_dgb", [64, 8], F32)
        c_id = SB("c_id", [128, 128], BF16)
        diag8 = SB("diag8", [8, 16], F32)
        cds = SB("cds", [128, 24], F32)
        wcols = SB("wcols", [128, 8], F32)
        wca = SB("wca", [128, 8], F32)
        wcb = SB("wcb", [128, 8], F32)
        pcol = SB("pcol", [128, 1], F32)
        zb = SB("zb", [128, 8, Q], F32)
        z0b = SB("z0b", [128, 8, Q], F32)
        st16 = SB("st16", [128, 8 * Q], BF16)
        cl16 = SB("cl16", [128, 8 * Q], BF16)
        bt16 = SB("bt16", [128, 128], BF16)
        xt16 = SB("xt16", [128, 8, 64], BF16)
        bw16 = SB("bw16", [128, 8, 128], BF16)
        ht = SB("ht", [128, 512], F32)
        ht16 = SB("ht16", [128, 512], BF16)
        ps_tr = [PSUM("pstr0", [128, 512]), PSUM("pstr1", [128, 512])]
        ps_cb = [PSUM("pscb0", [128, 512]), PSUM("pscb1", [128, 512])]
        ps_yc = [PSUM("psyc0", [128, 512]), PSUM("psyc1", [128, 512])]
        ps_tp = [PSUM("pstp0", [128, 1024], BF16), PSUM("pstp1", [128, 1024], BF16)]
        sems = {e: st.enter_context(nc.semaphore("sem_" + e)) for e in Emitter.ENGS}
        dma_sems = [st.enter_context(nc.semaphore(f"dsem{i}")) for i in range(Emitter.NPOOL)]
        block = st.enter_context(nc.Block())
        em = Emitter(nc)

        consts = em.sync_dma(lambda sy: [
            sy.dma_start(out=c_sel[:], in_=sel_h.ap()),
            sy.dma_start(out=c_i8[:], in_=i8_h.ap()),
            sy.dma_start(out=c_on8[:], in_=ones8_h.ap()),
            sy.dma_start(out=c_tri[:], in_=tri_h.ap()),
            sy.dma_start(out=c_dgb[:], in_=dgb_h.ap()),
            sy.dma_start(out=c_id[:], in_=id_h.ap()),
        ], n=6)
        init = em.v(lambda v: [
            v.memset(zr8[:], 0.0),
            v.memset(carry[:], 0.0),
            v.memset(ht[:], 0.0)], deps=[])

        prev = {}
        gc = 0
        for h2 in range(NQTR):
            hsl = slice(h2 * TTE, (h2 + 1) * TTE)
            lds = em.sync_dma(
                (lambda h2: lambda sy: [
                    sy.dma_start(out=xg[:],
                                 in_=xg_h.ap().rearrange("hh p t -> p hh t")
                                 [:, :, h2 * TTE:(h2 + 1) * TTE]),
                    sy.dma_start(out=bg[:], in_=bg_h.ap()[:, h2 * TTE:(h2 + 1) * TTE]),
                    sy.dma_start(out=cg[:], in_=cg_h.ap()[:, h2 * TTE:(h2 + 1) * TTE]),
                    sy.dma_start(out=ag8[:], in_=ag_h.ap()[:, h2 * TTE:(h2 + 1) * TTE]),
                    sy.dma_start(out=dt8[:], in_=dt_h.ap()[:, h2 * TTE:(h2 + 1) * TTE]),
                ])(h2),
                deps=[prev.get('ydma'), prev.get('lastcompute')], n=5)
            cum = em.v(
                lambda v: [
                    v.tensor_copy(out=cumx[:, 0:1], in_=carry[:]),
                    v.tensor_tensor_scan(
                        out=cumx[:, 1:TTE + 1], data0=ag8[:], data1=zr8[:],
                        initial=carry[:, 0:1], op0=ALU.add, op1=ALU.add)],
                deps=[lds, init, prev.get('carry_upd')])
            ccp = em.v(lambda v: v.tensor_copy(out=carry[:], in_=cumx[:, TTE:TTE + 1]),
                       deps=[cum])
            prev['carry_upd'] = ccp

            for c in range(NCH):
                cs = c * Q
                tp = ps_tp[gc % 2]
                pcb = ps_cb[gc % 2]
                trm = em.t(
                    (lambda cs, tp: lambda t: [
                        t.transpose(tp[:, 0:128], bg[:, cs:cs + Q], c_id[:])] + [
                        t.transpose(tp[:, 128 + h * 64: 128 + (h + 1) * 64],
                                    xg[0:64, h, cs:cs + Q], c_id[0:64, 0:64])
                        for h in range(8)])(cs, tp),
                    deps=[lds, consts, prev.get(('tpr', gc % 2))])
                trev = em.v(
                    (lambda tp: lambda v: [
                        v.tensor_copy(out=bt16[:], in_=tp[:, 0:128])] + [
                        v.tensor_copy(out=xt16[:, h, :],
                                      in_=tp[:, 128 + h * 64:128 + (h + 1) * 64])
                        for h in range(8)])(tp),
                    deps=[trm, prev.get('dk3'), prev.get('ymm_last')])
                prev[('tpr', gc % 2)] = trev
                dg = em.v(
                    (lambda cs, gc: lambda v: v.tensor_scalar(
                        out=diag8[:, (gc % 2) * 8:(gc % 2) * 8 + 8], in0=c_i8[:],
                        scalar1=cumx[:, cs + Q:cs + Q + 1], scalar2=None,
                        op0=ALU.mult))(cs, gc),
                    deps=[cum, consts, prev.get(('dgr', gc % 2))])
                cbm = em.t(
                    (lambda cs, pcb, gc: lambda t: [
                        t.matmul(pcb[:, 0:128], bg[:, cs:cs + Q], cg[:, cs:cs + Q],
                                 start=True, stop=True),
                        t.matmul(pcb[:, 128:136], cumx[:, cs + 1:cs + 1 + Q], c_i8[:],
                                 start=True, stop=True),
                        t.matmul(pcb[:, 136:144], dt8[:, cs:cs + Q], c_i8[:],
                                 start=True, stop=True),
                        t.matmul(pcb[:, 144:152], c_on8[:],
                                 diag8[:, (gc % 2) * 8:(gc % 2) * 8 + 8],
                                 start=True, stop=True)])(cs, pcb, gc),
                    deps=[cum, dg, prev.get(('cbr', gc % 2))])
                prev[('dgr', gc % 2)] = cbm
                wc0 = em.s(
                    (lambda pcb: lambda sc: sc.copy(out=cds[:], in_=pcb[:, 128:152]))(pcb),
                    deps=[cbm, prev.get('wcr')])
                wc = em.v(
                    lambda v: v.tensor_tensor(out=wca[:], in0=cds[:, 16:24],
                                              in1=cds[:, 0:8], op=ALU.subtract),
                    deps=[wc0])
                wc2 = em.s(lambda sc: sc.activation(out=wcb[:], in_=wca[:],
                                                    func=AF.Exp), deps=[wc])
                wc3 = em.v(lambda v: v.tensor_tensor(out=wcols[:], in0=wcb[:],
                                                     in1=cds[:, 8:16], op=ALU.mult),
                           deps=[wc2, prev.get('wcolr')])
                zunits = []
                for h in range(8):
                    ptr = ps_tr[h % 2]
                    tr = em.t(
                        (lambda h, cs, ptr: lambda t: t.matmul(
                            ptr[:, 0:129], c_sel[:, h * 128:(h + 1) * 128],
                            cumx[:, cs:cs + 129], start=True, stop=True))(h, cs, ptr),
                        deps=[cum, prev.get(('trr', h % 2)), consts])
                    zu = em.v(
                        (lambda h, ptr: lambda v: [
                            v.tensor_copy(out=pcol[:], in_=ptr[:, 0:1]),
                            v.tensor_scalar(out=zb[:, h, :], in0=ptr[:, 1:129],
                                            scalar1=cds[:, h:h + 1], scalar2=None,
                                            op0=ALU.subtract),
                            v.tensor_scalar(out=z0b[:, h, :], in0=ptr[:, 1:129],
                                            scalar1=pcol[:, 0:1], scalar2=None,
                                            op0=ALU.subtract)])(h, ptr),
                        deps=[tr, wc, prev.get('dk2')])
                    prev[('trr', h % 2)] = zu
                    zunits.append(zu)
                dk1 = em.v(lambda v: v.tensor_scalar(
                    out=zb[:].rearrange("p h q -> p (h q)"),
                    in0=zb[:].rearrange("p h q -> p (h q)"),
                    scalar1=0.0, scalar2=None, op0=ALU.min), deps=zunits)
                dk2 = em.s(lambda sc: [
                    sc.activation(out=zb[:].rearrange("p h q -> p (h q)"),
                                  in_=zb[:].rearrange("p h q -> p (h q)"),
                                  func=AF.Exp),
                    sc.activation(out=z0b[:].rearrange("p h q -> p (h q)"),
                                  in_=z0b[:].rearrange("p h q -> p (h q)"),
                                  func=AF.Exp)],
                    deps=[dk1])
                prev['dk2'] = dk2
                dk3 = em.v(
                    (lambda pcb, cs: lambda v: [
                        v.tensor_tensor(out=zb[:], in0=zb[:],
                                        in1=c_tri[:].rearrange("p q -> p () q")
                                        .broadcast_to([128, 8, Q]),
                                        op=ALU.mult),
                        v.tensor_tensor(out=zb[:], in0=zb[:],
                                        in1=pcb[:, 0:128].rearrange("p q -> p () q")
                                        .broadcast_to([128, 8, Q]),
                                        op=ALU.mult),
                        v.tensor_tensor(out=st16[:].rearrange("p (h q) -> p h q", q=Q),
                                        in0=zb[:],
                                        in1=cds[:, 8:16].rearrange("p h -> p h ()")
                                        .broadcast_to([128, 8, Q]),
                                        op=ALU.mult),
                        v.tensor_tensor(out=cl16[:].rearrange("p (h q) -> p h q", q=Q),
                                        in0=z0b[:],
                                        in1=cg[:, cs:cs + Q].rearrange("p q -> p () q")
                                        .broadcast_to([128, 8, Q]),
                                        op=ALU.mult),
                        v.tensor_tensor(out=bw16[:],
                                        in0=bt16[:].rearrange("p n -> p () n")
                                        .broadcast_to([128, 8, 128]),
                                        in1=wcols[:].rearrange("p h -> p h ()")
                                        .broadcast_to([128, 8, 128]),
                                        op=ALU.mult)])(pcb, cs),
                    deps=[dk2, wc3, trev, prev.get('ymm_last2')])
                prev[('cbr', gc % 2)] = dk3
                prev['dk3'] = dk3
                prev['wcolr'] = dk3
                prev['wcr'] = dk3
                htc = em.v(lambda v: v.tensor_copy(out=ht16[:], in_=ht[:]),
                           deps=[prev.get('htu_last', init), prev.get('yil')])
                ytu = None
                htu = None
                ymm = None
                for h in range(8):
                    pyc = ps_yc[h % 2]
                    ymm = em.t(
                        (lambda h, gc, cs, pyc: lambda t: [
                            t.matmul(pyc[0:64, 0:128], xt16[:, h, :],
                                     st16[:, h * Q:(h + 1) * Q],
                                     start=True, stop=(gc == 0))] + ([
                            t.matmul(pyc[0:64, 0:128],
                                     ht16[:, h * 64:(h + 1) * 64],
                                     cl16[:, h * Q:(h + 1) * Q],
                                     start=False, stop=True)] if gc > 0 else []) + [
                            t.matmul(pyc[:, 128:192], bw16[:, h, :], xt16[:, h, :],
                                     start=True, stop=True)])(h, gc, cs, pyc),
                        deps=[dk3, htc, trev, prev.get(('ycr', h % 2))])
                    ytu = em.v(
                        (lambda h, cs, pyc: lambda v: v.scalar_tensor_tensor(
                            out=yg[0:64, h, cs:cs + Q],
                            in0=xg[0:64, h, cs:cs + Q],
                            scalar=c_dgb[0:64, h:h + 1],
                            in1=pyc[0:64, 0:128],
                            op0=ALU.mult, op1=ALU.add))(h, cs, pyc),
                        deps=[ymm, consts])
                    htu = em.v(
                        (lambda h, pyc: lambda v: v.scalar_tensor_tensor(
                            out=ht[:, h * 64:(h + 1) * 64],
                            in0=ht[:, h * 64:(h + 1) * 64],
                            scalar=z0b[:, h, Q - 1:Q],
                            in1=pyc[:, 128:192],
                            op0=ALU.mult, op1=ALU.add))(h, pyc),
                        deps=[ymm, htc, init])
                    prev[('ycr', h % 2)] = htu
                prev['htu_last'] = htu
                prev['yil'] = htu
                prev['ymm_last'] = ymm
                prev['ymm_last2'] = ymm
                prev['str'] = ytu
                prev['lastcompute'] = htu
                gc += 1
            ydma = em.sync_dma(
                (lambda h2: lambda sy: sy.dma_start(
                    out=yg_h.ap().rearrange("hh p t -> p hh t")
                    [:, :, h2 * TTE:(h2 + 1) * TTE], in_=yg[:]))(h2),
                deps=[prev['str'], prev['htu_last']], n=1)
            prev['ydma'] = ydma
        em.g(lambda g: g.engine_nop(), deps=[prev['ydma']])
        em.realize(block, sems, dma_sems)
    return nc


def host_inputs_E(resD, inputs):
    xbc_all = np.concatenate(
        [np.asarray(r["xbc16"]).view(BF).reshape(CONVD, TCK) for r in resD], axis=1)
    dt_all = np.concatenate([np.asarray(r["dt2"]) for r in resD], axis=1)
    a2_all = np.concatenate([np.asarray(r["a2"]) for r in resD], axis=1)
    sel = np.zeros((8, 8, 128), np.float32)
    for h in range(8):
        sel[h, h, :] = 1.0
    dvec = np.asarray(inputs["L1_attn_D"])
    maps = []
    for g in range(NC):
        maps.append({
            "xg16": np.ascontiguousarray(
                xbc_all[g * 512:(g + 1) * 512]).reshape(8, 64, S),
            "bg16": np.ascontiguousarray(xbc_all[4096 + g * 128: 4096 + (g + 1) * 128]),
            "cg16": np.ascontiguousarray(xbc_all[5120 + g * 128: 5120 + (g + 1) * 128]),
            "ag8": np.ascontiguousarray(a2_all[g * 8:(g + 1) * 8]),
            "dt8": np.ascontiguousarray(dt_all[g * 8:(g + 1) * 8]),
            "sel8": np.ascontiguousarray(sel.transpose(1, 0, 2).reshape(8, 8 * 128)),
            "i8": np.eye(8, dtype=np.float32),
            "ones8": np.ones((8, 128), np.float32),
            "tri": np.triu(np.ones((128, 128), np.float32)),
            "dgb": np.tile(dvec[g * 8:(g + 1) * 8], (64, 1)).astype(np.float32),
            "ident16": np.eye(128, dtype=np.float32).astype(BF),
        })
    return maps


def run_E(resD, inputs):
    nc = build_E()
    maps = host_inputs_E(resD, inputs)
    res = run_bass_kernel_spmd(nc, maps, core_ids=list(range(NC)))
    return res.results



def build_G():
    nc = bass.Bass("TRN2", target_bir_lowering=False, debug=False)
    gpk_h = nc.dram_tensor("gup_pk", [11, 128, 256], I32, kind="ExternalInput")
    gsc_h = nc.dram_tensor("gup_sc", [11, 128, 16], F16, kind="ExternalInput")
    dpk_h = nc.dram_tensor("dn_pk", [2, 128, 704], I32, kind="ExternalInput")
    dsc_h = nc.dram_tensor("dn_sc", [2, 128, 44], F16, kind="ExternalInput")
    id_h = nc.dram_tensor("ident16", [128, 128], BF16, kind="ExternalInput")
    wgu_h = nc.dram_tensor("wguT", [16, 128, 1408], BF16, kind="ExternalOutput")
    wdn_h = nc.dram_tensor("wdnT", [44, 128, 256], BF16, kind="ExternalOutput")

    st = ExitStack()
    with st:
        def SB(name, shape, dt):
            return st.enter_context(nc.sbuf_tensor(name, shape, dt))
        c_id = SB("c_id", [128, 128], BF16)
        pk = SB("pk", [128, 704], I32)
        sc16 = SB("sc16", [128, 44], F16)
        sc32 = SB("sc32", [128, 44], F32)
        t1 = SB("t1", [128, 704], I32)
        fव = SB("fv", [128, 704], F32)
        tv = SB("tv", [128, 704], F32)
        mv = SB("mv", [128, 704], F32)
        sg = SB("sg", [128, 704], F32)
        rv = SB("rv", [128, 704], F32)
        bv = SB("bv", [128, 704], F32)
        av = SB("av", [128, 704], F32)
        wrow = SB("wrow", [128, 5632], BF16)
        trev_t = [SB("trev_t0", [128, 128], BF16), SB("trev_t1", [128, 128], BF16),
                  SB("trev_t2", [128, 128], BF16), SB("trev_t3", [128, 128], BF16)]
        ps_tp = [st.enter_context(nc.psum_tensor("pstp0", [128, 1024], BF16)),
                 st.enter_context(nc.psum_tensor("pstp1", [128, 1024], BF16))]
        sems = {e: st.enter_context(nc.semaphore("sem_" + e)) for e in Emitter.ENGS}
        dma_sems = [st.enter_context(nc.semaphore(f"dsem{i}")) for i in range(Emitter.NPOOL)]
        block = st.enter_context(nc.Block())
        em = Emitter(nc)
        fv = fव

        consts = em.sync_dma(lambda sy: sy.dma_start(out=c_id[:], in_=id_h.ap()))
        outs = []
        prev = {}
        for part, (nrc, nwords, ngrp, pk_ap, sc_ap, out_ap) in enumerate([
                (11, 256, 16, gpk_h, gsc_h, wgu_h), (2, 704, 44, dpk_h, dsc_h, wdn_h)]):
            nk = nwords * 8 // 128
            for rc in range(nrc):
                ld = em.sync_dma(
                    (lambda rc, pk_ap, sc_ap, nwords, ngrp: lambda sy: [
                        sy.dma_start(out=pk[:, 0:nwords], in_=pk_ap.ap()[rc]),
                        sy.dma_start(out=sc16[:, 0:ngrp], in_=sc_ap.ap()[rc])])(
                            rc, pk_ap, sc_ap, nwords, ngrp),
                    deps=[prev.get('lastk')], n=2)
                scc = em.s(
                    (lambda ngrp: lambda sc: sc.copy(out=sc32[:, 0:ngrp],
                                                     in_=sc16[:, 0:ngrp]))(ngrp),
                    deps=[ld])
                kunits = []
                for k in range(8):
                    u = em.v(
                        (lambda k, nwords, ngrp: lambda v: [
                            v.tensor_scalar(out=t1[:, 0:nwords], in0=pk[:, 0:nwords],
                                            scalar1=4 * k, scalar2=15,
                                            op0=ALU.logical_shift_right,
                                            op1=ALU.bitwise_and),
                            v.tensor_copy(out=fv[:, 0:nwords], in_=t1[:, 0:nwords]),
                            v.tensor_single_scalar(out=tv[:, 0:nwords],
                                                   in_=fv[:, 0:nwords],
                                                   scalar=7.5, op=ALU.is_ge),
                            v.scalar_tensor_tensor(out=mv[:, 0:nwords],
                                                   in0=tv[:, 0:nwords], scalar=-8.0,
                                                   in1=fv[:, 0:nwords],
                                                   op0=ALU.mult, op1=ALU.add),
                            v.tensor_scalar(out=sg[:, 0:nwords], in0=tv[:, 0:nwords],
                                            scalar1=-2.0, scalar2=1.0,
                                            op0=ALU.mult, op1=ALU.add)])(k, nwords, ngrp),
                        deps=[ld, prev.get('kread')])
                    ru = em.v(
                        (lambda nwords: lambda v: v.tensor_scalar(
                            out=rv[:, 0:nwords], in0=mv[:, 0:nwords],
                            scalar1=-6.0, scalar2=0.0,
                            op0=ALU.add, op1=ALU.max))(nwords),
                        deps=[u])
                    u2 = em.v(
                        (lambda k, nwords, ngrp: lambda v: [
                            v.scalar_tensor_tensor(out=bv[:, 0:nwords],
                                                   in0=mv[:, 0:nwords], scalar=2.0,
                                                   in1=rv[:, 0:nwords],
                                                   op0=ALU.subtract, op1=ALU.add),
                            v.tensor_scalar(out=av[:, 0:nwords], in0=mv[:, 0:nwords],
                                            scalar1=0.5, scalar2=None, op0=ALU.mult),
                            v.tensor_tensor(out=bv[:, 0:nwords], in0=av[:, 0:nwords],
                                            in1=bv[:, 0:nwords], op=ALU.max),
                            v.tensor_tensor(out=bv[:, 0:nwords], in0=bv[:, 0:nwords],
                                            in1=sg[:, 0:nwords], op=ALU.mult),
                            v.tensor_tensor(
                                out=wrow[:].rearrange("p (j k8) -> p j k8", k8=8)
                                [:, 0:nwords, k].rearrange("p (g j2) -> p g j2", j2=16),
                                in0=bv[:, 0:nwords].rearrange("p (g j2) -> p g j2", j2=16),
                                in1=sc32[:, 0:ngrp].rearrange("p g -> p g 1")
                                if False else
                                sc32[:, 0:ngrp].broadcast_to([128, ngrp, 16]),
                                op=ALU.mult)])(k, nwords, ngrp),
                        deps=[ru, scc, prev.get('wrowread')])
                    kunits.append(u2)
                prev['kread'] = kunits[-1]
                prev['lastk'] = kunits[-1]
                trus = []
                tr_mm = []
                for kc in range(nk):
                    tp = ps_tp[kc % 2]
                    tb = trev_t[kc % 4]
                    tru = em.t(
                        (lambda kc, tp: lambda t: t.transpose(
                            tp[:, 0:128], wrow[:, kc * 128:(kc + 1) * 128],
                            c_id[:]))(kc, tp),
                        deps=kunits + [consts, prev.get(('tprd', kc % 2))])
                    tr_mm.append(tru)
                    tre = em.v(
                        (lambda tp, tb: lambda v: v.tensor_copy(out=tb[:],
                                                                in_=tp[:, 0:128]))(tp, tb),
                        deps=[tru, prev.get(('trd', kc % 4))])
                    prev[('tprd', kc % 2)] = tre
                    od = em.sync_dma(
                        (lambda rc, kc, out_ap, tb: lambda sy: sy.dma_start(
                            out=out_ap.ap()[kc][:, rc * 128:(rc + 1) * 128],
                            in_=tb[:]))(rc, kc, out_ap, tb),
                        deps=[tre])
                    prev[('trd', kc % 4)] = od
                    trus.append(od)
                    outs.append(od)
                prev['wrowread'] = tr_mm[-1]
        em.g(lambda g: g.engine_nop(), deps=outs[-12:])
        em.realize(block, sems, dma_sems)
    return nc


def host_inputs_G(inputs):
    gpk = np.asarray(inputs["L1_ffn_gate_up_packed"])
    gsc = np.asarray(inputs["L1_ffn_gate_up_scales"])
    dpk = np.asarray(inputs["L1_ffn_down_packed"])
    dsc = np.asarray(inputs["L1_ffn_down_scales"])
    ident = np.eye(128, dtype=np.float32).astype(BF)
    maps = []
    for c in range(NC):
        g_sl = slice(c * 1408, (c + 1) * 1408)
        d_sl = slice(c * 256, (c + 1) * 256)
        maps.append({
            "gup_pk": np.ascontiguousarray(gpk[g_sl]).reshape(11, 128, 256),
            "gup_sc": np.ascontiguousarray(gsc[:, g_sl].T).reshape(11, 128, 16),
            "dn_pk": np.ascontiguousarray(dpk[d_sl]).reshape(2, 128, 704),
            "dn_sc": np.ascontiguousarray(dsc[:, d_sl].T).reshape(2, 128, 44),
            "ident16": ident,
        })
    return maps


def run_G(inputs):
    nc = build_G()
    maps = host_inputs_G(inputs)
    res = run_bass_kernel_spmd(nc, maps, core_ids=list(range(NC)))
    return res.results



def build_F1():
    nc = bass.Bass("TRN2", target_bir_lowering=False, debug=False)
    so_h = nc.dram_tensor("so16", [32, 128, TCK], BF16, kind="ExternalInput")
    ga_h = nc.dram_tensor("gate16", [32, 128, TCK], BF16, kind="ExternalInput")
    x2_h = nc.dram_tensor("x2T", [16, 128, TCK], F32, kind="ExternalInput")
    nw_h = nc.dram_tensor("nw", [128, 32], F32, kind="ExternalInput")
    wout_h = nc.dram_tensor("wout2_t", [16, 32, 128, 128], F32, kind="ExternalInput")
    onesc_h = nc.dram_tensor("onesc", [128, 1], F32, kind="ExternalInput")
    onesr_h = nc.dram_tensor("onesr", [1, 128], F32, kind="ExternalInput")
    x3_h = nc.dram_tensor("x3", [16, 128, TCK], F32, kind="ExternalOutput")
    xn3_h = nc.dram_tensor("xn3", [16, 128, TCK], BF16, kind="ExternalOutput")

    st = ExitStack()
    with st:
        def SB(name, shape, dt):
            return st.enter_context(nc.sbuf_tensor(name, shape, dt))

        def PSUM(name, shape):
            return st.enter_context(nc.psum_tensor(name, shape, F32))
        c_onesc = SB("c_onesc", [128, 1], F32)
        c_onesr = SB("c_onesr", [1, 128], F32)
        c_nw = SB("c_nw", [128, 32], F32)
        sbuf_a = [SB("sba0", [128, TCK], BF16), SB("sba1", [128, TCK], BF16)]
        sbuf_b = [SB("sbb0", [128, TCK], BF16), SB("sbb1", [128, TCK], BF16)]
        g16 = SB("g16", [128, 32, TCK], BF16)
        xs_ = SB("x3pre", [128, 16, TCK], F32)
        wbuf = [SB("wbuf0", [128, 32 * 128], BF16), SB("wbuf1", [128, 32 * 128], BF16)]
        sq = SB("sq", [128, TCK], F32)
        rows = SB("rows", [1, 4 * TCK], F32)
        rrow2 = SB("rrow2", [1, TCK], F32)
        mean_b = SB("mean_b", [128, TCK], F32)
        rstd_b = SB("rstd_b", [128, TCK], F32)
        rstd2_b = SB("rstd2_b", [128, TCK], F32)
        xn3b = SB("xn3b", [128, TCK], BF16)
        psmm = Slots([PSUM(f"ps_m{i}", [128, 512]) for i in range(4)])
        ps_st = [PSUM(f"ps_s{i}", [1, 512]) for i in range(4)]
        sems = {e: st.enter_context(nc.semaphore("sem_" + e)) for e in Emitter.ENGS}
        dma_sems = [st.enter_context(nc.semaphore(f"dsem{i}")) for i in range(Emitter.NPOOL)]
        block = st.enter_context(nc.Block())
        em = Emitter(nc)

        consts = em.sync_dma(lambda sy: [
            sy.dma_start(out=c_onesc[:], in_=onesc_h.ap()),
            sy.dma_start(out=c_onesr[:], in_=onesr_h.ap()),
            sy.dma_start(out=c_nw[:], in_=nw_h.ap())], n=3)
        ldx = em.sync_dma(lambda sy: sy.dma_start(
            out=xs_[:], in_=x2_h.ap().rearrange("kc p t -> p kc t")))

        gat = [None] * 32
        sq_st = []
        for dc in range(32):
            ld = em.sync_dma(
                (lambda dc: lambda sy: [
                    sy.dma_start(out=sbuf_a[dc % 2][:], in_=so_h.ap()[dc]),
                    sy.dma_start(out=sbuf_b[dc % 2][:], in_=ga_h.ap()[dc])])(dc),
                deps=[gat[dc - 2] if dc >= 2 else None], n=2)
            gat[dc] = em.v(
                (lambda dc: lambda v: v.tensor_tensor(
                    out=g16[:, dc, :], in0=sbuf_a[dc % 2][:], in1=sbuf_b[dc % 2][:],
                    op=ALU.mult))(dc),
                deps=[ld])
            sqd = em.s(
                (lambda dc: lambda sc: sc.activation(
                    out=sq[:], in_=g16[:, dc, :], func=AF.Square))(dc),
                deps=[gat[dc], sq_st[-1] if sq_st else None])
            sq_st.append(em.t(
                (lambda dc: lambda t: [
                    t.matmul(ps_st[0][:, 0:512], c_onesc[:], sq[:, 0:512],
                             start=(dc == 0), stop=(dc == 31)),
                    t.matmul(ps_st[1][:, 0:512], c_onesc[:], sq[:, 512:1024],
                             start=(dc == 0), stop=(dc == 31))])(dc),
                deps=[sqd, consts]))
        g1c = em.s(lambda sc: [
            sc.copy(out=rrow2[:, 0:512], in_=ps_st[0][:, 0:512]),
            sc.copy(out=rrow2[:, 512:1024], in_=ps_st[1][:, 0:512])],
            deps=[sq_st[31]])
        g1 = em.v(lambda v: v.tensor_scalar(out=rrow2[:], in0=rrow2[:],
                                            scalar1=1.0 / INTER, scalar2=EPS5,
                                            op0=ALU.mult, op1=ALU.add), deps=[g1c])
        g2 = em.s(lambda sc: sc.activation(out=rrow2[:], in_=rrow2[:], func=AF.Sqrt),
                  deps=[g1])
        g3 = em.v(lambda v: v.reciprocal(out=rrow2[:], in_=rrow2[:]), deps=[g2])
        bc2 = [psmm.next(), psmm.next()]
        g4 = em.t(lambda t: [
            t.matmul(psmm.tile(bc2[0])[:, 0:512], c_onesr[:], rrow2[:, 0:512],
                     start=True, stop=True),
            t.matmul(psmm.tile(bc2[1])[:, 0:512], c_onesr[:], rrow2[:, 512:1024],
                     start=True, stop=True)], deps=[g3, consts])
        g5 = em.s(lambda sc: [
            sc.copy(out=rstd2_b[:, 0:512], in_=psmm.tile(bc2[0])[:, 0:512]),
            sc.copy(out=rstd2_b[:, 512:1024], in_=psmm.tile(bc2[1])[:, 0:512])],
            deps=[g4])
        for s_ in bc2:
            psmm.reader[s_] = g5
        norm_done = [None] * 32
        for dc in range(32):
            norm_done[dc] = em.v(
                (lambda dc: lambda v: v.scalar_tensor_tensor(
                    out=g16[:, dc, :], in0=g16[:, dc, :],
                    scalar=c_nw[:, dc:dc + 1], in1=rstd2_b[:],
                    op0=ALU.mult, op1=ALU.mult))(dc),
                deps=[g5, gat[31]])

        ldw = [None] * 16
        stat_units = []
        ev_units = []
        for mc in range(16):
            ldw[mc] = em.gp_dma(
                (lambda mc: lambda g: g.dma_start(
                    out=wbuf[mc % 2][:].rearrange("p (kc j) -> p kc j", j=128),
                    in_=wout_h.ap()[mc].rearrange("kc p j -> p kc j")))(mc),
                deps=[ev_units[-3] if mc >= 2 else None])
            for i, (o, w) in enumerate(NCP_TC):
                s_ = psmm.next()
                mm = em.t(
                    (lambda mc, o, w, s_: lambda t: [
                        t.matmul(psmm.tile(s_)[:, 0:w],
                                 wbuf[mc % 2][:, kc * 128:(kc + 1) * 128],
                                 g16[:, kc, o:o + w],
                                 start=(kc == 0), stop=(kc == 31))
                        for kc in range(32)])(mc, o, w, s_),
                    deps=[ldw[mc], psmm.reader[s_]] + (norm_done if mc == 0 else []))
                ev = em.v(
                    (lambda mc, o, w, s_: lambda v: v.tensor_tensor(
                        out=xs_[:, mc, o:o + w], in0=xs_[:, mc, o:o + w],
                        in1=psmm.tile(s_)[:, 0:w], op=ALU.add))(mc, o, w, s_),
                    deps=[mm, ldx])
                psmm.reader[s_] = ev
                ev_units.append(ev)
            sqd = em.s(
                (lambda mc: lambda sc: sc.activation(
                    out=sq[:], in_=xs_[:, mc, :], func=AF.Square))(mc),
                deps=[ev_units[-1], ev_units[-2],
                      stat_units[-1] if stat_units else None])
            stat_units.append(em.t(
                (lambda mc: lambda t: [
                    t.matmul(ps_st[2][:, 0:512], c_onesc[:], xs_[:, mc, 0:512],
                             start=(mc == 0), stop=(mc == 15)),
                    t.matmul(ps_st[3][:, 0:512], c_onesc[:], xs_[:, mc, 512:1024],
                             start=(mc == 0), stop=(mc == 15)),
                    t.matmul(ps_st[0][:, 0:512], c_onesc[:], sq[:, 0:512],
                             start=(mc == 0), stop=(mc == 15)),
                    t.matmul(ps_st[1][:, 0:512], c_onesc[:], sq[:, 512:1024],
                             start=(mc == 0), stop=(mc == 15))])(mc),
                deps=[sqd, consts, sq_st[31] if mc == 0 else None]))

        rrc = em.s(lambda sc: [
            sc.copy(out=rows[:, 0:512], in_=ps_st[2][:, 0:512]),
            sc.copy(out=rows[:, 512:1024], in_=ps_st[3][:, 0:512]),
            sc.copy(out=rows[:, 1024:1536], in_=ps_st[0][:, 0:512]),
            sc.copy(out=rows[:, 1536:2048], in_=ps_st[1][:, 0:512])],
            deps=[stat_units[15]])
        rr = em.v(lambda v: [
            v.tensor_scalar(out=rows[:, 2048:3072], in0=rows[:, 0:1024],
                            scalar1=1.0 / DM, scalar2=None, op0=ALU.mult),
            v.tensor_scalar(out=rows[:, 1024:2048], in0=rows[:, 1024:2048],
                            scalar1=1.0 / DM, scalar2=None, op0=ALU.mult),
            v.tensor_tensor(out=rows[:, 0:1024], in0=rows[:, 2048:3072],
                            in1=rows[:, 2048:3072], op=ALU.mult),
            v.tensor_tensor(out=rows[:, 3072:4096], in0=rows[:, 1024:2048],
                            in1=rows[:, 0:1024], op=ALU.subtract),
            v.tensor_scalar(out=rows[:, 3072:4096], in0=rows[:, 3072:4096],
                            scalar1=EPS5, scalar2=None, op0=ALU.add)],
            deps=[rrc])
        r2 = em.s(lambda sc: sc.activation(out=rows[:, 3072:4096],
                                           in_=rows[:, 3072:4096], func=AF.Sqrt),
                  deps=[rr])
        r3 = em.v(lambda v: v.reciprocal(out=rows[:, 3072:4096],
                                         in_=rows[:, 3072:4096]), deps=[r2])
        bcs = [psmm.next() for _ in range(4)]
        r4 = em.t(lambda t: [
            t.matmul(psmm.tile(bcs[0])[:, 0:512], c_onesr[:], rows[:, 2048:2560],
                     start=True, stop=True),
            t.matmul(psmm.tile(bcs[1])[:, 0:512], c_onesr[:], rows[:, 2560:3072],
                     start=True, stop=True),
            t.matmul(psmm.tile(bcs[2])[:, 0:512], c_onesr[:], rows[:, 3072:3584],
                     start=True, stop=True),
            t.matmul(psmm.tile(bcs[3])[:, 0:512], c_onesr[:], rows[:, 3584:4096],
                     start=True, stop=True)],
            deps=[r3] + [psmm.reader[s_] for s_ in bcs])
        r5 = em.s(lambda sc: [
            sc.copy(out=mean_b[:, 0:512], in_=psmm.tile(bcs[0])[:, 0:512]),
            sc.copy(out=mean_b[:, 512:1024], in_=psmm.tile(bcs[1])[:, 0:512]),
            sc.copy(out=rstd_b[:, 0:512], in_=psmm.tile(bcs[2])[:, 0:512]),
            sc.copy(out=rstd_b[:, 512:1024], in_=psmm.tile(bcs[3])[:, 0:512])],
            deps=[r4])
        for s_ in bcs:
            psmm.reader[s_] = r5
        outs = []
        for mc in range(16):
            ap1 = em.v(
                (lambda mc: lambda v: [
                    v.tensor_tensor(out=xs_[:, mc, :], in0=xs_[:, mc, :],
                                    in1=mean_b[:], op=ALU.subtract),
                    v.tensor_tensor(out=xs_[:, mc, :], in0=xs_[:, mc, :],
                                    in1=rstd_b[:], op=ALU.mult)])(mc),
                deps=[r5, stat_units[15]])
            cp = em.s(
                (lambda mc: lambda sc: sc.copy(out=xn3b[:], in_=xs_[:, mc, :]))(mc),
                deps=[ap1, outs[-1] if mc >= 1 else None])
            o1 = em.sync_dma(
                (lambda mc: lambda sy: [
                    sy.dma_start(out=x3_h.ap()[mc], in_=xs_[:, mc, :]),
                    sy.dma_start(out=xn3_h.ap()[mc], in_=xn3b[:])])(mc),
                deps=[cp], n=2)
            outs.append(o1)
        em.g(lambda g: g.engine_nop(), deps=outs)
        em.realize(block, sems, dma_sems)
    return nc


def host_inputs_F1(resD, resE, resC2, inputs):
    y_all = np.concatenate(
        [np.asarray(r["yg16"]).view(BF).reshape(512, S) for r in resE], axis=0)
    wout = lhsT_tiles(np.asarray(inputs["L1_attn_Wout"]))
    nw = percol(np.asarray(inputs["L1_attn_norm_w"]))
    maps = []
    for c in range(NC):
        cols = slice(c * TCK, (c + 1) * TCK)
        maps.append({
            "so16": np.ascontiguousarray(y_all[:, cols]).reshape(32, 128, TCK),
            "gate16": np.asarray(resD[c]["gate16"]).view(BF).reshape(32, 128, TCK),
            "x2T": np.asarray(resC2[c]["x2"]).reshape(16, 128, TCK),
            "nw": nw,
            "wout2_t": wout,
            "onesc": np.ones((128, 1), np.float32),
            "onesr": np.ones((1, 128), np.float32),
        })
    return maps


def run_F1(resD, resE, resC2, inputs):
    nc = build_F1()
    maps = host_inputs_F1(resD, resE, resC2, inputs)
    res = run_bass_kernel_spmd(nc, maps, core_ids=list(range(NC)))
    return res.results



def build_F2():
    nc = bass.Bass("TRN2", target_bir_lowering=False, debug=False)
    xn3_h = nc.dram_tensor("xn3", [16, 128, TCK], BF16, kind="ExternalInput")
    x3_h = nc.dram_tensor("x3", [16, 128, TCK], F32, kind="ExternalInput")
    wgu_h = nc.dram_tensor("wgu_t", [88, 16, 128, 128], BF16, kind="ExternalInput")
    wdn_h = nc.dram_tensor("wdn_t", [16, 44, 128, 128], BF16, kind="ExternalInput")
    onesc_h = nc.dram_tensor("onesc", [128, 1], F32, kind="ExternalInput")
    onesr_h = nc.dram_tensor("onesr", [1, 128], F32, kind="ExternalInput")
    out_h = nc.dram_tensor("outp", [16, 128, TCK], F32, kind="ExternalOutput")

    TT = 512
    st = ExitStack()
    with st:
        def SB(name, shape, dt):
            return st.enter_context(nc.sbuf_tensor(name, shape, dt))

        def PSUM(name, shape):
            return st.enter_context(nc.psum_tensor(name, shape, F32))
        c_onesc = SB("c_onesc", [128, 1], F32)
        c_onesr = SB("c_onesr", [1, 128], F32)
        xn3 = SB("xn3s", [128, 16, TCK], BF16)
        hT = SB("hT", [128, 44, TT], BF16)
        gs = SB("gs", [128, TT], BF16)
        wbuf = [SB("wbuf0", [128, 16 * 128], BF16), SB("wbuf1", [128, 16 * 128], BF16),
                SB("wbuf2", [128, 16 * 128], BF16), SB("wbuf3", [128, 16 * 128], BF16)]
        wdb = [SB("wdb0", [128, 44 * 128], BF16), SB("wdb1", [128, 44 * 128], BF16)]
        yf = SB("yf", [128, 16, TT], F32)
        x3h = SB("x3h", [128, 16, TT], F32)
        sq = SB("sq", [128, TT], F32)
        rows = SB("rows", [1, 4 * TT], F32)
        rrow2 = SB("rrow2", [1, TT], F32)
        mean_b = SB("mean_b", [128, TT], F32)
        rstd_b = SB("rstd_b", [128, TT], F32)
        rms_b = SB("rms_b", [128, TT], F32)
        psmm = Slots([PSUM(f"ps_m{i}", [128, 512]) for i in range(4)])
        ps_st = [PSUM(f"ps_s{i}", [1, 512]) for i in range(4)]
        sems = {e: st.enter_context(nc.semaphore("sem_" + e)) for e in Emitter.ENGS}
        dma_sems = [st.enter_context(nc.semaphore(f"dsem{i}")) for i in range(Emitter.NPOOL)]
        block = st.enter_context(nc.Block())
        em = Emitter(nc)

        consts = em.sync_dma(lambda sy: [
            sy.dma_start(out=c_onesc[:], in_=onesc_h.ap()),
            sy.dma_start(out=c_onesr[:], in_=onesr_h.ap())], n=2)
        ldxn = em.sync_dma(lambda sy: sy.dma_start(
            out=xn3[:], in_=xn3_h.ap().rearrange("kc p t -> p kc t")))

        outs_all = []
        prev = {}
        for th in range(2):
            ldx3 = em.sync_dma(
                (lambda th: lambda sy: sy.dma_start(
                    out=x3h[:],
                    in_=x3_h.ap().rearrange("kc p t -> p kc t")
                    [:, :, th * TT:(th + 1) * TT]))(th),
                deps=prev.get('outs', []))
            hdone = []
            for g in range(44):
                ldg = em.gp_dma(
                    (lambda g: lambda gg: gg.dma_start(
                        out=wbuf[g % 2][:].rearrange("p (kc j) -> p kc j", j=128),
                        in_=wgu_h.ap()[g].rearrange("kc p j -> p kc j")))(g),
                    deps=[prev.get(('gmm', g % 2))])
                s_ = psmm.next()
                gmm = em.t(
                    (lambda g, th, s_: lambda t: [
                        t.matmul(psmm.tile(s_)[:, 0:TT],
                                 wbuf[g % 2][:, kc * 128:(kc + 1) * 128],
                                 xn3[:, kc, th * TT:(th + 1) * TT],
                                 start=(kc == 0), stop=(kc == 15))
                        for kc in range(16)])(g, th, s_),
                    deps=[ldg, ldxn, psmm.reader[s_]])
                prev[('gmm', g % 2)] = gmm
                gev = em.s(
                    (lambda s_: lambda sc: sc.activation(
                        out=gs[:], in_=psmm.tile(s_)[:, 0:TT], func=AF.Silu))(s_),
                    deps=[gmm, prev.get('gsr')])
                psmm.reader[s_] = gev
                ldu = em.gp_dma(
                    (lambda g: lambda gg: gg.dma_start(
                        out=wbuf[2 + g % 2][:].rearrange("p (kc j) -> p kc j", j=128),
                        in_=wgu_h.ap()[44 + g].rearrange("kc p j -> p kc j")))(g),
                    deps=[prev.get(('umm', g % 2))])
                s2 = psmm.next()
                umm = em.t(
                    (lambda g, th, s2: lambda t: [
                        t.matmul(psmm.tile(s2)[:, 0:TT],
                                 wbuf[2 + g % 2][:, kc * 128:(kc + 1) * 128],
                                 xn3[:, kc, th * TT:(th + 1) * TT],
                                 start=(kc == 0), stop=(kc == 15))
                        for kc in range(16)])(g, th, s2),
                    deps=[ldu, ldxn, psmm.reader[s2]])
                prev[('umm', g % 2)] = umm
                hu = em.v(
                    (lambda g, s2: lambda v: v.tensor_tensor(
                        out=hT[:, g, :], in0=gs[:], in1=psmm.tile(s2)[:, 0:TT],
                        op=ALU.mult))(g, s2),
                    deps=[umm, gev, prev.get('hTr'), prev.get(('dmm', 0))])
                psmm.reader[s2] = hu
                prev['gsr'] = hu
                hdone.append(hu)
            dev_units = []
            rms_st = []
            for mc in range(16):
                ldd = em.gp_dma(
                    (lambda mc: lambda gg: gg.dma_start(
                        out=wdb[mc % 2][:].rearrange("p (kc j) -> p kc j", j=128),
                        in_=wdn_h.ap()[mc].rearrange("kc p j -> p kc j")))(mc),
                    deps=[prev.get(('dmm', mc % 2))])
                s_ = psmm.next()
                dmm = em.t(
                    (lambda mc, s_: lambda t: [
                        t.matmul(psmm.tile(s_)[:, 0:TT], wdb[mc % 2][:, kc * 128:(kc + 1) * 128],
                                 hT[:, kc, :], start=(kc == 0), stop=(kc == 43))
                        for kc in range(44)])(mc, s_),
                    deps=[ldd, psmm.reader[s_]] + (hdone if mc == 0 else []))
                prev[('dmm', mc % 2)] = dmm
                dev = em.s(
                    (lambda mc, s_: lambda sc: sc.copy(
                        out=yf[:, mc, :], in_=psmm.tile(s_)[:, 0:TT]))(mc, s_),
                    deps=[dmm] + (prev.get('outs', []) if mc == 0 else []))
                psmm.reader[s_] = dev
                dev_units.append(dev)
                sqd = em.v(
                    (lambda mc: lambda v: v.tensor_tensor(
                        out=sq[:], in0=yf[:, mc, :], in1=yf[:, mc, :],
                        op=ALU.mult))(mc),
                    deps=[dev, rms_st[-1] if rms_st else None])
                rms_st.append(em.t(
                    (lambda mc: lambda t: t.matmul(
                        ps_st[0][:, 0:512], c_onesc[:], sq[:],
                        start=(mc == 0), stop=(mc == 15)))(mc),
                    deps=[sqd, consts]))
            q1c = em.s(lambda sc: sc.copy(out=rrow2[:], in_=ps_st[0][:, 0:512]),
                       deps=[rms_st[15]])
            q1 = em.v(lambda v: v.tensor_scalar(out=rrow2[:], in0=rrow2[:],
                                                scalar1=1.0 / DM, scalar2=EPS6,
                                                op0=ALU.mult, op1=ALU.add), deps=[q1c])
            q2 = em.s(lambda sc: sc.activation(out=rrow2[:], in_=rrow2[:],
                                               func=AF.Sqrt), deps=[q1])
            q3 = em.v(lambda v: v.reciprocal(out=rrow2[:], in_=rrow2[:]), deps=[q2])
            bq = psmm.next()
            q4 = em.t(lambda t: t.matmul(psmm.tile(bq)[:, 0:512], c_onesr[:],
                                         rrow2[:], start=True, stop=True),
                      deps=[q3, consts, psmm.reader[bq]])
            q5 = em.s(lambda sc: sc.copy(out=rms_b[:], in_=psmm.tile(bq)[:, 0:512]),
                      deps=[q4])
            psmm.reader[bq] = q5
            st_units = []
            for mc in range(16):
                a1 = em.v(
                    (lambda mc: lambda v: [
                        v.tensor_tensor(out=yf[:, mc, :], in0=yf[:, mc, :],
                                        in1=rms_b[:], op=ALU.mult),
                        v.tensor_tensor(out=yf[:, mc, :], in0=yf[:, mc, :],
                                        in1=x3h[:, mc, :], op=ALU.add)])(mc),
                    deps=[q5, ldx3, rms_st[15]])
                sqd = em.s(
                    (lambda mc: lambda sc: sc.activation(
                        out=sq[:], in_=yf[:, mc, :], func=AF.Square))(mc),
                    deps=[a1, st_units[-1] if st_units else None])
                st_units.append(em.t(
                    (lambda mc: lambda t: [
                        t.matmul(ps_st[1][:, 0:512], c_onesc[:], yf[:, mc, :],
                                 start=(mc == 0), stop=(mc == 15)),
                        t.matmul(ps_st[2][:, 0:512], c_onesc[:], sq[:],
                                 start=(mc == 0), stop=(mc == 15))])(mc),
                    deps=[sqd, consts]))
            rrc = em.s(lambda sc: [
                sc.copy(out=rows[:, 0:512], in_=ps_st[1][:, 0:512]),
                sc.copy(out=rows[:, 512:1024], in_=ps_st[2][:, 0:512])],
                deps=[st_units[15]])
            rr = em.v(lambda v: [
                v.tensor_scalar(out=rows[:, 1024:1536], in0=rows[:, 0:512],
                                scalar1=1.0 / DM, scalar2=None, op0=ALU.mult),
                v.tensor_scalar(out=rows[:, 512:1024], in0=rows[:, 512:1024],
                                scalar1=1.0 / DM, scalar2=None, op0=ALU.mult),
                v.tensor_tensor(out=rows[:, 0:512], in0=rows[:, 1024:1536],
                                in1=rows[:, 1024:1536], op=ALU.mult),
                v.tensor_tensor(out=rows[:, 1536:2048], in0=rows[:, 512:1024],
                                in1=rows[:, 0:512], op=ALU.subtract),
                v.tensor_scalar(out=rows[:, 1536:2048], in0=rows[:, 1536:2048],
                                scalar1=EPS5, scalar2=None, op0=ALU.add)],
                deps=[rrc])
            r2 = em.s(lambda sc: sc.activation(out=rows[:, 1536:2048],
                                               in_=rows[:, 1536:2048], func=AF.Sqrt),
                      deps=[rr])
            r3 = em.v(lambda v: v.reciprocal(out=rows[:, 1536:2048],
                                             in_=rows[:, 1536:2048]), deps=[r2])
            bcs = [psmm.next(), psmm.next()]
            r4 = em.t(lambda t: [
                t.matmul(psmm.tile(bcs[0])[:, 0:512], c_onesr[:], rows[:, 1024:1536],
                         start=True, stop=True),
                t.matmul(psmm.tile(bcs[1])[:, 0:512], c_onesr[:], rows[:, 1536:2048],
                         start=True, stop=True)],
                deps=[r3] + [psmm.reader[s_] for s_ in bcs])
            r5 = em.s(lambda sc: [
                sc.copy(out=mean_b[:], in_=psmm.tile(bcs[0])[:, 0:512]),
                sc.copy(out=rstd_b[:], in_=psmm.tile(bcs[1])[:, 0:512])],
                deps=[r4])
            for s_ in bcs:
                psmm.reader[s_] = r5
            outs = []
            for mc in range(16):
                a2 = em.v(
                    (lambda mc: lambda v: [
                        v.tensor_tensor(out=yf[:, mc, :], in0=yf[:, mc, :],
                                        in1=mean_b[:], op=ALU.subtract),
                        v.tensor_tensor(out=yf[:, mc, :], in0=yf[:, mc, :],
                                        in1=rstd_b[:], op=ALU.mult)])(mc),
                    deps=[r5, st_units[15]])
                o1 = em.sync_dma(
                    (lambda mc, th: lambda sy: sy.dma_start(
                        out=out_h.ap()[mc][:, th * TT:(th + 1) * TT],
                        in_=yf[:, mc, :]))(mc, th),
                    deps=[a2])
                outs.append(o1)
            outs_all += outs
            prev['outs'] = list(outs)
            prev['hTr'] = prev[('dmm', 1)]
        em.g(lambda g: g.engine_nop(), deps=outs_all)
        em.realize(block, sems, dma_sems)
    return nc


def host_inputs_F2(resF1, resG, inputs):
    wgu = np.concatenate(
        [np.asarray(rr["wguT"]).view(BF) for rr in resG], axis=2).reshape(2048, 11264)
    wdn = np.concatenate(
        [np.asarray(rr["wdnT"]).view(BF) for rr in resG], axis=2).reshape(5632, 2048)
    wgu_t = np.ascontiguousarray(
        wgu.reshape(16, 128, 88, 128).transpose(2, 0, 1, 3))
    wdn_t = np.ascontiguousarray(
        wdn.reshape(44, 128, 16, 128).transpose(2, 0, 1, 3))
    maps = []
    for c in range(NC):
        maps.append({
            "xn3": np.asarray(resF1[c]["xn3"]).view(BF).reshape(16, 128, TCK),
            "x3": np.asarray(resF1[c]["x3"]).reshape(16, 128, TCK),
            "wgu_t": wgu_t,
            "wdn_t": wdn_t,
            "onesc": np.ones((128, 1), np.float32),
            "onesr": np.ones((1, 128), np.float32),
        })
    return maps


def run_F2(resF1, resG, inputs):
    nc = build_F2()
    maps = host_inputs_F2(resF1, resG, inputs)
    res = run_bass_kernel_spmd(nc, maps, core_ids=list(range(NC)))
    return res.results


# version 04
# speedup vs baseline: 1.0723x; 1.0723x over previous
from contextlib import ExitStack
import numpy as np
import ml_dtypes
import concourse.bass as bass
import concourse.mybir as mybir
from concourse.bass_utils import run_bass_kernel_spmd

F32 = mybir.dt.float32
BF16 = mybir.dt.bfloat16
F16 = mybir.dt.float16
I32 = mybir.dt.int32
U8 = mybir.dt.uint8
AF = mybir.ActivationFunctionType
ALU = mybir.AluOpType
BF = ml_dtypes.bfloat16

NC = 8
S = 8192
TCK = 1024
HALO = 4
TH = TCK + HALO
DM = 2048
DI = 4096
DSTATE = 16
EPS5 = 1e-5
EPS6 = 1e-6

NHEADS = 64
HEADDIM = 64
INTER = 4096
NGROUPS = 8
NSTATE = 128
CONVD = 6144
I2 = 5632

NCP_TH = [(0, 512), (512, 512), (1024, TH - 1024)]
NCP_TC = [(0, 512), (512, 512)]



class _Unit:
    __slots__ = ("eng", "fn", "deps", "dma", "n_dma", "flagged", "count_after",
                 "sem_idx", "sem_target")

    def __init__(self, eng, fn, deps, dma, n_dma):
        self.eng = eng
        self.fn = fn
        self.deps = [d for d in deps if d is not None]
        self.dma = dma
        self.n_dma = n_dma
        self.flagged = False
        self.count_after = None
        self.sem_idx = None
        self.sem_target = None

    def dep(self, *units):
        for u in units:
            if u is not None:
                self.deps.append(u)
        return self


class Emitter:
    ENGS = ("sync", "gpsimd", "vector", "scalar", "tensor")
    NPOOL = 12

    def __init__(self, nc):
        self.nc = nc
        self.units = {e: [] for e in self.ENGS}
        self.pool_i = 0
        self.pool_totals = [0] * self.NPOOL
        self.pool_last_unit = [None] * self.NPOOL

    def unit(self, eng, fn, deps=(), dma=False, n_dma=1):
        u = _Unit(eng, fn, deps, dma, n_dma)
        if dma:
            i = self.pool_i % self.NPOOL
            self.pool_i += 1
            u.sem_idx = i
            prev = self.pool_last_unit[i]
            if prev is not None:
                u.deps.append(prev)
            self.pool_totals[i] += 16 * n_dma
            u.sem_target = self.pool_totals[i]
            self.pool_last_unit[i] = u
        self.units[eng].append(u)
        return u

    def sync_dma(self, fn, deps=(), n=1):
        return self.unit("sync", fn, deps, dma=True, n_dma=n)

    def gp_dma(self, fn, deps=(), n=1):
        return self.unit("gpsimd", fn, deps, dma=True, n_dma=n)

    def v(self, fn, deps=()):
        return self.unit("vector", fn, deps)

    def s(self, fn, deps=()):
        return self.unit("scalar", fn, deps)

    def t(self, fn, deps=()):
        return self.unit("tensor", fn, deps)

    def g(self, fn, deps=()):
        return self.unit("gpsimd", fn, deps)

    def realize(self, block, sems, dma_sems):
        for e in self.ENGS:
            for u in self.units[e]:
                for d in u.deps:
                    if d.dma or d.eng != u.eng:
                        d.flagged = True
        for e in self.ENGS:
            c = 0
            for u in self.units[e]:
                if u.flagged and not u.dma:
                    c += 1
                u.count_after = c
        em = self

        def mk(engname):
            def body(eng):
                waited = {}
                for u in em.units[engname]:
                    for d in u.deps:
                        if d.dma:
                            key = ("dma", d.sem_idx)
                            tgt = d.sem_target
                            semh = dma_sems[d.sem_idx]
                        else:
                            if d.eng == engname:
                                continue
                            key = d.eng
                            tgt = d.count_after
                            semh = sems[d.eng]
                        if waited.get(key, -1) >= tgt:
                            continue
                        eng.wait_ge(semh, tgt)
                        waited[key] = tgt
                    ins = u.fn(eng)
                    if u.dma:
                        if not isinstance(ins, (list, tuple)):
                            ins = [ins]
                        assert len(ins) == u.n_dma, (
                            f"{engname} dma unit: expected {u.n_dma} dmas, got {len(ins)}")
                        for i_ in ins:
                            i_.then_inc(dma_sems[u.sem_idx], 16)
                    elif u.flagged:
                        if isinstance(ins, (list, tuple)):
                            ins = ins[-1]
                        if ins is None:
                            ins = eng.nop()
                        ins.then_inc(sems[u.eng], 1)
            return body

        block.sync(mk("sync"))
        block.gpsimd(mk("gpsimd"))
        block.vector(mk("vector"))
        block.scalar(mk("scalar"))
        block.tensor(mk("tensor"))


class Slots:

    def __init__(self, tiles):
        self.tiles = tiles
        self.i = 0
        self.reader = [None] * len(tiles)
        self.writer = [None] * len(tiles)

    def next(self):
        s = self.i % len(self.tiles)
        self.i += 1
        return s

    def tile(self, s):
        return self.tiles[s]



def lhsT_tiles(w):
    K, M = w.shape
    assert K % 128 == 0 and M % 128 == 0
    return np.ascontiguousarray(
        w.reshape(K // 128, 128, M // 128, 128).transpose(2, 0, 1, 3))


def percol(v):
    n = v.shape[0]
    return np.ascontiguousarray(v.reshape(n // 128, 128).T)


def tok_slices(x_t, halo):
    D = x_t.shape[0]
    out = []
    for c in range(NC):
        lo = c * TCK - halo
        sl = np.zeros((D, TCK + halo), x_t.dtype)
        if lo < 0:
            sl[:, halo:] = x_t[:, c * TCK:(c + 1) * TCK]
        else:
            sl[:, :] = x_t[:, lo:(c + 1) * TCK]
        out.append(np.ascontiguousarray(sl))
    return out



def build_A():
    nc = bass.Bass("TRN2", target_bir_lowering=False, debug=False)
    xT_h = nc.dram_tensor("xT", [16, 128, TH], F32, kind="ExternalInput")
    wn_h = nc.dram_tensor("wn", [128, 16], F32, kind="ExternalInput")
    cwt_h = nc.dram_tensor("cwt", [128, 128], F32, kind="ExternalInput")
    cb_h = nc.dram_tensor("cb", [128, 32], F32, kind="ExternalInput")
    bdt_h = nc.dram_tensor("bdt", [128, 32], F32, kind="ExternalInput")
    onesc_h = nc.dram_tensor("onesc", [128, 1], F32, kind="ExternalInput")
    onesr_h = nc.dram_tensor("onesr", [1, 128], F32, kind="ExternalInput")
    win_h = nc.dram_tensor("win_t", [64, 16, 128, 128], F32, kind="ExternalInput")
    wxp_h = nc.dram_tensor("wxp_t", [2, 32, 128, 128], F32, kind="ExternalInput")
    wdt_h = nc.dram_tensor("wdt_t", [32, 1, 128, 128], F32, kind="ExternalInput")

    xs_h = nc.dram_tensor("xs16", [32, 128, TCK], BF16, kind="ExternalOutput")
    res_h = nc.dram_tensor("res16", [32, 128, TCK], BF16, kind="ExternalOutput")
    delta_h = nc.dram_tensor("delta", [32, 128, TCK], F32, kind="ExternalOutput")
    bc_h = nc.dram_tensor("bc", [32, TCK], F32, kind="ExternalOutput")

    st = ExitStack()
    with st:
        def SB(name, shape, dt):
            return st.enter_context(nc.sbuf_tensor(name, shape, dt))

        def PSUM(name, shape):
            return st.enter_context(nc.psum_tensor(name, shape, F32))

        c_wn = SB("c_wn", [128, 16], F32)
        c_cwt = SB("c_cwt", [128, 128], F32)
        c_cb = SB("c_cb", [128, 32], F32)
        c_bdt = SB("c_bdt", [128, 32], F32)
        c_onesc = SB("c_onesc", [128, 1], F32)
        c_onesr = SB("c_onesr", [1, 128], F32)
        xbuf0 = SB("xbuf0", [128, TH], F32)
        xbuf1 = SB("xbuf1", [128, TH], F32)
        sq = SB("sq", [128, TH], F32)
        rrow = SB("rrow", [1, TH], F32)
        rstd_b = SB("rstd_b", [128, TH], F32)
        xn16 = SB("xn16", [128, 16, TH], BF16)
        wbuf0 = SB("wbuf0", [128, 2048], BF16)
        wbuf1 = SB("wbuf1", [128, 2048], BF16)
        xsraw = SB("xsraw", [128, TH], F32)
        cacc = SB("cacc", [128, TCK], F32)
        res0 = SB("res0", [128, TCK], BF16)
        res1 = SB("res1", [128, TCK], BF16)
        xs16s = SB("xs16s", [128, 32, TCK], BF16)
        wxp16 = SB("wxp16", [128, 2 * 32 * 128], BF16)
        wdt16 = SB("wdt16", [128, 32 * 128], BF16)
        dtr16 = SB("dtr16", [128, TCK], BF16)
        bcbuf = SB("bcbuf", [32, TCK], F32)
        sp1 = SB("sp1", [128, TCK], F32)
        sp2 = SB("sp2", [128, TCK], F32)
        delt0 = SB("delt0", [128, TCK], F32)
        delt1 = SB("delt1", [128, TCK], F32)
        ps_s0 = PSUM("ps_s0", [1, 512])
        ps_s1 = PSUM("ps_s1", [1, 512])
        ps_s2 = PSUM("ps_s2", [1, 512])
        ps_m0 = PSUM("ps_m0", [128, 512])
        ps_m1 = PSUM("ps_m1", [128, 512])
        ps_m2 = PSUM("ps_m2", [128, 512])
        ps_m3 = PSUM("ps_m3", [128, 512])
        sems = {e: st.enter_context(nc.semaphore("sem_" + e)) for e in Emitter.ENGS}
        dma_sems = [st.enter_context(nc.semaphore(f"dsem{i}")) for i in range(Emitter.NPOOL)]
        block = st.enter_context(nc.Block())
        em = Emitter(nc)
        psum_stats = [ps_s0, ps_s1, ps_s2]
        xbufs = [xbuf0, xbuf1]
        wbufs = [wbuf0, wbuf1]
        ps = Slots([ps_m0, ps_m1, ps_m2, ps_m3])
        resb = [res0, res1]

        consts = em.sync_dma(lambda sy: [
            sy.dma_start(out=c_wn[:], in_=wn_h.ap()),
            sy.dma_start(out=c_cwt[:], in_=cwt_h.ap()),
            sy.dma_start(out=c_cb[:], in_=cb_h.ap()),
            sy.dma_start(out=c_bdt[:], in_=bdt_h.ap()),
            sy.dma_start(out=c_onesc[:], in_=onesc_h.ap()),
            sy.dma_start(out=c_onesr[:], in_=onesr_h.ap()),
        ], n=6)
        ldwx = em.gp_dma(lambda g: g.dma_start(
            out=wxp16[:].rearrange("p (g j) -> p g j", j=128),
            in_=wxp_h.ap().rearrange("m kc p j -> p (m kc) j")))
        ldwd = em.gp_dma(lambda g: g.dma_start(
            out=wdt16[:].rearrange("p (dc j) -> p dc j", j=128),
            in_=wdt_h.ap().rearrange("dc one p j -> p (dc one) j")))

        sq_done = [None] * 16
        stat_done = [None] * 16
        for kc in range(16):
            ld = em.sync_dma(
                (lambda kc: lambda sy: sy.dma_start(
                    out=xbufs[kc % 2][:], in_=xT_h.ap()[kc]))(kc),
                deps=[sq_done[kc - 2] if kc >= 2 else None])
            sq_done[kc] = em.s(
                (lambda kc: lambda sc: sc.activation(
                    out=sq[:], in_=xbufs[kc % 2][:], func=AF.Square))(kc),
                deps=[ld, stat_done[kc - 1] if kc >= 1 else None])
            stat_done[kc] = em.t(
                (lambda kc: lambda t: [
                    t.matmul(psum_stats[i][:, 0:w], c_onesc[:], sq[:, o:o + w],
                             start=(kc == 0), stop=(kc == 15))
                    for i, (o, w) in enumerate(NCP_TH)])(kc),
                deps=[sq_done[kc], consts])

        r1c = em.s(lambda sc: [
            sc.copy(out=rrow[:, o:o + w], in_=psum_stats[i][:, 0:w])
            for i, (o, w) in enumerate(NCP_TH)],
            deps=[stat_done[15]])
        r1 = em.v(lambda v: v.tensor_scalar(out=rrow[:], in0=rrow[:], scalar1=1.0 / DM,
                                            scalar2=EPS5, op0=ALU.mult, op1=ALU.add),
                  deps=[r1c])
        r2 = em.s(lambda sc: sc.activation(out=rrow[:], in_=rrow[:], func=AF.Sqrt),
                  deps=[r1])
        r3 = em.v(lambda v: v.reciprocal(out=rrow[:], in_=rrow[:]), deps=[r2])
        bc_slots = [ps.next() for _ in range(3)]
        r4 = em.t(lambda t: [
            t.matmul(ps.tile(bc_slots[i])[:, 0:w], c_onesr[:], rrow[:, o:o + w],
                     start=True, stop=True)
            for i, (o, w) in enumerate(NCP_TH)], deps=[r3, consts])
        r5 = em.s(lambda sc: [
            sc.copy(out=rstd_b[:, o:o + w], in_=ps.tile(bc_slots[i])[:, 0:w])
            for i, (o, w) in enumerate(NCP_TH)], deps=[r4])
        for s_ in bc_slots:
            ps.reader[s_] = r5

        xn_done = [None] * 16
        for kc in range(16):
            ld = em.sync_dma(
                (lambda kc: lambda sy: sy.dma_start(
                    out=xbufs[kc % 2][:], in_=xT_h.ap()[kc]))(kc),
                deps=[xn_done[kc - 2] if kc >= 2 else sq_done[15 - (1 - kc)]])
            xn_done[kc] = em.v(
                (lambda kc: lambda v: v.scalar_tensor_tensor(
                    out=xn16[:, kc, :], in0=xbufs[kc % 2][:],
                    scalar=c_wn[:, kc:kc + 1], in1=rstd_b[:],
                    op0=ALU.mult, op1=ALU.mult))(kc),
                deps=[ld, r5, consts])

        ldw_units = [None] * 64
        conv_done = [None] * 32
        resdma = [None] * 32
        last_mm_user_of_wbuf = [None, None]
        for mc in range(64):
            ldw_units[mc] = em.gp_dma(
                (lambda mc: lambda g: g.dma_start(
                    out=wbufs[mc % 2][:].rearrange("p (kc j) -> p kc j", j=128),
                    in_=win_h.ap()[mc].rearrange("kc p j -> p kc j")))(mc),
                deps=[last_mm_user_of_wbuf[mc % 2]])
            evacs = []
            for i, (o, w) in enumerate(NCP_TH):
                s_ = ps.next()
                mm = em.t(
                    (lambda mc, o, w, s_: lambda t: [
                        t.matmul(ps.tile(s_)[:, 0:w],
                                 wbufs[mc % 2][:, kc * 128:(kc + 1) * 128],
                                 xn16[:, kc, o:o + w],
                                 start=(kc == 0), stop=(kc == 15))
                        for kc in range(16)])(mc, o, w, s_),
                    deps=[ldw_units[mc], xn_done[15], ps.reader[s_]])
                if mc < 32:
                    ev = em.s(
                        (lambda o, w, s_: lambda sc: sc.copy(
                            out=xsraw[:, o:o + w], in_=ps.tile(s_)[:, 0:w]))(o, w, s_),
                        deps=[mm, conv_done[mc - 1] if (i == 0 and mc >= 1) else None])
                else:
                    def resf(o, w, s_, mc):
                        def fn(sc):
                            if o == 0:
                                return sc.activation(out=resb[mc % 2][:, 0:w - HALO],
                                                     in_=ps.tile(s_)[:, HALO:w],
                                                     func=AF.Silu)
                            return sc.activation(
                                out=resb[mc % 2][:, o - HALO:o - HALO + w],
                                in_=ps.tile(s_)[:, 0:w], func=AF.Silu)
                        return fn
                    ev = em.s(resf(o, w, s_, mc),
                              deps=[mm, resdma[mc - 2 - 32] if (i == 0 and mc >= 34) else None])
                ps.reader[s_] = ev
                evacs.append(ev)
            last_mm_user_of_wbuf[mc % 2] = mm
            if mc < 32:
                cd = em.v(
                    (lambda mc: lambda v: [
                        v.tensor_scalar(out=cacc[:], in0=xsraw[:, 1:1 + TCK],
                                        scalar1=c_cwt[:, mc * 4:mc * 4 + 1],
                                        scalar2=c_cb[:, mc:mc + 1],
                                        op0=ALU.mult, op1=ALU.add),
                        *[v.scalar_tensor_tensor(
                            out=cacc[:], in0=xsraw[:, 1 + wt:1 + wt + TCK],
                            scalar=c_cwt[:, mc * 4 + wt:mc * 4 + wt + 1],
                            in1=cacc[:], op0=ALU.mult, op1=ALU.add)
                          for wt in range(1, 4)]])(mc),
                    deps=evacs + [conv_done[mc - 1] if mc >= 1 else None])
                conv_done[mc] = em.s(
                    (lambda mc: lambda sc: sc.activation(
                        out=xs16s[:, mc, :], in_=cacc[:], func=AF.Silu))(mc),
                    deps=[cd])
            else:
                resdma[mc - 32] = em.sync_dma(
                    (lambda mc: lambda sy: sy.dma_start(
                        out=res_h.ap()[mc - 32], in_=resb[mc % 2][:]))(mc),
                    deps=evacs)

        xs_out = em.sync_dma(lambda sy: [
            sy.dma_start(out=xs_h.ap()[mcc], in_=xs16s[:, mcc, :])
            for mcc in range(32)], deps=list(conv_done), n=32)

        xdbc_ev = {}
        for m2 in range(2):
            for i, (o, w) in enumerate(NCP_TC):
                s_ = ps.next()
                mm = em.t(
                    (lambda m2, o, w, s_: lambda t: [
                        t.matmul(ps.tile(s_)[:, 0:w],
                                 wxp16[:, (m2 * 32 + kc) * 128:(m2 * 32 + kc + 1) * 128],
                                 xs16s[:, kc, o:o + w],
                                 start=(kc == 0), stop=(kc == 31))
                        for kc in range(32)])(m2, o, w, s_),
                    deps=[ldwx, ps.reader[s_]] + list(conv_done))
                if m2 == 0:
                    ev = em.s(
                        (lambda o, w, s_: lambda sc: sc.copy(
                            out=dtr16[:, o:o + w], in_=ps.tile(s_)[:, 0:w]))(o, w, s_),
                        deps=[mm])
                else:
                    ev = em.v(
                        (lambda o, w, s_: lambda v: v.tensor_copy(
                            out=bcbuf[:, o:o + w], in_=ps.tile(s_)[0:32, 0:w]))(o, w, s_),
                        deps=[mm])
                ps.reader[s_] = ev
                xdbc_ev[(m2, i)] = ev
        bc_out = em.sync_dma(
            lambda sy: sy.dma_start(out=bc_h.ap(), in_=bcbuf[:]),
            deps=[xdbc_ev[(1, 0)], xdbc_ev[(1, 1)]])

        deltb = [delt0, delt1]
        add_done = [None] * 32
        ddma = [None] * 32
        for dc in range(32):
            acts = []
            for i, (o, w) in enumerate(NCP_TC):
                s_ = ps.next()
                mm = em.t(
                    (lambda dc, o, w, s_: lambda t: t.matmul(
                        ps.tile(s_)[:, 0:w], wdt16[:, dc * 128:(dc + 1) * 128],
                        dtr16[:, o:o + w], start=True, stop=True))(dc, o, w, s_),
                    deps=[ldwd, xdbc_ev[(0, 0)], xdbc_ev[(0, 1)], ps.reader[s_]])
                act = em.s(
                    (lambda dc, o, w, s_: lambda sc: [
                        sc.activation(out=sp1[:, o:o + w], in_=ps.tile(s_)[:, 0:w],
                                      func=AF.Abs, bias=c_bdt[:, dc:dc + 1]),
                        sc.activation(out=sp1[:, o:o + w], in_=sp1[:, o:o + w],
                                      func=AF.Exp, scale=-1.0),
                        sc.activation(out=sp1[:, o:o + w], in_=sp1[:, o:o + w],
                                      func=AF.Ln, bias=1.0),
                        sc.activation(out=sp2[:, o:o + w], in_=ps.tile(s_)[:, 0:w],
                                      func=AF.Relu, bias=c_bdt[:, dc:dc + 1]),
                    ][-1])(dc, o, w, s_),
                    deps=[mm, add_done[dc - 1] if (i == 0 and dc >= 1) else None])
                ps.reader[s_] = act
                acts.append(act)
            add_done[dc] = em.v(
                (lambda dc: lambda v: v.tensor_tensor(
                    out=deltb[dc % 2][:], in0=sp1[:], in1=sp2[:], op=ALU.add))(dc),
                deps=acts + [ddma[dc - 2] if dc >= 2 else None])
            ddma[dc] = em.sync_dma(
                (lambda dc: lambda sy: sy.dma_start(
                    out=delta_h.ap()[dc], in_=deltb[dc % 2][:]))(dc),
                deps=[add_done[dc]])

        em.g(lambda g: g.engine_nop(),
             deps=[xs_out, bc_out] + ddma + resdma)

        em.realize(block, sems, dma_sems)
    return nc


def host_inputs_A(inputs):
    x = np.asarray(inputs["L0_attn_x"])[0]
    xT = np.ascontiguousarray(x.T)
    xs = tok_slices(xT, HALO)
    w_in = np.asarray(inputs["L0_attn_W_in"])
    w_xp = np.asarray(inputs["L0_attn_W_xp"])
    w_xp_pad = np.zeros((DI, 256), np.float32)
    w_xp_pad[:, :160] = w_xp
    w_dt = np.asarray(inputs["L0_attn_W_dt"])
    conv_w = np.asarray(inputs["L0_attn_conv_w"])
    cwt = np.ascontiguousarray(
        conv_w.T.reshape(32, 128, 4).transpose(1, 0, 2).reshape(128, 128))
    shared = {
        "wn": percol(np.asarray(inputs["L0_attn_w_norm"])),
        "cwt": cwt,
        "cb": percol(np.asarray(inputs["L0_attn_conv_b"])),
        "bdt": percol(np.asarray(inputs["L0_attn_b_dt"])),
        "onesc": np.ones((128, 1), np.float32),
        "onesr": np.ones((1, 128), np.float32),
        "win_t": lhsT_tiles(w_in),
        "wxp_t": lhsT_tiles(w_xp_pad),
        "wdt_t": lhsT_tiles(w_dt),
    }
    maps = []
    for c in range(NC):
        m = dict(shared)
        m["xT"] = xs[c].reshape(16, 128, TH)
        maps.append(m)
    return maps


def run_A(inputs):
    nc = build_A()
    maps = host_inputs_A(inputs)
    res = run_bass_kernel_spmd(nc, maps, core_ids=list(range(NC)))
    return res.results


def kernel(**inputs):
    resG = run_G(inputs)
    resA = run_A(inputs)
    resB = run_B(resA, inputs)
    resC1 = run_C1(resA, resB, inputs)
    resC2 = run_C2(resC1, inputs)
    resD = run_D(resC2, inputs)
    resE = run_E(resD, inputs)
    resF1 = run_F1(resD, resE, resC2, inputs)
    resF2 = run_F2(resF1, resG, inputs)
    out = np.concatenate(
        [np.asarray(rr["outp"]).reshape(DM, TCK) for rr in resF2], axis=1).T[None]
    return np.ascontiguousarray(out.astype(np.float32))



def build_B():
    nc = bass.Bass("TRN2", target_bir_lowering=False, debug=False)
    delta_h = nc.dram_tensor("delta_s", [4, 128, S], F32, kind="ExternalInput")
    xs_h = nc.dram_tensor("xs_s", [4, 128, S], BF16, kind="ExternalInput")
    bf_h = nc.dram_tensor("bfull", [16, S], F32, kind="ExternalInput")
    cf_h = nc.dram_tensor("cfull", [16, S], F32, kind="ExternalInput")
    alog_h = nc.dram_tensor("alog_s", [128, 64], F32, kind="ExternalInput")
    sel_h = nc.dram_tensor("sel16", [16, 16 * 128], BF16, kind="ExternalInput")
    d_h = nc.dram_tensor("d_s", [128, 4], F32, kind="ExternalInput")
    g_h = nc.dram_tensor("y16", [4, 128, S], BF16, kind="ExternalOutput")

    st = ExitStack()
    with st:
        def SB(name, shape, dt):
            return st.enter_context(nc.sbuf_tensor(name, shape, dt))
        deltab = SB("deltab", [128, S], F32)
        xsb = SB("xsb", [128, S], BF16)
        dub = SB("dub", [128, S], BF16)
        yb = SB("yb", [128, S], BF16)
        dab = SB("dab", [128, S], F32)
        dbub = SB("dbub", [128, S], BF16)
        hb = dbub
        bbc = SB("bbc", [128, S], BF16)
        cbc = SB("cbc", [128, S], BF16)
        asb = SB("asb", [128, 64], F32)
        dsb = SB("dsb", [128, 4], F32)
        bfs = SB("bfs", [16, S], BF16)
        cfs = SB("cfs", [16, S], BF16)
        c_sel = SB("c_sel", [16, 16 * 128], BF16)
        psb = [st.enter_context(nc.psum_tensor(f"psb{i}", [128, 512], F32))
               for i in range(4)]
        sems = {e: st.enter_context(nc.semaphore("sem_" + e)) for e in Emitter.ENGS}
        dma_sems = [st.enter_context(nc.semaphore(f"dsem{i}")) for i in range(Emitter.NPOOL)]
        block = st.enter_context(nc.Block())
        em = Emitter(nc)

        lda = em.sync_dma(lambda sy: [
            sy.dma_start(out=asb[:], in_=alog_h.ap()),
            sy.dma_start(out=dsb[:], in_=d_h.ap()),
            sy.dma_start(out=c_sel[:], in_=sel_h.ap())], n=3)
        ldbc = em.gp_dma(lambda g: [
            g.dma_start(out=bfs[:], in_=bf_h.ap()),
            g.dma_start(out=cfs[:], in_=cf_h.ap())], n=2)
        a1 = em.s(lambda sc: sc.activation(out=asb[:], in_=asb[:], func=AF.Exp),
                  deps=[lda])
        a2 = em.v(lambda v: v.tensor_scalar(out=asb[:], in0=asb[:], scalar1=-1.0,
                                            scalar2=None, op0=ALU.mult), deps=[a1])

        gdma = [None] * 4
        for dc in range(4):
            ldd = em.sync_dma(
                (lambda dc: lambda sy: [
                    sy.dma_start(out=deltab[:], in_=delta_h.ap()[dc]),
                    sy.dma_start(out=xsb[:], in_=xs_h.ap()[dc])])(dc),
                deps=[gdma[dc - 1] if dc >= 1 else None], n=2)
            du = em.v((lambda: lambda v: v.tensor_tensor(
                out=dub[:], in0=deltab[:], in1=xsb[:], op=ALU.mult))(),
                deps=[ldd])
            yinit = em.v((lambda dc: lambda v: v.tensor_scalar(
                out=yb[:], in0=xsb[:], scalar1=dsb[:, dc:dc + 1], scalar2=None,
                op0=ALU.mult))(dc), deps=[ldd, lda,
                                          gdma[dc - 1] if dc >= 1 else None])
            prev_acc = yinit
            prev_dbu = None
            prev_prod = None
            prev_scan = None
            for n in range(DSTATE):
                if dc == 0 and n == 0:
                    prev_psb = [None] * 4

                ev_units = []
                for piece in range(16):
                    pb = psb[piece % 4]
                    mmb = em.t(
                        (lambda n, piece, pb: lambda t: [
                            t.matmul(pb[:, 0:512], c_sel[:, n * 128:(n + 1) * 128],
                                     bfs[:, piece * 512:(piece + 1) * 512],
                                     start=True, stop=True)])(n, piece, pb),
                        deps=[lda, ldbc, prev_psb[piece % 4]])
                    evb = em.s(
                        (lambda piece, pb: lambda sc: sc.copy(
                            out=bbc[:, piece * 512:(piece + 1) * 512],
                            in_=pb[:, 0:512]))(piece, pb),
                        deps=[mmb, prev_dbu])
                    pc = psb[(piece + 2) % 4] if False else None
                    mmc = em.t(
                        (lambda n, piece, pb: lambda t: [
                            t.matmul(pb[:, 0:512], c_sel[:, n * 128:(n + 1) * 128],
                                     cfs[:, piece * 512:(piece + 1) * 512],
                                     start=True, stop=True)])(n, piece, pb),
                        deps=[lda, evb])
                    evc = em.s(
                        (lambda piece, pb: lambda sc: sc.copy(
                            out=cbc[:, piece * 512:(piece + 1) * 512],
                            in_=pb[:, 0:512]))(piece, pb),
                        deps=[mmc, prev_prod])
                    prev_psb[piece % 4] = evc
                    ev_units.append(evb)
                    ev_units.append(evc)
                ldb = ev_units[-1]
                da = em.s(
                    (lambda dc, n: lambda sc: sc.activation(
                        out=dab[:], in_=deltab[:], func=AF.Exp,
                        scale=asb[:, dc * 16 + n:dc * 16 + n + 1]))(dc, n),
                    deps=[ldd, a2, prev_scan])
                dbu = em.v(
                    lambda v: v.tensor_tensor(out=dbub[:], in0=dub[:], in1=bbc[:],
                                              op=ALU.mult),
                    deps=[du, ldb, prev_scan])
                hsc = em.v(
                    lambda v: v.tensor_tensor_scan(out=hb[:], data0=dab[:],
                                                   data1=dbub[:], initial=0.0,
                                                   op0=ALU.mult, op1=ALU.add),
                    deps=[da, dbu, prev_prod])
                prod = em.v(
                    lambda v: v.tensor_tensor(out=hb[:], in0=hb[:], in1=cbc[:],
                                              op=ALU.mult),
                    deps=[hsc, ldb])
                prev_acc = em.v(
                    lambda v: v.tensor_tensor(out=yb[:], in0=yb[:], in1=hb[:],
                                              op=ALU.add),
                    deps=[prod, prev_acc])
                prev_dbu = dbu
                prev_prod = prev_acc
                prev_scan = hsc
            gdma[dc] = em.sync_dma(
                (lambda dc: lambda sy: sy.dma_start(out=g_h.ap()[dc], in_=yb[:]))(dc),
                deps=[prev_acc])
        em.g(lambda g: g.engine_nop(), deps=gdma)
        em.realize(block, sems, dma_sems)
    return nc


def host_inputs_B(resA, inputs):
    delta_all = np.concatenate(
        [np.asarray(r["delta"]).reshape(DI, TCK) for r in resA], axis=1)
    xs_all = np.concatenate(
        [np.asarray(r["xs16"]).view(BF).reshape(DI, TCK) for r in resA], axis=1)
    res_all = np.concatenate(
        [np.asarray(r["res16"]).view(BF).reshape(DI, TCK) for r in resA], axis=1)
    bc_all = np.concatenate(
        [np.asarray(r["bc"]).reshape(32, TCK) for r in resA], axis=1)
    alog = np.asarray(inputs["L0_attn_A_log"])
    dvec = np.asarray(inputs["L0_attn_D"])
    sel16 = np.zeros((16, 16, 128), np.float32)
    for n_ in range(16):
        sel16[n_, n_, :] = 1.0
    sel16 = np.ascontiguousarray(sel16.transpose(1, 0, 2).reshape(16, 16 * 128)).astype(BF)
    maps = []
    for c in range(NC):
        sl = slice(c * 512, (c + 1) * 512)
        m = {
            "delta_s": np.ascontiguousarray(delta_all[sl]).reshape(4, 128, S),
            "xs_s": np.ascontiguousarray(xs_all[sl]).reshape(4, 128, S),
            "bfull": np.ascontiguousarray(bc_all[0:16]),
            "cfull": np.ascontiguousarray(bc_all[16:32]),
            "alog_s": np.ascontiguousarray(
                alog[sl].reshape(4, 128, 16).transpose(1, 0, 2).reshape(128, 64)),
            "sel16": sel16,
            "d_s": np.ascontiguousarray(dvec[sl].reshape(4, 128).T),
        }
        maps.append(m)
    return maps


def run_B(resA, inputs):
    nc = build_B()
    maps = host_inputs_B(resA, inputs)
    res = run_bass_kernel_spmd(nc, maps, core_ids=list(range(NC)))
    return res.results



def build_C1():
    nc = bass.Bass("TRN2", target_bir_lowering=False, debug=False)
    y_h = nc.dram_tensor("y16", [32, 128, TCK], BF16, kind="ExternalInput")
    res_h = nc.dram_tensor("res16", [32, 128, TCK], BF16, kind="ExternalInput")
    xT_h = nc.dram_tensor("xT", [16, 128, TCK], F32, kind="ExternalInput")
    wout_h = nc.dram_tensor("wout_t", [16, 32, 128, 128], F32, kind="ExternalInput")
    onesc_h = nc.dram_tensor("onesc", [128, 1], F32, kind="ExternalInput")
    onesr_h = nc.dram_tensor("onesr", [1, 128], F32, kind="ExternalInput")
    x1_h = nc.dram_tensor("x1", [16, 128, TCK], F32, kind="ExternalOutput")
    xn1_h = nc.dram_tensor("xn1", [16, 128, TCK], BF16, kind="ExternalOutput")

    st = ExitStack()
    with st:
        def SB(name, shape, dt):
            return st.enter_context(nc.sbuf_tensor(name, shape, dt))

        def PSUM(name, shape, dt=F32):
            return st.enter_context(nc.psum_tensor(name, shape, dt))
        c_onesc = SB("c_onesc", [128, 1], F32)
        c_onesr = SB("c_onesr", [1, 128], F32)
        ybuf = [SB("ybuf0", [128, TCK], BF16), SB("ybuf1", [128, TCK], BF16)]
        rbuf = [SB("rbuf0", [128, TCK], BF16), SB("rbuf1", [128, TCK], BF16)]
        gated = SB("gated", [128, 32, TCK], BF16)
        xs_ = SB("x1pre", [128, 16, TCK], F32)
        wbuf = [SB("wbuf0", [128, 32 * 128], BF16), SB("wbuf1", [128, 32 * 128], BF16)]
        sq = SB("sq", [128, TCK], F32)
        rows = SB("rows", [1, 4 * TCK], F32)
        mean_b = SB("mean_b", [128, TCK], F32)
        rstd_b = SB("rstd_b", [128, TCK], F32)
        xn1b = SB("xn1b", [128, TCK], BF16)
        psmm = Slots([PSUM(f"ps_m{i}", [128, 512]) for i in range(4)])
        ps_st = [PSUM(f"ps_s{i}", [1, 512]) for i in range(4)]
        sems = {e: st.enter_context(nc.semaphore("sem_" + e)) for e in Emitter.ENGS}
        dma_sems = [st.enter_context(nc.semaphore(f"dsem{i}")) for i in range(Emitter.NPOOL)]
        block = st.enter_context(nc.Block())
        em = Emitter(nc)

        consts = em.sync_dma(lambda sy: [
            sy.dma_start(out=c_onesc[:], in_=onesc_h.ap()),
            sy.dma_start(out=c_onesr[:], in_=onesr_h.ap())], n=2)
        ldx = em.sync_dma(lambda sy: sy.dma_start(
            out=xs_[:], in_=xT_h.ap().rearrange("kc p t -> p kc t")))

        gat = [None] * 32
        for dc in range(32):
            ld = em.sync_dma(
                (lambda dc: lambda sy: [
                    sy.dma_start(out=ybuf[dc % 2][:], in_=y_h.ap()[dc]),
                    sy.dma_start(out=rbuf[dc % 2][:], in_=res_h.ap()[dc])])(dc),
                deps=[gat[dc - 2] if dc >= 2 else None], n=2)
            gat[dc] = em.v(
                (lambda dc: lambda v: v.tensor_tensor(
                    out=gated[:, dc, :], in0=ybuf[dc % 2][:], in1=rbuf[dc % 2][:],
                    op=ALU.mult))(dc),
                deps=[ld])

        ldw = [None] * 16
        stat_units = []
        ev_units = []
        for mc in range(16):
            ldw[mc] = em.gp_dma(
                (lambda mc: lambda g: g.dma_start(
                    out=wbuf[mc % 2][:].rearrange("p (kc j) -> p kc j", j=128),
                    in_=wout_h.ap()[mc].rearrange("kc p j -> p kc j")))(mc),
                deps=[ev_units[-3] if mc >= 2 else None])
            for i, (o, w) in enumerate(NCP_TC):
                s_ = psmm.next()
                mm = em.t(
                    (lambda mc, o, w, s_: lambda t: [
                        t.matmul(psmm.tile(s_)[:, 0:w],
                                 wbuf[mc % 2][:, kc * 128:(kc + 1) * 128],
                                 gated[:, kc, o:o + w],
                                 start=(kc == 0), stop=(kc == 31))
                        for kc in range(32)])(mc, o, w, s_),
                    deps=[ldw[mc], psmm.reader[s_]] + ([gat[31]] if mc == 0 else []))
                ev = em.v(
                    (lambda mc, o, w, s_: lambda v: v.scalar_tensor_tensor(
                        out=xs_[:, mc, o:o + w], in0=xs_[:, mc, o:o + w],
                        scalar=2.0, in1=psmm.tile(s_)[:, 0:w],
                        op0=ALU.mult, op1=ALU.add))(mc, o, w, s_),
                    deps=[mm, ldx])
                psmm.reader[s_] = ev
                ev_units.append(ev)
            sqd = em.s(
                (lambda mc: lambda sc: sc.activation(
                    out=sq[:], in_=xs_[:, mc, :], func=AF.Square))(mc),
                deps=[ev_units[-1], ev_units[-2],
                      stat_units[-1] if mc >= 1 else None])
            stat_units.append(em.t(
                (lambda mc: lambda t: [
                    t.matmul(ps_st[0][:, 0:512], c_onesc[:], xs_[:, mc, 0:512],
                             start=(mc == 0), stop=(mc == 15)),
                    t.matmul(ps_st[1][:, 0:512], c_onesc[:], xs_[:, mc, 512:1024],
                             start=(mc == 0), stop=(mc == 15)),
                    t.matmul(ps_st[2][:, 0:512], c_onesc[:], sq[:, 0:512],
                             start=(mc == 0), stop=(mc == 15)),
                    t.matmul(ps_st[3][:, 0:512], c_onesc[:], sq[:, 512:1024],
                             start=(mc == 0), stop=(mc == 15))])(mc),
                deps=[sqd, consts]))

        rrc = em.s(lambda sc: [
            sc.copy(out=rows[:, 0:512], in_=ps_st[0][:, 0:512]),
            sc.copy(out=rows[:, 512:1024], in_=ps_st[1][:, 0:512]),
            sc.copy(out=rows[:, 1024:1536], in_=ps_st[2][:, 0:512]),
            sc.copy(out=rows[:, 1536:2048], in_=ps_st[3][:, 0:512])],
            deps=[stat_units[15]])
        rr = em.v(lambda v: [
            v.tensor_scalar(out=rows[:, 2048:3072], in0=rows[:, 0:1024],
                            scalar1=1.0 / DM, scalar2=None, op0=ALU.mult),
            v.tensor_scalar(out=rows[:, 1024:2048], in0=rows[:, 1024:2048],
                            scalar1=1.0 / DM, scalar2=None, op0=ALU.mult),
            v.tensor_tensor(out=rows[:, 0:1024], in0=rows[:, 2048:3072],
                            in1=rows[:, 2048:3072], op=ALU.mult),
            v.tensor_tensor(out=rows[:, 3072:4096], in0=rows[:, 1024:2048],
                            in1=rows[:, 0:1024], op=ALU.subtract),
            v.tensor_scalar(out=rows[:, 3072:4096], in0=rows[:, 3072:4096],
                            scalar1=EPS5, scalar2=None, op0=ALU.add)],
            deps=[rrc])
        r2 = em.s(lambda sc: sc.activation(out=rows[:, 3072:4096],
                                           in_=rows[:, 3072:4096], func=AF.Sqrt),
                  deps=[rr])
        r3 = em.v(lambda v: v.reciprocal(out=rows[:, 3072:4096],
                                         in_=rows[:, 3072:4096]), deps=[r2])
        bcs = [psmm.next() for _ in range(4)]
        r4 = em.t(lambda t: [
            t.matmul(psmm.tile(bcs[0])[:, 0:512], c_onesr[:], rows[:, 2048:2560],
                     start=True, stop=True),
            t.matmul(psmm.tile(bcs[1])[:, 0:512], c_onesr[:], rows[:, 2560:3072],
                     start=True, stop=True),
            t.matmul(psmm.tile(bcs[2])[:, 0:512], c_onesr[:], rows[:, 3072:3584],
                     start=True, stop=True),
            t.matmul(psmm.tile(bcs[3])[:, 0:512], c_onesr[:], rows[:, 3584:4096],
                     start=True, stop=True)],
            deps=[r3] + [psmm.reader[s_] for s_ in bcs])
        r5 = em.s(lambda sc: [
            sc.copy(out=mean_b[:, 0:512], in_=psmm.tile(bcs[0])[:, 0:512]),
            sc.copy(out=mean_b[:, 512:1024], in_=psmm.tile(bcs[1])[:, 0:512]),
            sc.copy(out=rstd_b[:, 0:512], in_=psmm.tile(bcs[2])[:, 0:512]),
            sc.copy(out=rstd_b[:, 512:1024], in_=psmm.tile(bcs[3])[:, 0:512])],
            deps=[r4])
        for s_ in bcs:
            psmm.reader[s_] = r5

        outs = []
        for mc in range(16):
            ap1 = em.v(
                (lambda mc: lambda v: [
                    v.tensor_tensor(out=xs_[:, mc, :], in0=xs_[:, mc, :],
                                    in1=mean_b[:], op=ALU.subtract),
                    v.tensor_tensor(out=xs_[:, mc, :], in0=xs_[:, mc, :],
                                    in1=rstd_b[:], op=ALU.mult)])(mc),
                deps=[r5, stat_units[15]])
            cp = em.s(
                (lambda mc: lambda sc: sc.copy(out=xn1b[:], in_=xs_[:, mc, :]))(mc),
                deps=[ap1, outs[-1] if mc >= 1 else None])
            o1 = em.sync_dma(
                (lambda mc: lambda sy: [
                    sy.dma_start(out=x1_h.ap()[mc], in_=xs_[:, mc, :]),
                    sy.dma_start(out=xn1_h.ap()[mc], in_=xn1b[:])])(mc),
                deps=[cp], n=2)
            outs.append(o1)
        em.g(lambda g: g.engine_nop(), deps=outs)
        em.realize(block, sems, dma_sems)
    return nc


def host_inputs_C1(resA, resB, inputs):
    y_all = np.concatenate(
        [np.asarray(r["y16"]).view(BF).reshape(512, S) for r in resB], axis=0)
    x = np.asarray(inputs["L0_attn_x"])[0]
    xT = np.ascontiguousarray(x.T)
    wout = lhsT_tiles(np.asarray(inputs["L0_attn_W_out"]))
    maps = []
    for c in range(NC):
        cols = slice(c * TCK, (c + 1) * TCK)
        maps.append({
            "y16": np.ascontiguousarray(y_all[:, cols]).reshape(32, 128, TCK),
            "res16": np.asarray(resA[c]["res16"]).view(BF).reshape(32, 128, TCK),
            "xT": np.ascontiguousarray(xT[:, cols]).reshape(16, 128, TCK),
            "wout_t": wout,
            "onesc": np.ones((128, 1), np.float32),
            "onesr": np.ones((1, 128), np.float32),
        })
    return maps


def run_C1(resA, resB, inputs):
    nc = build_C1()
    maps = host_inputs_C1(resA, resB, inputs)
    res = run_bass_kernel_spmd(nc, maps, core_ids=list(range(NC)))
    return res.results



def build_C2():
    nc = bass.Bass("TRN2", target_bir_lowering=False, debug=False)
    xn1_h = nc.dram_tensor("xn1", [16, 128, TCK], BF16, kind="ExternalInput")
    x1_h = nc.dram_tensor("x1", [16, 128, TCK], F32, kind="ExternalInput")
    wg_h = nc.dram_tensor("wg", [DM, 8192], F32, kind="ExternalInput")
    wu_h = nc.dram_tensor("wu", [DM, 8192], F32, kind="ExternalInput")
    wdn_h = nc.dram_tensor("wdn_t", [16, 64, 128, 128], F32, kind="ExternalInput")
    id_h = nc.dram_tensor("ident16", [128, 128], BF16, kind="ExternalInput")
    onesc_h = nc.dram_tensor("onesc", [128, 1], F32, kind="ExternalInput")
    onesr_h = nc.dram_tensor("onesr", [1, 128], F32, kind="ExternalInput")
    x2_h = nc.dram_tensor("x2", [16, 128, TCK], F32, kind="ExternalOutput")

    TT = 512
    st = ExitStack()
    with st:
        def SB(name, shape, dt):
            return st.enter_context(nc.sbuf_tensor(name, shape, dt))

        def PSUM(name, shape, dt=F32):
            return st.enter_context(nc.psum_tensor(name, shape, dt))
        c_id = SB("c_id", [128, 128], BF16)
        c_onesc = SB("c_onesc", [128, 1], F32)
        c_onesr = SB("c_onesr", [1, 128], F32)
        xn1 = SB("xn1s", [128, 16, TCK], BF16)
        hT = SB("hT", [128, 64, TT], BF16)
        wgb = SB("wgb", [128, 16, 512], BF16)
        wub = SB("wub", [128, 16, 512], BF16)
        wdb = [SB("wdb0", [128, 64, 128], BF16)]
        gs = SB("gs", [128, 4, 512], BF16)
        habs = SB("habs", [128, 2048], BF16)
        hsp = habs
        q = SB("q", [128, 4, 512], BF16)
        msk = SB("msk", [128, 2048], U8)
        zero16 = SB("zero16", [128, 512], BF16)
        x1h = SB("x1h", [128, 16, TT], F32)
        sq = SB("sq", [128, TT], F32)
        rows = SB("rows", [1, 4 * TT], F32)
        mean_b = SB("mean_b", [128, TT], F32)
        rstd_b = SB("rstd_b", [128, TT], F32)
        psg = Slots([PSUM(f"psg{i}", [128, 512]) for i in range(4)])
        ps_tr = [PSUM("pstr0", [128, 1024], BF16), PSUM("pstr1", [128, 1024], BF16)]
        ps_st = [PSUM("ps_su", [1, 512]), PSUM("ps_sq", [1, 512])]
        sems = {e: st.enter_context(nc.semaphore("sem_" + e)) for e in Emitter.ENGS}
        dma_sems = [st.enter_context(nc.semaphore(f"dsem{i}")) for i in range(Emitter.NPOOL)]
        block = st.enter_context(nc.Block())
        em = Emitter(nc)

        consts = em.sync_dma(lambda sy: [
            sy.dma_start(out=c_id[:], in_=id_h.ap()),
            sy.dma_start(out=c_onesc[:], in_=onesc_h.ap()),
            sy.dma_start(out=c_onesr[:], in_=onesr_h.ap())], n=3)
        ldxn = em.sync_dma(lambda sy: sy.dma_start(
            out=xn1[:], in_=xn1_h.ap().rearrange("kc p t -> p kc t")))
        z0 = em.v(lambda v: v.memset(zero16[:], 0.0))

        x2outs = []
        prev_ldw = {}
        prev_half = {}
        for th in range(2):
            tsl = slice(th * TT, (th + 1) * TT)
            ldx1 = em.sync_dma(
                (lambda th: lambda sy: sy.dma_start(
                    out=x1h[:],
                    in_=x1_h.ap().rearrange("kc p t -> p kc t")[:, :, th * TT:(th + 1) * TT]))(th),
                deps=prev_half.get('x2dma_all', []))
            tr_units = []
            for ncc in range(16):
                ldwg = em.gp_dma(
                    (lambda ncc: lambda g: g.dma_start(
                        out=wgb[:],
                        in_=wg_h.ap().rearrange("(kc p) n -> p kc n", p=128)
                        [:, :, ncc * 512:(ncc + 1) * 512]))(ncc),
                    deps=[prev_ldw.get('g')])
                gmm = []
                for tc in range(4):
                    s_ = psg.next()
                    mm = em.t(
                        (lambda ncc, tc, th, s_: lambda t: [
                            t.matmul(psg.tile(s_)[:, 0:512],
                                     xn1[:, kc, th * TT + tc * 128: th * TT + (tc + 1) * 128],
                                     wgb[:, kc, :],
                                     start=(kc == 0), stop=(kc == 15))
                            for kc in range(16)])(ncc, tc, th, s_),
                        deps=[ldwg, ldxn, psg.reader[s_]])
                    gmm.append((mm, s_))
                gevac = em.s(
                    (lambda gmm: lambda sc: [
                        sc.activation(out=gs[:, tc, :], in_=psg.tile(s_)[:, 0:512],
                                      func=AF.Silu)
                        for tc, (mm, s_) in enumerate(gmm)])(gmm),
                    deps=[mm for mm, _ in gmm])
                for _, s_ in gmm:
                    psg.reader[s_] = gevac
                prev_ldw['g'] = gmm[-1][0]
                ldwu = em.gp_dma(
                    (lambda ncc: lambda g: g.dma_start(
                        out=wub[:],
                        in_=wu_h.ap().rearrange("(kc p) n -> p kc n", p=128)
                        [:, :, ncc * 512:(ncc + 1) * 512]))(ncc),
                    deps=[prev_ldw.get('u')])
                umm = []
                for tc in range(4):
                    s_ = psg.next()
                    mm = em.t(
                        (lambda ncc, tc, th, s_: lambda t: [
                            t.matmul(psg.tile(s_)[:, 0:512],
                                     xn1[:, kc, th * TT + tc * 128: th * TT + (tc + 1) * 128],
                                     wub[:, kc, :],
                                     start=(kc == 0), stop=(kc == 15))
                            for kc in range(16)])(ncc, tc, th, s_),
                        deps=[ldwu, ldxn, psg.reader[s_]])
                    umm.append((mm, s_))
                prev_ldw['u'] = umm[-1][0]
                hvs = []
                for tc in range(4):
                    mm, s_ = umm[tc]
                    hv = em.v(
                        (lambda tc, s_: lambda v: v.tensor_tensor(
                            out=gs[:, tc, :], in0=gs[:, tc, :],
                            in1=psg.tile(s_)[:, 0:512], op=ALU.mult))(tc, s_),
                        deps=[mm, gevac])
                    psg.reader[s_] = hv
                    hvs.append(hv)
                gsf = gs[:].rearrange("p tc t -> p (tc t)")
                ab = em.s(
                    lambda sc: sc.activation(out=habs[:], in_=gsf, func=AF.Abs),
                    deps=hvs + [tr_units[-1] if tr_units else None])
                h4 = habs[:].rearrange("p (g4 j) -> p g4 j", j=4)
                spar = em.v(
                    (lambda h4, gsf: lambda v: [
                        v.tensor_tensor(out=q[:, 0, :], in0=h4[:, :, 0], in1=h4[:, :, 1], op=ALU.max),
                        v.tensor_tensor(out=q[:, 1, :], in0=h4[:, :, 0], in1=h4[:, :, 1], op=ALU.min),
                        v.tensor_tensor(out=q[:, 2, :], in0=h4[:, :, 2], in1=h4[:, :, 3], op=ALU.max),
                        v.tensor_tensor(out=q[:, 3, :], in0=h4[:, :, 2], in1=h4[:, :, 3], op=ALU.min),
                        v.tensor_tensor(out=q[:, 0, :], in0=q[:, 0, :], in1=q[:, 2, :], op=ALU.min),
                        v.tensor_tensor(out=q[:, 1, :], in0=q[:, 1, :], in1=q[:, 3, :], op=ALU.max),
                        v.tensor_tensor(out=q[:, 2, :], in0=q[:, 0, :], in1=q[:, 1, :], op=ALU.max),
                        v.tensor_tensor(out=msk[:].rearrange("p (g4 j) -> p g4 j", j=4),
                                        in0=h4,
                                        in1=q[:, 2, :].broadcast_to([128, 512, 4]),
                                        op=ALU.is_ge),
                        v.select(out=hsp[:].rearrange("p (g4 j) -> p g4 j", j=4),
                                 mask=msk[:].rearrange("p (g4 j) -> p g4 j", j=4),
                                 on_true=gsf.rearrange("p (g4 j) -> p g4 j", j=4),
                                 on_false=zero16[:].broadcast_to([128, 512, 4]))])(h4, gsf),
                    deps=[ab, z0])
                for half16 in range(2):
                    trs = half16
                    trm = em.t(
                        (lambda half16, trs: lambda t: [
                            t.transpose(ps_tr[trs][:, gg * 128:(gg + 1) * 128],
                                        hsp[:, half16 * 1024 + gg * 128:
                                            half16 * 1024 + (gg + 1) * 128], c_id[:])
                            for gg in range(8)])(half16, trs),
                        deps=[spar, consts,
                              tr_units[-2] if len(tr_units) >= 2 else None])
                    trev = em.v(
                        (lambda ncc, half16, trs: lambda v: [
                            v.tensor_copy(
                                out=hT[:, ncc * 4 + ((half16 * 1024 + gg * 128) % 512) // 128,
                                       ((half16 * 1024 + gg * 128) // 512) * 128:
                                       ((half16 * 1024 + gg * 128) // 512) * 128 + 128],
                                in_=ps_tr[trs][:, gg * 128:(gg + 1) * 128])
                            for gg in range(8)])(ncc, half16, trs),
                        deps=[trm, prev_half.get('down_last')])
                    tr_units.append(trev)
            stat_units = []
            ev_units = []
            for mc in range(16):
                ldwd = em.gp_dma(
                    (lambda mc: lambda g: g.dma_start(
                        out=wdb[0][:],
                        in_=wdn_h.ap()[mc].rearrange("kc p j -> p kc j")))(mc),
                    deps=[prev_ldw.get('d')])
                for i in range(1):
                    s_ = psg.next()
                    mm = em.t(
                        (lambda mc, s_: lambda t: [
                            t.matmul(psg.tile(s_)[:, 0:512], wdb[0][:, kc, :],
                                     hT[:, kc, :],
                                     start=(kc == 0), stop=(kc == 63))
                            for kc in range(64)])(mc, s_),
                        deps=[ldwd, psg.reader[s_]] + tr_units[-2:])
                    prev_ldw['d'] = mm
                    ev = em.v(
                        (lambda mc, s_: lambda v: v.tensor_tensor(
                            out=x1h[:, mc, :], in0=x1h[:, mc, :],
                            in1=psg.tile(s_)[:, 0:512], op=ALU.add))(mc, s_),
                        deps=[mm, ldx1])
                    psg.reader[s_] = ev
                    ev_units.append(ev)
                sqd = em.s(
                    (lambda mc: lambda sc: sc.activation(
                        out=sq[:], in_=x1h[:, mc, :], func=AF.Square))(mc),
                    deps=[ev_units[-1], stat_units[-1] if stat_units else None])
                stat_units.append(em.t(
                    (lambda mc: lambda t: [
                        t.matmul(ps_st[0][:, 0:512], c_onesc[:], x1h[:, mc, :],
                                 start=(mc == 0), stop=(mc == 15)),
                        t.matmul(ps_st[1][:, 0:512], c_onesc[:], sq[:],
                                 start=(mc == 0), stop=(mc == 15))])(mc),
                    deps=[sqd, consts]))
            rrc = em.s(
                lambda sc: [
                    sc.copy(out=rows[:, 0:512], in_=ps_st[0][:, 0:512]),
                    sc.copy(out=rows[:, 512:1024], in_=ps_st[1][:, 0:512])],
                deps=[stat_units[15]])
            rr = em.v(
                lambda v: [
                    v.tensor_scalar(out=rows[:, 1024:1536], in0=rows[:, 0:512],
                                    scalar1=1.0 / DM, scalar2=None, op0=ALU.mult),
                    v.tensor_scalar(out=rows[:, 512:1024], in0=rows[:, 512:1024],
                                    scalar1=1.0 / DM, scalar2=None, op0=ALU.mult),
                    v.tensor_tensor(out=rows[:, 0:512], in0=rows[:, 1024:1536],
                                    in1=rows[:, 1024:1536], op=ALU.mult),
                    v.tensor_tensor(out=rows[:, 1536:2048], in0=rows[:, 512:1024],
                                    in1=rows[:, 0:512], op=ALU.subtract),
                    v.tensor_scalar(out=rows[:, 1536:2048], in0=rows[:, 1536:2048],
                                    scalar1=EPS5, scalar2=None, op0=ALU.add)],
                deps=[rrc])
            r2 = em.s(lambda sc: sc.activation(out=rows[:, 1536:2048],
                                               in_=rows[:, 1536:2048], func=AF.Sqrt),
                      deps=[rr])
            r3 = em.v(lambda v: v.reciprocal(out=rows[:, 1536:2048],
                                             in_=rows[:, 1536:2048]), deps=[r2])
            bcs = [psg.next(), psg.next()]
            r4 = em.t(lambda t: [
                t.matmul(psg.tile(bcs[0])[:, 0:512], c_onesr[:], rows[:, 1024:1536],
                         start=True, stop=True),
                t.matmul(psg.tile(bcs[1])[:, 0:512], c_onesr[:], rows[:, 1536:2048],
                         start=True, stop=True)],
                deps=[r3] + [psg.reader[s_] for s_ in bcs])
            r5 = em.s(lambda sc: [
                sc.copy(out=mean_b[:], in_=psg.tile(bcs[0])[:, 0:512]),
                sc.copy(out=rstd_b[:], in_=psg.tile(bcs[1])[:, 0:512])],
                deps=[r4])
            for s_ in bcs:
                psg.reader[s_] = r5
            outs_th = []
            for mc in range(16):
                ap1 = em.v(
                    (lambda mc: lambda v: [
                        v.tensor_tensor(out=x1h[:, mc, :], in0=x1h[:, mc, :],
                                        in1=mean_b[:], op=ALU.subtract),
                        v.tensor_tensor(out=x1h[:, mc, :], in0=x1h[:, mc, :],
                                        in1=rstd_b[:], op=ALU.mult)])(mc),
                    deps=[r5])
                o1 = em.sync_dma(
                    (lambda mc, th: lambda sy: sy.dma_start(
                        out=x2_h.ap()[mc][:, th * TT:(th + 1) * TT],
                        in_=x1h[:, mc, :]))(mc, th),
                    deps=[ap1])
                outs_th.append(o1)
            x2outs += outs_th
            prev_half['x2dma_all'] = list(outs_th)
            prev_half['down_last'] = prev_ldw.get('d')
        em.g(lambda g: g.engine_nop(), deps=x2outs)
        em.realize(block, sems, dma_sems)
    return nc


def host_inputs_C2(resC1, inputs):
    wdn = lhsT_tiles(np.asarray(inputs["L0_ffn_w_down"]))
    ident = np.eye(128, dtype=np.float32).astype(BF)
    maps = []
    for c in range(NC):
        maps.append({
            "xn1": np.asarray(resC1[c]["xn1"]).view(BF).reshape(16, 128, TCK),
            "x1": np.asarray(resC1[c]["x1"]).reshape(16, 128, TCK),
            "wg": np.asarray(inputs["L0_ffn_w_gate"]),
            "wu": np.asarray(inputs["L0_ffn_w_up"]),
            "wdn_t": wdn,
            "ident16": ident,
            "onesc": np.ones((128, 1), np.float32),
            "onesr": np.ones((1, 128), np.float32),
        })
    return maps


def run_C2(resC1, inputs):
    nc = build_C2()
    maps = host_inputs_C2(resC1, inputs)
    res = run_bass_kernel_spmd(nc, maps, core_ids=list(range(NC)))
    return res.results



def build_D():
    nc = bass.Bass("TRN2", target_bir_lowering=False, debug=False)
    x2_h = nc.dram_tensor("x2T", [16, 128, TH], F32, kind="ExternalInput")
    wl_h = nc.dram_tensor("wl_t", [81, 16, 128, 128], F32, kind="ExternalInput")
    cwt_h = nc.dram_tensor("cwt2", [128, 192], F32, kind="ExternalInput")
    cb_h = nc.dram_tensor("cb2", [128, 48], F32, kind="ExternalInput")
    dtb_h = nc.dram_tensor("dtb", [64, 1], F32, kind="ExternalInput")
    alog_h = nc.dram_tensor("alog2", [64, 1], F32, kind="ExternalInput")

    gate_h = nc.dram_tensor("gate16", [32, 128, TCK], BF16, kind="ExternalOutput")
    xbc_h = nc.dram_tensor("xbc16", [48, 128, TCK], BF16, kind="ExternalOutput")
    dt2_h = nc.dram_tensor("dt2", [64, TCK], F32, kind="ExternalOutput")
    a2_h = nc.dram_tensor("a2", [64, TCK], F32, kind="ExternalOutput")

    st = ExitStack()
    with st:
        def SB(name, shape, dt):
            return st.enter_context(nc.sbuf_tensor(name, shape, dt))

        def PSUM(name, shape, dt=F32):
            return st.enter_context(nc.psum_tensor(name, shape, dt))
        c_cwt = SB("c_cwt", [128, 192], F32)
        c_cb = SB("c_cb", [128, 48], F32)
        c_dtb = SB("c_dtb", [64, 1], F32)
        c_al = SB("c_al", [64, 1], F32)
        x2n = SB("x2n", [128, 16, TH], BF16)
        wbuf = [SB("wbuf0", [128, 2048], BF16), SB("wbuf1", [128, 2048], BF16)]
        xraw = SB("xraw", [128, TH], F32)
        cacc = SB("cacc", [128, TCK], F32)
        ob = [SB("ob0", [128, TCK], BF16), SB("ob1", [128, TCK], BF16)]
        sp1 = SB("sp1", [128, TCK], F32)
        sp2 = SB("sp2", [128, TCK], F32)
        dtb_ = SB("dtbuf", [128, TCK], F32)
        a2b = SB("a2buf", [128, TCK], F32)
        psmm = Slots([PSUM(f"ps{i}", [128, 512]) for i in range(4)])
        sems = {e: st.enter_context(nc.semaphore("sem_" + e)) for e in Emitter.ENGS}
        dma_sems = [st.enter_context(nc.semaphore(f"dsem{i}")) for i in range(Emitter.NPOOL)]
        block = st.enter_context(nc.Block())
        em = Emitter(nc)

        consts = em.sync_dma(lambda sy: [
            sy.dma_start(out=c_cwt[:], in_=cwt_h.ap()),
            sy.dma_start(out=c_cb[:], in_=cb_h.ap()),
            sy.dma_start(out=c_dtb[:], in_=dtb_h.ap()),
            sy.dma_start(out=c_al[:], in_=alog_h.ap())], n=4)
        ldx = em.gp_dma(lambda g: g.dma_start(
            out=x2n[:], in_=x2_h.ap().rearrange("kc p t -> p kc t")))
        a1u = em.s(lambda sc: sc.activation(out=c_al[:], in_=c_al[:], func=AF.Exp),
                   deps=[consts])
        a2u = em.v(lambda v: v.tensor_scalar(out=c_al[:], in0=c_al[:], scalar1=-1.0,
                                             scalar2=None, op0=ALU.mult), deps=[a1u])

        ldw = [None] * 81
        out_units = []
        prev_gate_dma = {}
        prev_conv = [None] * 81
        last_wbuf_mm = [None, None]
        for mc in range(81):
            ldw[mc] = em.gp_dma(
                (lambda mc: lambda g: g.dma_start(
                    out=wbuf[mc % 2][:].rearrange("p (kc j) -> p kc j", j=128),
                    in_=wl_h.ap()[mc].rearrange("kc p j -> p kc j")))(mc),
                deps=[last_wbuf_mm[mc % 2]])
            evacs = []
            mm = None
            for i, (o, w) in enumerate(NCP_TH):
                s_ = psmm.next()
                mm = em.t(
                    (lambda mc, o, w, s_: lambda t: [
                        t.matmul(psmm.tile(s_)[:, 0:w],
                                 wbuf[mc % 2][:, kc * 128:(kc + 1) * 128],
                                 x2n[:, kc, o:o + w],
                                 start=(kc == 0), stop=(kc == 15))
                        for kc in range(16)])(mc, o, w, s_),
                    deps=[ldw[mc], ldx, psmm.reader[s_]])
                if mc < 32:
                    def gfn(o, w, s_, mc):
                        def fn(sc):
                            if o == 0:
                                return sc.activation(out=ob[mc % 2][:, 0:w - HALO],
                                                     in_=psmm.tile(s_)[:, HALO:w],
                                                     func=AF.Silu)
                            return sc.activation(out=ob[mc % 2][:, o - HALO:o - HALO + w],
                                                 in_=psmm.tile(s_)[:, 0:w], func=AF.Silu)
                        return fn
                    ev = em.s(gfn(o, w, s_, mc),
                              deps=[mm, prev_gate_dma.get(mc % 2)])
                elif mc < 80:
                    ev = em.s(
                        (lambda o, w, s_: lambda sc: sc.copy(
                            out=xraw[:, o:o + w], in_=psmm.tile(s_)[:, 0:w]))(o, w, s_),
                        deps=[mm, prev_conv[mc - 1] if i == 0 else None])
                else:
                    def dfn(o, w, s_):
                        def fn(sc):
                            if o == 0:
                                src = psmm.tile(s_)[0:64, HALO:w]
                                dsl = slice(0, w - HALO)
                            else:
                                src = psmm.tile(s_)[0:64, 0:w]
                                dsl = slice(o - HALO, o - HALO + w)
                            sc.activation(out=sp1[0:64, dsl], in_=src,
                                          func=AF.Abs, bias=c_dtb[:, 0:1])
                            sc.activation(out=sp1[0:64, dsl], in_=sp1[0:64, dsl],
                                          func=AF.Exp, scale=-1.0)
                            sc.activation(out=sp1[0:64, dsl], in_=sp1[0:64, dsl],
                                          func=AF.Ln, bias=1.0)
                            return sc.activation(out=sp2[0:64, dsl], in_=src,
                                                 func=AF.Relu, bias=c_dtb[:, 0:1])
                        return fn
                    ev = em.s(dfn(o, w, s_), deps=[mm])
                psmm.reader[s_] = ev
                evacs.append(ev)
            last_wbuf_mm[mc % 2] = mm
            if mc < 32:
                d1 = em.sync_dma(
                    (lambda mc: lambda sy: sy.dma_start(
                        out=gate_h.ap()[mc], in_=ob[mc % 2][:]))(mc),
                    deps=evacs)
                prev_gate_dma[mc % 2] = d1
                out_units.append(d1)
                prev_conv[mc] = evacs[-1]
            elif mc < 80:
                dc2 = mc - 32
                cd = em.v(
                    (lambda dc2: lambda v: [
                        v.tensor_scalar(out=cacc[:], in0=xraw[:, 1:1 + TCK],
                                        scalar1=c_cwt[:, dc2 * 4:dc2 * 4 + 1],
                                        scalar2=c_cb[:, dc2:dc2 + 1],
                                        op0=ALU.mult, op1=ALU.add),
                        *[v.scalar_tensor_tensor(
                            out=cacc[:], in0=xraw[:, 1 + wt:1 + wt + TCK],
                            scalar=c_cwt[:, dc2 * 4 + wt:dc2 * 4 + wt + 1],
                            in1=cacc[:], op0=ALU.mult, op1=ALU.add)
                          for wt in range(1, 4)]])(dc2),
                    deps=evacs + [prev_conv[mc - 1], consts])
                sil = em.s(
                    (lambda mc: lambda sc: sc.activation(
                        out=ob[mc % 2][:], in_=cacc[:], func=AF.Silu))(mc),
                    deps=[cd, prev_gate_dma.get(mc % 2)])
                prev_conv[mc] = sil
                d1 = em.sync_dma(
                    (lambda mc: lambda sy: sy.dma_start(
                        out=xbc_h.ap()[mc - 32], in_=ob[mc % 2][:]))(mc),
                    deps=[sil])
                prev_gate_dma[mc % 2] = d1
                out_units.append(d1)
            else:
                add = em.v(lambda v: v.tensor_tensor(
                    out=dtb_[0:64, :], in0=sp1[0:64, :], in1=sp2[0:64, :],
                    op=ALU.add), deps=evacs)
                a2c = em.v(lambda v: v.tensor_scalar(
                    out=a2b[0:64, :], in0=dtb_[0:64, :], scalar1=c_al[:, 0:1],
                    scalar2=None, op0=ALU.mult), deps=[add, a2u])
                d1 = em.sync_dma(lambda sy: [
                    sy.dma_start(out=dt2_h.ap(), in_=dtb_[0:64, :]),
                    sy.dma_start(out=a2_h.ap(), in_=a2b[0:64, :])],
                    deps=[add, a2c], n=2)
                out_units.append(d1)
        em.g(lambda g: g.engine_nop(), deps=out_units)
        em.realize(block, sems, dma_sems)
    return nc


def host_inputs_D(resC2, inputs):
    x2_all = np.concatenate(
        [np.asarray(r["x2"]).reshape(DM, TCK) for r in resC2], axis=1)
    wdt_pad = np.zeros((DM, 128), np.float32)
    wdt_pad[:, :64] = np.asarray(inputs["L1_attn_Wdt"])
    wl = np.concatenate([np.asarray(inputs["L1_attn_Wz"]),
                         np.asarray(inputs["L1_attn_Wx"]),
                         np.asarray(inputs["L1_attn_WB"]),
                         np.asarray(inputs["L1_attn_WC"]),
                         wdt_pad], axis=1)
    wl_t = lhsT_tiles(wl)
    conv_w = np.asarray(inputs["L1_attn_conv_w"])
    cwt2 = np.ascontiguousarray(
        conv_w.reshape(48, 128, 4).transpose(1, 0, 2).reshape(128, 192))
    cb2 = percol(np.asarray(inputs["L1_attn_conv_b"]))
    x2slices = tok_slices(x2_all, HALO)
    maps = []
    for c in range(NC):
        maps.append({
            "x2T": x2slices[c].reshape(16, 128, TH),
            "wl_t": wl_t,
            "cwt2": cwt2,
            "cb2": cb2,
            "dtb": np.asarray(inputs["L1_attn_dt_bias"]).reshape(64, 1),
            "alog2": np.asarray(inputs["L1_attn_A_log"]).reshape(64, 1),
        })
    return maps


def run_D(resC2, inputs):
    nc = build_D()
    maps = host_inputs_D(resC2, inputs)
    res = run_bass_kernel_spmd(nc, maps, core_ids=list(range(NC)))
    return res.results



def build_E():
    nc = bass.Bass("TRN2", target_bir_lowering=False, debug=False)
    xg_h = nc.dram_tensor("xg16", [8, 64, S], BF16, kind="ExternalInput")
    bg_h = nc.dram_tensor("bg16", [128, S], BF16, kind="ExternalInput")
    cg_h = nc.dram_tensor("cg16", [128, S], BF16, kind="ExternalInput")
    ag_h = nc.dram_tensor("ag8", [8, S], F32, kind="ExternalInput")
    dt_h = nc.dram_tensor("dt8", [8, S], F32, kind="ExternalInput")
    sel_h = nc.dram_tensor("sel8", [8, 8 * 128], F32, kind="ExternalInput")
    i8_h = nc.dram_tensor("i8", [8, 8], F32, kind="ExternalInput")
    ones8_h = nc.dram_tensor("ones8", [8, 128], F32, kind="ExternalInput")
    tri_h = nc.dram_tensor("tri", [128, 128], F32, kind="ExternalInput")
    dgb_h = nc.dram_tensor("dgb", [64, 8], F32, kind="ExternalInput")
    id_h = nc.dram_tensor("ident16", [128, 128], BF16, kind="ExternalInput")
    yg_h = nc.dram_tensor("yg16", [8, 64, S], BF16, kind="ExternalOutput")

    Q = 128
    NQTR = 4
    TTE = S // NQTR
    NCH = TTE // Q
    st = ExitStack()
    with st:
        def SB(name, shape, dt):
            return st.enter_context(nc.sbuf_tensor(name, shape, dt))

        def PSUM(name, shape, dt=F32):
            return st.enter_context(nc.psum_tensor(name, shape, dt))
        xg = SB("xg", [64, 8, TTE], BF16)
        bg = SB("bg", [128, TTE], BF16)
        cg = SB("cg", [128, TTE], BF16)
        yg = SB("yg", [64, 8, TTE], BF16)
        ag8 = SB("ag8s", [8, TTE], F32)
        dt8 = SB("dt8s", [8, TTE], F32)
        zr8 = SB("zr8", [8, TTE], BF16)
        cumx = SB("cumx", [8, TTE + 1], F32)
        carry = SB("carry", [8, 1], F32)
        c_sel = SB("c_sel", [8, 8 * 128], F32)
        c_i8 = SB("c_i8", [8, 8], F32)
        c_on8 = SB("c_on8", [8, 128], F32)
        c_tri = SB("c_tri", [128, 128], F32)
        c_dgb = SB("c_dgb", [64, 8], F32)
        c_id = SB("c_id", [128, 128], BF16)
        diag8 = SB("diag8", [8, 16], F32)
        cds = SB("cds", [128, 24], F32)
        wcols = SB("wcols", [128, 8], F32)
        wca = SB("wca", [128, 8], F32)
        wcb = SB("wcb", [128, 8], F32)
        pcol = SB("pcol", [128, 1], F32)
        zb = SB("zb", [128, 8, Q], F32)
        z0b = SB("z0b", [128, 8, Q], F32)
        st16 = SB("st16", [128, 8 * Q], BF16)
        cl16 = SB("cl16", [128, 8 * Q], BF16)
        bt16 = SB("bt16", [128, 128], BF16)
        xt16 = SB("xt16", [128, 8, 64], BF16)
        bw16 = SB("bw16", [128, 8, 128], BF16)
        ht = SB("ht", [128, 512], F32)
        ht16 = SB("ht16", [128, 512], BF16)
        ps_tr = [PSUM("pstr0", [128, 512]), PSUM("pstr1", [128, 512])]
        ps_cb = [PSUM("pscb0", [128, 512]), PSUM("pscb1", [128, 512])]
        ps_yc = [PSUM("psyc0", [128, 512]), PSUM("psyc1", [128, 512])]
        ps_tp = [PSUM("pstp0", [128, 1024], BF16), PSUM("pstp1", [128, 1024], BF16)]
        sems = {e: st.enter_context(nc.semaphore("sem_" + e)) for e in Emitter.ENGS}
        dma_sems = [st.enter_context(nc.semaphore(f"dsem{i}")) for i in range(Emitter.NPOOL)]
        block = st.enter_context(nc.Block())
        em = Emitter(nc)

        consts = em.sync_dma(lambda sy: [
            sy.dma_start(out=c_sel[:], in_=sel_h.ap()),
            sy.dma_start(out=c_i8[:], in_=i8_h.ap()),
            sy.dma_start(out=c_on8[:], in_=ones8_h.ap()),
            sy.dma_start(out=c_tri[:], in_=tri_h.ap()),
            sy.dma_start(out=c_dgb[:], in_=dgb_h.ap()),
            sy.dma_start(out=c_id[:], in_=id_h.ap()),
        ], n=6)
        init = em.v(lambda v: [
            v.memset(zr8[:], 0.0),
            v.memset(carry[:], 0.0),
            v.memset(ht[:], 0.0)], deps=[])

        prev = {}
        gc = 0
        for h2 in range(NQTR):
            hsl = slice(h2 * TTE, (h2 + 1) * TTE)
            lds = em.sync_dma(
                (lambda h2: lambda sy: [
                    sy.dma_start(out=xg[:],
                                 in_=xg_h.ap().rearrange("hh p t -> p hh t")
                                 [:, :, h2 * TTE:(h2 + 1) * TTE]),
                    sy.dma_start(out=bg[:], in_=bg_h.ap()[:, h2 * TTE:(h2 + 1) * TTE]),
                    sy.dma_start(out=cg[:], in_=cg_h.ap()[:, h2 * TTE:(h2 + 1) * TTE]),
                    sy.dma_start(out=ag8[:], in_=ag_h.ap()[:, h2 * TTE:(h2 + 1) * TTE]),
                    sy.dma_start(out=dt8[:], in_=dt_h.ap()[:, h2 * TTE:(h2 + 1) * TTE]),
                ])(h2),
                deps=[prev.get('ydma'), prev.get('lastcompute')], n=5)
            cum = em.v(
                lambda v: [
                    v.tensor_copy(out=cumx[:, 0:1], in_=carry[:]),
                    v.tensor_tensor_scan(
                        out=cumx[:, 1:TTE + 1], data0=ag8[:], data1=zr8[:],
                        initial=carry[:, 0:1], op0=ALU.add, op1=ALU.add)],
                deps=[lds, init, prev.get('carry_upd')])
            ccp = em.v(lambda v: v.tensor_copy(out=carry[:], in_=cumx[:, TTE:TTE + 1]),
                       deps=[cum])
            prev['carry_upd'] = ccp

            for c in range(NCH):
                cs = c * Q
                tp = ps_tp[gc % 2]
                pcb = ps_cb[gc % 2]
                trm = em.t(
                    (lambda cs, tp: lambda t: [
                        t.transpose(tp[:, 0:128], bg[:, cs:cs + Q], c_id[:])] + [
                        t.transpose(tp[:, 128 + h * 64: 128 + (h + 1) * 64],
                                    xg[0:64, h, cs:cs + Q], c_id[0:64, 0:64])
                        for h in range(8)])(cs, tp),
                    deps=[lds, consts, prev.get(('tpr', gc % 2))])
                trev = em.v(
                    (lambda tp: lambda v: [
                        v.tensor_copy(out=bt16[:], in_=tp[:, 0:128])] + [
                        v.tensor_copy(out=xt16[:, h, :],
                                      in_=tp[:, 128 + h * 64:128 + (h + 1) * 64])
                        for h in range(8)])(tp),
                    deps=[trm, prev.get('dk3'), prev.get('ymm_last')])
                prev[('tpr', gc % 2)] = trev
                dg = em.v(
                    (lambda cs, gc: lambda v: v.tensor_scalar(
                        out=diag8[:, (gc % 2) * 8:(gc % 2) * 8 + 8], in0=c_i8[:],
                        scalar1=cumx[:, cs + Q:cs + Q + 1], scalar2=None,
                        op0=ALU.mult))(cs, gc),
                    deps=[cum, consts, prev.get(('dgr', gc % 2))])
                cbm = em.t(
                    (lambda cs, pcb, gc: lambda t: [
                        t.matmul(pcb[:, 0:128], bg[:, cs:cs + Q], cg[:, cs:cs + Q],
                                 start=True, stop=True),
                        t.matmul(pcb[:, 128:136], cumx[:, cs + 1:cs + 1 + Q], c_i8[:],
                                 start=True, stop=True),
                        t.matmul(pcb[:, 136:144], dt8[:, cs:cs + Q], c_i8[:],
                                 start=True, stop=True),
                        t.matmul(pcb[:, 144:152], c_on8[:],
                                 diag8[:, (gc % 2) * 8:(gc % 2) * 8 + 8],
                                 start=True, stop=True)])(cs, pcb, gc),
                    deps=[cum, dg, prev.get(('cbr', gc % 2))])
                prev[('dgr', gc % 2)] = cbm
                wc0 = em.s(
                    (lambda pcb: lambda sc: sc.copy(out=cds[:], in_=pcb[:, 128:152]))(pcb),
                    deps=[cbm, prev.get('wcr')])
                wc = em.v(
                    lambda v: v.tensor_tensor(out=wca[:], in0=cds[:, 16:24],
                                              in1=cds[:, 0:8], op=ALU.subtract),
                    deps=[wc0])
                wc2 = em.s(lambda sc: sc.activation(out=wcb[:], in_=wca[:],
                                                    func=AF.Exp), deps=[wc])
                wc3 = em.v(lambda v: v.tensor_tensor(out=wcols[:], in0=wcb[:],
                                                     in1=cds[:, 8:16], op=ALU.mult),
                           deps=[wc2, prev.get('wcolr')])
                zunits = []
                for h in range(8):
                    ptr = ps_tr[h % 2]
                    tr = em.t(
                        (lambda h, cs, ptr: lambda t: t.matmul(
                            ptr[:, 0:129], c_sel[:, h * 128:(h + 1) * 128],
                            cumx[:, cs:cs + 129], start=True, stop=True))(h, cs, ptr),
                        deps=[cum, prev.get(('trr', h % 2)), consts])
                    zu = em.v(
                        (lambda h, ptr: lambda v: [
                            v.tensor_copy(out=pcol[:], in_=ptr[:, 0:1]),
                            v.tensor_scalar(out=zb[:, h, :], in0=ptr[:, 1:129],
                                            scalar1=cds[:, h:h + 1], scalar2=None,
                                            op0=ALU.subtract),
                            v.tensor_scalar(out=z0b[:, h, :], in0=ptr[:, 1:129],
                                            scalar1=pcol[:, 0:1], scalar2=None,
                                            op0=ALU.subtract)])(h, ptr),
                        deps=[tr, wc, prev.get('dk2')])
                    prev[('trr', h % 2)] = zu
                    zunits.append(zu)
                dk1 = em.v(lambda v: v.tensor_scalar(
                    out=zb[:].rearrange("p h q -> p (h q)"),
                    in0=zb[:].rearrange("p h q -> p (h q)"),
                    scalar1=0.0, scalar2=None, op0=ALU.min), deps=zunits)
                dk2 = em.s(lambda sc: [
                    sc.activation(out=zb[:].rearrange("p h q -> p (h q)"),
                                  in_=zb[:].rearrange("p h q -> p (h q)"),
                                  func=AF.Exp),
                    sc.activation(out=z0b[:].rearrange("p h q -> p (h q)"),
                                  in_=z0b[:].rearrange("p h q -> p (h q)"),
                                  func=AF.Exp)],
                    deps=[dk1])
                prev['dk2'] = dk2
                dk3 = em.v(
                    (lambda pcb, cs: lambda v: [
                        v.tensor_tensor(out=zb[:], in0=zb[:],
                                        in1=c_tri[:].rearrange("p q -> p () q")
                                        .broadcast_to([128, 8, Q]),
                                        op=ALU.mult),
                        v.tensor_tensor(out=zb[:], in0=zb[:],
                                        in1=pcb[:, 0:128].rearrange("p q -> p () q")
                                        .broadcast_to([128, 8, Q]),
                                        op=ALU.mult),
                        v.tensor_tensor(out=st16[:].rearrange("p (h q) -> p h q", q=Q),
                                        in0=zb[:],
                                        in1=cds[:, 8:16].rearrange("p h -> p h ()")
                                        .broadcast_to([128, 8, Q]),
                                        op=ALU.mult),
                        v.tensor_tensor(out=cl16[:].rearrange("p (h q) -> p h q", q=Q),
                                        in0=z0b[:],
                                        in1=cg[:, cs:cs + Q].rearrange("p q -> p () q")
                                        .broadcast_to([128, 8, Q]),
                                        op=ALU.mult),
                        v.tensor_tensor(out=bw16[:],
                                        in0=bt16[:].rearrange("p n -> p () n")
                                        .broadcast_to([128, 8, 128]),
                                        in1=wcols[:].rearrange("p h -> p h ()")
                                        .broadcast_to([128, 8, 128]),
                                        op=ALU.mult)])(pcb, cs),
                    deps=[dk2, wc3, trev, prev.get('ymm_last2')])
                prev[('cbr', gc % 2)] = dk3
                prev['dk3'] = dk3
                prev['wcolr'] = dk3
                prev['wcr'] = dk3
                htc = em.v(lambda v: v.tensor_copy(out=ht16[:], in_=ht[:]),
                           deps=[prev.get('htu_last', init), prev.get('yil')])
                ytu = None
                htu = None
                ymm = None
                for h in range(8):
                    pyc = ps_yc[h % 2]
                    ymm = em.t(
                        (lambda h, gc, cs, pyc: lambda t: [
                            t.matmul(pyc[0:64, 0:128], xt16[:, h, :],
                                     st16[:, h * Q:(h + 1) * Q],
                                     start=True, stop=(gc == 0))] + ([
                            t.matmul(pyc[0:64, 0:128],
                                     ht16[:, h * 64:(h + 1) * 64],
                                     cl16[:, h * Q:(h + 1) * Q],
                                     start=False, stop=True)] if gc > 0 else []) + [
                            t.matmul(pyc[:, 128:192], bw16[:, h, :], xt16[:, h, :],
                                     start=True, stop=True)])(h, gc, cs, pyc),
                        deps=[dk3, htc, trev, prev.get(('ycr', h % 2))])
                    ytu = em.v(
                        (lambda h, cs, pyc: lambda v: v.scalar_tensor_tensor(
                            out=yg[0:64, h, cs:cs + Q],
                            in0=xg[0:64, h, cs:cs + Q],
                            scalar=c_dgb[0:64, h:h + 1],
                            in1=pyc[0:64, 0:128],
                            op0=ALU.mult, op1=ALU.add))(h, cs, pyc),
                        deps=[ymm, consts])
                    htu = em.v(
                        (lambda h, pyc: lambda v: v.scalar_tensor_tensor(
                            out=ht[:, h * 64:(h + 1) * 64],
                            in0=ht[:, h * 64:(h + 1) * 64],
                            scalar=z0b[:, h, Q - 1:Q],
                            in1=pyc[:, 128:192],
                            op0=ALU.mult, op1=ALU.add))(h, pyc),
                        deps=[ymm, htc, init])
                    prev[('ycr', h % 2)] = htu
                prev['htu_last'] = htu
                prev['yil'] = htu
                prev['ymm_last'] = ymm
                prev['ymm_last2'] = ymm
                prev['str'] = ytu
                prev['lastcompute'] = htu
                gc += 1
            ydma = em.sync_dma(
                (lambda h2: lambda sy: sy.dma_start(
                    out=yg_h.ap().rearrange("hh p t -> p hh t")
                    [:, :, h2 * TTE:(h2 + 1) * TTE], in_=yg[:]))(h2),
                deps=[prev['str'], prev['htu_last']], n=1)
            prev['ydma'] = ydma
        em.g(lambda g: g.engine_nop(), deps=[prev['ydma']])
        em.realize(block, sems, dma_sems)
    return nc


def host_inputs_E(resD, inputs):
    xbc_all = np.concatenate(
        [np.asarray(r["xbc16"]).view(BF).reshape(CONVD, TCK) for r in resD], axis=1)
    dt_all = np.concatenate([np.asarray(r["dt2"]) for r in resD], axis=1)
    a2_all = np.concatenate([np.asarray(r["a2"]) for r in resD], axis=1)
    sel = np.zeros((8, 8, 128), np.float32)
    for h in range(8):
        sel[h, h, :] = 1.0
    dvec = np.asarray(inputs["L1_attn_D"])
    maps = []
    for g in range(NC):
        maps.append({
            "xg16": np.ascontiguousarray(
                xbc_all[g * 512:(g + 1) * 512]).reshape(8, 64, S),
            "bg16": np.ascontiguousarray(xbc_all[4096 + g * 128: 4096 + (g + 1) * 128]),
            "cg16": np.ascontiguousarray(xbc_all[5120 + g * 128: 5120 + (g + 1) * 128]),
            "ag8": np.ascontiguousarray(a2_all[g * 8:(g + 1) * 8]),
            "dt8": np.ascontiguousarray(dt_all[g * 8:(g + 1) * 8]),
            "sel8": np.ascontiguousarray(sel.transpose(1, 0, 2).reshape(8, 8 * 128)),
            "i8": np.eye(8, dtype=np.float32),
            "ones8": np.ones((8, 128), np.float32),
            "tri": np.triu(np.ones((128, 128), np.float32)),
            "dgb": np.tile(dvec[g * 8:(g + 1) * 8], (64, 1)).astype(np.float32),
            "ident16": np.eye(128, dtype=np.float32).astype(BF),
        })
    return maps


def run_E(resD, inputs):
    nc = build_E()
    maps = host_inputs_E(resD, inputs)
    res = run_bass_kernel_spmd(nc, maps, core_ids=list(range(NC)))
    return res.results



def build_G():
    nc = bass.Bass("TRN2", target_bir_lowering=False, debug=False)
    gpk_h = nc.dram_tensor("gup_pk", [11, 128, 256], I32, kind="ExternalInput")
    gsc_h = nc.dram_tensor("gup_sc", [11, 128, 16], F16, kind="ExternalInput")
    dpk_h = nc.dram_tensor("dn_pk", [2, 128, 704], I32, kind="ExternalInput")
    dsc_h = nc.dram_tensor("dn_sc", [2, 128, 44], F16, kind="ExternalInput")
    id_h = nc.dram_tensor("ident16", [128, 128], BF16, kind="ExternalInput")
    wgu_h = nc.dram_tensor("wguT", [16, 128, 1408], BF16, kind="ExternalOutput")
    wdn_h = nc.dram_tensor("wdnT", [44, 128, 256], BF16, kind="ExternalOutput")

    st = ExitStack()
    with st:
        def SB(name, shape, dt):
            return st.enter_context(nc.sbuf_tensor(name, shape, dt))
        c_id = SB("c_id", [128, 128], BF16)
        pk = SB("pk", [128, 704], I32)
        sc16 = SB("sc16", [128, 44], F16)
        sc32 = SB("sc32", [128, 44], F32)
        t1 = SB("t1", [128, 704], I32)
        fव = SB("fv", [128, 704], F32)
        tv = SB("tv", [128, 704], F32)
        mv = SB("mv", [128, 704], F32)
        sg = SB("sg", [128, 704], F32)
        rv = SB("rv", [128, 704], F32)
        bv = SB("bv", [128, 704], F32)
        av = SB("av", [128, 704], F32)
        wrow = SB("wrow", [128, 5632], BF16)
        trev_t = [SB("trev_t0", [128, 128], BF16), SB("trev_t1", [128, 128], BF16),
                  SB("trev_t2", [128, 128], BF16), SB("trev_t3", [128, 128], BF16)]
        ps_tp = [st.enter_context(nc.psum_tensor("pstp0", [128, 1024], BF16)),
                 st.enter_context(nc.psum_tensor("pstp1", [128, 1024], BF16))]
        sems = {e: st.enter_context(nc.semaphore("sem_" + e)) for e in Emitter.ENGS}
        dma_sems = [st.enter_context(nc.semaphore(f"dsem{i}")) for i in range(Emitter.NPOOL)]
        block = st.enter_context(nc.Block())
        em = Emitter(nc)
        fv = fव

        consts = em.sync_dma(lambda sy: sy.dma_start(out=c_id[:], in_=id_h.ap()))
        outs = []
        prev = {}
        for part, (nrc, nwords, ngrp, pk_ap, sc_ap, out_ap) in enumerate([
                (11, 256, 16, gpk_h, gsc_h, wgu_h), (2, 704, 44, dpk_h, dsc_h, wdn_h)]):
            nk = nwords * 8 // 128
            for rc in range(nrc):
                ld = em.sync_dma(
                    (lambda rc, pk_ap, sc_ap, nwords, ngrp: lambda sy: [
                        sy.dma_start(out=pk[:, 0:nwords], in_=pk_ap.ap()[rc]),
                        sy.dma_start(out=sc16[:, 0:ngrp], in_=sc_ap.ap()[rc])])(
                            rc, pk_ap, sc_ap, nwords, ngrp),
                    deps=[prev.get('lastk')], n=2)
                scc = em.s(
                    (lambda ngrp: lambda sc: sc.copy(out=sc32[:, 0:ngrp],
                                                     in_=sc16[:, 0:ngrp]))(ngrp),
                    deps=[ld])
                kunits = []
                for k in range(8):
                    u = em.v(
                        (lambda k, nwords, ngrp: lambda v: [
                            v.tensor_scalar(out=t1[:, 0:nwords], in0=pk[:, 0:nwords],
                                            scalar1=4 * k, scalar2=15,
                                            op0=ALU.logical_shift_right,
                                            op1=ALU.bitwise_and),
                            v.tensor_copy(out=fv[:, 0:nwords], in_=t1[:, 0:nwords]),
                            v.tensor_single_scalar(out=tv[:, 0:nwords],
                                                   in_=fv[:, 0:nwords],
                                                   scalar=7.5, op=ALU.is_ge),
                            v.scalar_tensor_tensor(out=mv[:, 0:nwords],
                                                   in0=tv[:, 0:nwords], scalar=-8.0,
                                                   in1=fv[:, 0:nwords],
                                                   op0=ALU.mult, op1=ALU.add),
                            v.tensor_scalar(out=sg[:, 0:nwords], in0=tv[:, 0:nwords],
                                            scalar1=-2.0, scalar2=1.0,
                                            op0=ALU.mult, op1=ALU.add)])(k, nwords, ngrp),
                        deps=[ld, prev.get('kread')])
                    ru = em.v(
                        (lambda nwords: lambda v: v.tensor_scalar(
                            out=rv[:, 0:nwords], in0=mv[:, 0:nwords],
                            scalar1=-6.0, scalar2=0.0,
                            op0=ALU.add, op1=ALU.max))(nwords),
                        deps=[u])
                    u2 = em.v(
                        (lambda k, nwords, ngrp: lambda v: [
                            v.scalar_tensor_tensor(out=bv[:, 0:nwords],
                                                   in0=mv[:, 0:nwords], scalar=2.0,
                                                   in1=rv[:, 0:nwords],
                                                   op0=ALU.subtract, op1=ALU.add),
                            v.tensor_scalar(out=av[:, 0:nwords], in0=mv[:, 0:nwords],
                                            scalar1=0.5, scalar2=None, op0=ALU.mult),
                            v.tensor_tensor(out=bv[:, 0:nwords], in0=av[:, 0:nwords],
                                            in1=bv[:, 0:nwords], op=ALU.max),
                            v.tensor_tensor(out=bv[:, 0:nwords], in0=bv[:, 0:nwords],
                                            in1=sg[:, 0:nwords], op=ALU.mult),
                            v.tensor_tensor(
                                out=wrow[:].rearrange("p (j k8) -> p j k8", k8=8)
                                [:, 0:nwords, k].rearrange("p (g j2) -> p g j2", j2=16),
                                in0=bv[:, 0:nwords].rearrange("p (g j2) -> p g j2", j2=16),
                                in1=sc32[:, 0:ngrp].rearrange("p g -> p g 1")
                                if False else
                                sc32[:, 0:ngrp].broadcast_to([128, ngrp, 16]),
                                op=ALU.mult)])(k, nwords, ngrp),
                        deps=[ru, scc, prev.get('wrowread')])
                    kunits.append(u2)
                prev['kread'] = kunits[-1]
                prev['lastk'] = kunits[-1]
                trus = []
                tr_mm = []
                for kc in range(nk):
                    tp = ps_tp[kc % 2]
                    tb = trev_t[kc % 4]
                    tru = em.t(
                        (lambda kc, tp: lambda t: t.transpose(
                            tp[:, 0:128], wrow[:, kc * 128:(kc + 1) * 128],
                            c_id[:]))(kc, tp),
                        deps=kunits + [consts, prev.get(('tprd', kc % 2))])
                    tr_mm.append(tru)
                    tre = em.v(
                        (lambda tp, tb: lambda v: v.tensor_copy(out=tb[:],
                                                                in_=tp[:, 0:128]))(tp, tb),
                        deps=[tru, prev.get(('trd', kc % 4))])
                    prev[('tprd', kc % 2)] = tre
                    od = em.sync_dma(
                        (lambda rc, kc, out_ap, tb: lambda sy: sy.dma_start(
                            out=out_ap.ap()[kc][:, rc * 128:(rc + 1) * 128],
                            in_=tb[:]))(rc, kc, out_ap, tb),
                        deps=[tre])
                    prev[('trd', kc % 4)] = od
                    trus.append(od)
                    outs.append(od)
                prev['wrowread'] = tr_mm[-1]
        em.g(lambda g: g.engine_nop(), deps=outs[-12:])
        em.realize(block, sems, dma_sems)
    return nc


def host_inputs_G(inputs):
    gpk = np.asarray(inputs["L1_ffn_gate_up_packed"])
    gsc = np.asarray(inputs["L1_ffn_gate_up_scales"])
    dpk = np.asarray(inputs["L1_ffn_down_packed"])
    dsc = np.asarray(inputs["L1_ffn_down_scales"])
    ident = np.eye(128, dtype=np.float32).astype(BF)
    maps = []
    for c in range(NC):
        g_sl = slice(c * 1408, (c + 1) * 1408)
        d_sl = slice(c * 256, (c + 1) * 256)
        maps.append({
            "gup_pk": np.ascontiguousarray(gpk[g_sl]).reshape(11, 128, 256),
            "gup_sc": np.ascontiguousarray(gsc[:, g_sl].T).reshape(11, 128, 16),
            "dn_pk": np.ascontiguousarray(dpk[d_sl]).reshape(2, 128, 704),
            "dn_sc": np.ascontiguousarray(dsc[:, d_sl].T).reshape(2, 128, 44),
            "ident16": ident,
        })
    return maps


def run_G(inputs):
    nc = build_G()
    maps = host_inputs_G(inputs)
    res = run_bass_kernel_spmd(nc, maps, core_ids=list(range(NC)))
    return res.results



def build_F1():
    nc = bass.Bass("TRN2", target_bir_lowering=False, debug=False)
    so_h = nc.dram_tensor("so16", [32, 128, TCK], BF16, kind="ExternalInput")
    ga_h = nc.dram_tensor("gate16", [32, 128, TCK], BF16, kind="ExternalInput")
    x2_h = nc.dram_tensor("x2T", [16, 128, TCK], F32, kind="ExternalInput")
    nw_h = nc.dram_tensor("nw", [128, 32], F32, kind="ExternalInput")
    wout_h = nc.dram_tensor("wout2_t", [16, 32, 128, 128], F32, kind="ExternalInput")
    onesc_h = nc.dram_tensor("onesc", [128, 1], F32, kind="ExternalInput")
    onesr_h = nc.dram_tensor("onesr", [1, 128], F32, kind="ExternalInput")
    x3_h = nc.dram_tensor("x3", [16, 128, TCK], F32, kind="ExternalOutput")
    xn3_h = nc.dram_tensor("xn3", [16, 128, TCK], BF16, kind="ExternalOutput")

    st = ExitStack()
    with st:
        def SB(name, shape, dt):
            return st.enter_context(nc.sbuf_tensor(name, shape, dt))

        def PSUM(name, shape):
            return st.enter_context(nc.psum_tensor(name, shape, F32))
        c_onesc = SB("c_onesc", [128, 1], F32)
        c_onesr = SB("c_onesr", [1, 128], F32)
        c_nw = SB("c_nw", [128, 32], F32)
        sbuf_a = [SB("sba0", [128, TCK], BF16), SB("sba1", [128, TCK], BF16)]
        sbuf_b = [SB("sbb0", [128, TCK], BF16), SB("sbb1", [128, TCK], BF16)]
        g16 = SB("g16", [128, 32, TCK], BF16)
        xs_ = SB("x3pre", [128, 16, TCK], F32)
        wbuf = [SB("wbuf0", [128, 32 * 128], BF16), SB("wbuf1", [128, 32 * 128], BF16)]
        sq = SB("sq", [128, TCK], F32)
        rows = SB("rows", [1, 4 * TCK], F32)
        rrow2 = SB("rrow2", [1, TCK], F32)
        mean_b = SB("mean_b", [128, TCK], F32)
        rstd_b = SB("rstd_b", [128, TCK], F32)
        rstd2_b = SB("rstd2_b", [128, TCK], F32)
        xn3b = SB("xn3b", [128, TCK], BF16)
        psmm = Slots([PSUM(f"ps_m{i}", [128, 512]) for i in range(4)])
        ps_st = [PSUM(f"ps_s{i}", [1, 512]) for i in range(4)]
        sems = {e: st.enter_context(nc.semaphore("sem_" + e)) for e in Emitter.ENGS}
        dma_sems = [st.enter_context(nc.semaphore(f"dsem{i}")) for i in range(Emitter.NPOOL)]
        block = st.enter_context(nc.Block())
        em = Emitter(nc)

        consts = em.sync_dma(lambda sy: [
            sy.dma_start(out=c_onesc[:], in_=onesc_h.ap()),
            sy.dma_start(out=c_onesr[:], in_=onesr_h.ap()),
            sy.dma_start(out=c_nw[:], in_=nw_h.ap())], n=3)
        ldx = em.sync_dma(lambda sy: sy.dma_start(
            out=xs_[:], in_=x2_h.ap().rearrange("kc p t -> p kc t")))

        gat = [None] * 32
        sq_st = []
        for dc in range(32):
            ld = em.sync_dma(
                (lambda dc: lambda sy: [
                    sy.dma_start(out=sbuf_a[dc % 2][:], in_=so_h.ap()[dc]),
                    sy.dma_start(out=sbuf_b[dc % 2][:], in_=ga_h.ap()[dc])])(dc),
                deps=[gat[dc - 2] if dc >= 2 else None], n=2)
            gat[dc] = em.v(
                (lambda dc: lambda v: v.tensor_tensor(
                    out=g16[:, dc, :], in0=sbuf_a[dc % 2][:], in1=sbuf_b[dc % 2][:],
                    op=ALU.mult))(dc),
                deps=[ld])
            sqd = em.s(
                (lambda dc: lambda sc: sc.activation(
                    out=sq[:], in_=g16[:, dc, :], func=AF.Square))(dc),
                deps=[gat[dc], sq_st[-1] if sq_st else None])
            sq_st.append(em.t(
                (lambda dc: lambda t: [
                    t.matmul(ps_st[0][:, 0:512], c_onesc[:], sq[:, 0:512],
                             start=(dc == 0), stop=(dc == 31)),
                    t.matmul(ps_st[1][:, 0:512], c_onesc[:], sq[:, 512:1024],
                             start=(dc == 0), stop=(dc == 31))])(dc),
                deps=[sqd, consts]))
        g1c = em.s(lambda sc: [
            sc.copy(out=rrow2[:, 0:512], in_=ps_st[0][:, 0:512]),
            sc.copy(out=rrow2[:, 512:1024], in_=ps_st[1][:, 0:512])],
            deps=[sq_st[31]])
        g1 = em.v(lambda v: v.tensor_scalar(out=rrow2[:], in0=rrow2[:],
                                            scalar1=1.0 / INTER, scalar2=EPS5,
                                            op0=ALU.mult, op1=ALU.add), deps=[g1c])
        g2 = em.s(lambda sc: sc.activation(out=rrow2[:], in_=rrow2[:], func=AF.Sqrt),
                  deps=[g1])
        g3 = em.v(lambda v: v.reciprocal(out=rrow2[:], in_=rrow2[:]), deps=[g2])
        bc2 = [psmm.next(), psmm.next()]
        g4 = em.t(lambda t: [
            t.matmul(psmm.tile(bc2[0])[:, 0:512], c_onesr[:], rrow2[:, 0:512],
                     start=True, stop=True),
            t.matmul(psmm.tile(bc2[1])[:, 0:512], c_onesr[:], rrow2[:, 512:1024],
                     start=True, stop=True)], deps=[g3, consts])
        g5 = em.s(lambda sc: [
            sc.copy(out=rstd2_b[:, 0:512], in_=psmm.tile(bc2[0])[:, 0:512]),
            sc.copy(out=rstd2_b[:, 512:1024], in_=psmm.tile(bc2[1])[:, 0:512])],
            deps=[g4])
        for s_ in bc2:
            psmm.reader[s_] = g5
        norm_done = [None] * 32
        for dc in range(32):
            norm_done[dc] = em.v(
                (lambda dc: lambda v: v.scalar_tensor_tensor(
                    out=g16[:, dc, :], in0=g16[:, dc, :],
                    scalar=c_nw[:, dc:dc + 1], in1=rstd2_b[:],
                    op0=ALU.mult, op1=ALU.mult))(dc),
                deps=[g5, gat[31]])

        ldw = [None] * 16
        stat_units = []
        ev_units = []
        for mc in range(16):
            ldw[mc] = em.gp_dma(
                (lambda mc: lambda g: g.dma_start(
                    out=wbuf[mc % 2][:].rearrange("p (kc j) -> p kc j", j=128),
                    in_=wout_h.ap()[mc].rearrange("kc p j -> p kc j")))(mc),
                deps=[ev_units[-3] if mc >= 2 else None])
            for i, (o, w) in enumerate(NCP_TC):
                s_ = psmm.next()
                mm = em.t(
                    (lambda mc, o, w, s_: lambda t: [
                        t.matmul(psmm.tile(s_)[:, 0:w],
                                 wbuf[mc % 2][:, kc * 128:(kc + 1) * 128],
                                 g16[:, kc, o:o + w],
                                 start=(kc == 0), stop=(kc == 31))
                        for kc in range(32)])(mc, o, w, s_),
                    deps=[ldw[mc], psmm.reader[s_]] + (norm_done if mc == 0 else []))
                ev = em.v(
                    (lambda mc, o, w, s_: lambda v: v.tensor_tensor(
                        out=xs_[:, mc, o:o + w], in0=xs_[:, mc, o:o + w],
                        in1=psmm.tile(s_)[:, 0:w], op=ALU.add))(mc, o, w, s_),
                    deps=[mm, ldx])
                psmm.reader[s_] = ev
                ev_units.append(ev)
            sqd = em.s(
                (lambda mc: lambda sc: sc.activation(
                    out=sq[:], in_=xs_[:, mc, :], func=AF.Square))(mc),
                deps=[ev_units[-1], ev_units[-2],
                      stat_units[-1] if stat_units else None])
            stat_units.append(em.t(
                (lambda mc: lambda t: [
                    t.matmul(ps_st[2][:, 0:512], c_onesc[:], xs_[:, mc, 0:512],
                             start=(mc == 0), stop=(mc == 15)),
                    t.matmul(ps_st[3][:, 0:512], c_onesc[:], xs_[:, mc, 512:1024],
                             start=(mc == 0), stop=(mc == 15)),
                    t.matmul(ps_st[0][:, 0:512], c_onesc[:], sq[:, 0:512],
                             start=(mc == 0), stop=(mc == 15)),
                    t.matmul(ps_st[1][:, 0:512], c_onesc[:], sq[:, 512:1024],
                             start=(mc == 0), stop=(mc == 15))])(mc),
                deps=[sqd, consts, sq_st[31] if mc == 0 else None]))

        rrc = em.s(lambda sc: [
            sc.copy(out=rows[:, 0:512], in_=ps_st[2][:, 0:512]),
            sc.copy(out=rows[:, 512:1024], in_=ps_st[3][:, 0:512]),
            sc.copy(out=rows[:, 1024:1536], in_=ps_st[0][:, 0:512]),
            sc.copy(out=rows[:, 1536:2048], in_=ps_st[1][:, 0:512])],
            deps=[stat_units[15]])
        rr = em.v(lambda v: [
            v.tensor_scalar(out=rows[:, 2048:3072], in0=rows[:, 0:1024],
                            scalar1=1.0 / DM, scalar2=None, op0=ALU.mult),
            v.tensor_scalar(out=rows[:, 1024:2048], in0=rows[:, 1024:2048],
                            scalar1=1.0 / DM, scalar2=None, op0=ALU.mult),
            v.tensor_tensor(out=rows[:, 0:1024], in0=rows[:, 2048:3072],
                            in1=rows[:, 2048:3072], op=ALU.mult),
            v.tensor_tensor(out=rows[:, 3072:4096], in0=rows[:, 1024:2048],
                            in1=rows[:, 0:1024], op=ALU.subtract),
            v.tensor_scalar(out=rows[:, 3072:4096], in0=rows[:, 3072:4096],
                            scalar1=EPS5, scalar2=None, op0=ALU.add)],
            deps=[rrc])
        r2 = em.s(lambda sc: sc.activation(out=rows[:, 3072:4096],
                                           in_=rows[:, 3072:4096], func=AF.Sqrt),
                  deps=[rr])
        r3 = em.v(lambda v: v.reciprocal(out=rows[:, 3072:4096],
                                         in_=rows[:, 3072:4096]), deps=[r2])
        bcs = [psmm.next() for _ in range(4)]
        r4 = em.t(lambda t: [
            t.matmul(psmm.tile(bcs[0])[:, 0:512], c_onesr[:], rows[:, 2048:2560],
                     start=True, stop=True),
            t.matmul(psmm.tile(bcs[1])[:, 0:512], c_onesr[:], rows[:, 2560:3072],
                     start=True, stop=True),
            t.matmul(psmm.tile(bcs[2])[:, 0:512], c_onesr[:], rows[:, 3072:3584],
                     start=True, stop=True),
            t.matmul(psmm.tile(bcs[3])[:, 0:512], c_onesr[:], rows[:, 3584:4096],
                     start=True, stop=True)],
            deps=[r3] + [psmm.reader[s_] for s_ in bcs])
        r5 = em.s(lambda sc: [
            sc.copy(out=mean_b[:, 0:512], in_=psmm.tile(bcs[0])[:, 0:512]),
            sc.copy(out=mean_b[:, 512:1024], in_=psmm.tile(bcs[1])[:, 0:512]),
            sc.copy(out=rstd_b[:, 0:512], in_=psmm.tile(bcs[2])[:, 0:512]),
            sc.copy(out=rstd_b[:, 512:1024], in_=psmm.tile(bcs[3])[:, 0:512])],
            deps=[r4])
        for s_ in bcs:
            psmm.reader[s_] = r5
        outs = []
        for mc in range(16):
            ap1 = em.v(
                (lambda mc: lambda v: [
                    v.tensor_tensor(out=xs_[:, mc, :], in0=xs_[:, mc, :],
                                    in1=mean_b[:], op=ALU.subtract),
                    v.tensor_tensor(out=xs_[:, mc, :], in0=xs_[:, mc, :],
                                    in1=rstd_b[:], op=ALU.mult)])(mc),
                deps=[r5, stat_units[15]])
            cp = em.s(
                (lambda mc: lambda sc: sc.copy(out=xn3b[:], in_=xs_[:, mc, :]))(mc),
                deps=[ap1, outs[-1] if mc >= 1 else None])
            o1 = em.sync_dma(
                (lambda mc: lambda sy: [
                    sy.dma_start(out=x3_h.ap()[mc], in_=xs_[:, mc, :]),
                    sy.dma_start(out=xn3_h.ap()[mc], in_=xn3b[:])])(mc),
                deps=[cp], n=2)
            outs.append(o1)
        em.g(lambda g: g.engine_nop(), deps=outs)
        em.realize(block, sems, dma_sems)
    return nc


def host_inputs_F1(resD, resE, resC2, inputs):
    y_all = np.concatenate(
        [np.asarray(r["yg16"]).view(BF).reshape(512, S) for r in resE], axis=0)
    wout = lhsT_tiles(np.asarray(inputs["L1_attn_Wout"]))
    nw = percol(np.asarray(inputs["L1_attn_norm_w"]))
    maps = []
    for c in range(NC):
        cols = slice(c * TCK, (c + 1) * TCK)
        maps.append({
            "so16": np.ascontiguousarray(y_all[:, cols]).reshape(32, 128, TCK),
            "gate16": np.asarray(resD[c]["gate16"]).view(BF).reshape(32, 128, TCK),
            "x2T": np.asarray(resC2[c]["x2"]).reshape(16, 128, TCK),
            "nw": nw,
            "wout2_t": wout,
            "onesc": np.ones((128, 1), np.float32),
            "onesr": np.ones((1, 128), np.float32),
        })
    return maps


def run_F1(resD, resE, resC2, inputs):
    nc = build_F1()
    maps = host_inputs_F1(resD, resE, resC2, inputs)
    res = run_bass_kernel_spmd(nc, maps, core_ids=list(range(NC)))
    return res.results



def build_F2():
    nc = bass.Bass("TRN2", target_bir_lowering=False, debug=False)
    xn3_h = nc.dram_tensor("xn3", [16, 128, TCK], BF16, kind="ExternalInput")
    x3_h = nc.dram_tensor("x3", [16, 128, TCK], F32, kind="ExternalInput")
    wgu_h = nc.dram_tensor("wgu_t", [88, 16, 128, 128], BF16, kind="ExternalInput")
    wdn_h = nc.dram_tensor("wdn_t", [16, 44, 128, 128], BF16, kind="ExternalInput")
    onesc_h = nc.dram_tensor("onesc", [128, 1], F32, kind="ExternalInput")
    onesr_h = nc.dram_tensor("onesr", [1, 128], F32, kind="ExternalInput")
    out_h = nc.dram_tensor("outp", [16, 128, TCK], F32, kind="ExternalOutput")

    TT = 512
    st = ExitStack()
    with st:
        def SB(name, shape, dt):
            return st.enter_context(nc.sbuf_tensor(name, shape, dt))

        def PSUM(name, shape):
            return st.enter_context(nc.psum_tensor(name, shape, F32))
        c_onesc = SB("c_onesc", [128, 1], F32)
        c_onesr = SB("c_onesr", [1, 128], F32)
        xn3 = SB("xn3s", [128, 16, TCK], BF16)
        hT = SB("hT", [128, 44, TT], BF16)
        gs = SB("gs", [128, TT], BF16)
        wbuf = [SB("wbuf0", [128, 16 * 128], BF16), SB("wbuf1", [128, 16 * 128], BF16),
                SB("wbuf2", [128, 16 * 128], BF16), SB("wbuf3", [128, 16 * 128], BF16)]
        wdb = [SB("wdb0", [128, 44 * 128], BF16), SB("wdb1", [128, 44 * 128], BF16)]
        yf = SB("yf", [128, 16, TT], F32)
        x3h = SB("x3h", [128, 16, TT], F32)
        sq = SB("sq", [128, TT], F32)
        rows = SB("rows", [1, 4 * TT], F32)
        rrow2 = SB("rrow2", [1, TT], F32)
        mean_b = SB("mean_b", [128, TT], F32)
        rstd_b = SB("rstd_b", [128, TT], F32)
        rms_b = SB("rms_b", [128, TT], F32)
        psmm = Slots([PSUM(f"ps_m{i}", [128, 512]) for i in range(4)])
        ps_st = [PSUM(f"ps_s{i}", [1, 512]) for i in range(4)]
        sems = {e: st.enter_context(nc.semaphore("sem_" + e)) for e in Emitter.ENGS}
        dma_sems = [st.enter_context(nc.semaphore(f"dsem{i}")) for i in range(Emitter.NPOOL)]
        block = st.enter_context(nc.Block())
        em = Emitter(nc)

        consts = em.sync_dma(lambda sy: [
            sy.dma_start(out=c_onesc[:], in_=onesc_h.ap()),
            sy.dma_start(out=c_onesr[:], in_=onesr_h.ap())], n=2)
        ldxn = em.sync_dma(lambda sy: sy.dma_start(
            out=xn3[:], in_=xn3_h.ap().rearrange("kc p t -> p kc t")))

        outs_all = []
        prev = {}
        for th in range(2):
            ldx3 = em.sync_dma(
                (lambda th: lambda sy: sy.dma_start(
                    out=x3h[:],
                    in_=x3_h.ap().rearrange("kc p t -> p kc t")
                    [:, :, th * TT:(th + 1) * TT]))(th),
                deps=prev.get('outs', []))
            hdone = []
            for g in range(44):
                ldg = em.gp_dma(
                    (lambda g: lambda gg: gg.dma_start(
                        out=wbuf[g % 2][:].rearrange("p (kc j) -> p kc j", j=128),
                        in_=wgu_h.ap()[g].rearrange("kc p j -> p kc j")))(g),
                    deps=[prev.get(('gmm', g % 2))])
                s_ = psmm.next()
                gmm = em.t(
                    (lambda g, th, s_: lambda t: [
                        t.matmul(psmm.tile(s_)[:, 0:TT],
                                 wbuf[g % 2][:, kc * 128:(kc + 1) * 128],
                                 xn3[:, kc, th * TT:(th + 1) * TT],
                                 start=(kc == 0), stop=(kc == 15))
                        for kc in range(16)])(g, th, s_),
                    deps=[ldg, ldxn, psmm.reader[s_]])
                prev[('gmm', g % 2)] = gmm
                gev = em.s(
                    (lambda s_: lambda sc: sc.activation(
                        out=gs[:], in_=psmm.tile(s_)[:, 0:TT], func=AF.Silu))(s_),
                    deps=[gmm, prev.get('gsr')])
                psmm.reader[s_] = gev
                ldu = em.gp_dma(
                    (lambda g: lambda gg: gg.dma_start(
                        out=wbuf[2 + g % 2][:].rearrange("p (kc j) -> p kc j", j=128),
                        in_=wgu_h.ap()[44 + g].rearrange("kc p j -> p kc j")))(g),
                    deps=[prev.get(('umm', g % 2))])
                s2 = psmm.next()
                umm = em.t(
                    (lambda g, th, s2: lambda t: [
                        t.matmul(psmm.tile(s2)[:, 0:TT],
                                 wbuf[2 + g % 2][:, kc * 128:(kc + 1) * 128],
                                 xn3[:, kc, th * TT:(th + 1) * TT],
                                 start=(kc == 0), stop=(kc == 15))
                        for kc in range(16)])(g, th, s2),
                    deps=[ldu, ldxn, psmm.reader[s2]])
                prev[('umm', g % 2)] = umm
                hu = em.v(
                    (lambda g, s2: lambda v: v.tensor_tensor(
                        out=hT[:, g, :], in0=gs[:], in1=psmm.tile(s2)[:, 0:TT],
                        op=ALU.mult))(g, s2),
                    deps=[umm, gev, prev.get('hTr'), prev.get(('dmm', 0))])
                psmm.reader[s2] = hu
                prev['gsr'] = hu
                hdone.append(hu)
            dev_units = []
            rms_st = []
            for mc in range(16):
                ldd = em.gp_dma(
                    (lambda mc: lambda gg: gg.dma_start(
                        out=wdb[mc % 2][:].rearrange("p (kc j) -> p kc j", j=128),
                        in_=wdn_h.ap()[mc].rearrange("kc p j -> p kc j")))(mc),
                    deps=[prev.get(('dmm', mc % 2))])
                s_ = psmm.next()
                dmm = em.t(
                    (lambda mc, s_: lambda t: [
                        t.matmul(psmm.tile(s_)[:, 0:TT], wdb[mc % 2][:, kc * 128:(kc + 1) * 128],
                                 hT[:, kc, :], start=(kc == 0), stop=(kc == 43))
                        for kc in range(44)])(mc, s_),
                    deps=[ldd, psmm.reader[s_]] + (hdone if mc == 0 else []))
                prev[('dmm', mc % 2)] = dmm
                dev = em.s(
                    (lambda mc, s_: lambda sc: sc.copy(
                        out=yf[:, mc, :], in_=psmm.tile(s_)[:, 0:TT]))(mc, s_),
                    deps=[dmm] + (prev.get('outs', []) if mc == 0 else []))
                psmm.reader[s_] = dev
                dev_units.append(dev)
                sqd = em.v(
                    (lambda mc: lambda v: v.tensor_tensor(
                        out=sq[:], in0=yf[:, mc, :], in1=yf[:, mc, :],
                        op=ALU.mult))(mc),
                    deps=[dev, rms_st[-1] if rms_st else None])
                rms_st.append(em.t(
                    (lambda mc: lambda t: t.matmul(
                        ps_st[0][:, 0:512], c_onesc[:], sq[:],
                        start=(mc == 0), stop=(mc == 15)))(mc),
                    deps=[sqd, consts]))
            q1c = em.s(lambda sc: sc.copy(out=rrow2[:], in_=ps_st[0][:, 0:512]),
                       deps=[rms_st[15]])
            q1 = em.v(lambda v: v.tensor_scalar(out=rrow2[:], in0=rrow2[:],
                                                scalar1=1.0 / DM, scalar2=EPS6,
                                                op0=ALU.mult, op1=ALU.add), deps=[q1c])
            q2 = em.s(lambda sc: sc.activation(out=rrow2[:], in_=rrow2[:],
                                               func=AF.Sqrt), deps=[q1])
            q3 = em.v(lambda v: v.reciprocal(out=rrow2[:], in_=rrow2[:]), deps=[q2])
            bq = psmm.next()
            q4 = em.t(lambda t: t.matmul(psmm.tile(bq)[:, 0:512], c_onesr[:],
                                         rrow2[:], start=True, stop=True),
                      deps=[q3, consts, psmm.reader[bq]])
            q5 = em.s(lambda sc: sc.copy(out=rms_b[:], in_=psmm.tile(bq)[:, 0:512]),
                      deps=[q4])
            psmm.reader[bq] = q5
            st_units = []
            for mc in range(16):
                a1 = em.v(
                    (lambda mc: lambda v: [
                        v.tensor_tensor(out=yf[:, mc, :], in0=yf[:, mc, :],
                                        in1=rms_b[:], op=ALU.mult),
                        v.tensor_tensor(out=yf[:, mc, :], in0=yf[:, mc, :],
                                        in1=x3h[:, mc, :], op=ALU.add)])(mc),
                    deps=[q5, ldx3, rms_st[15]])
                sqd = em.s(
                    (lambda mc: lambda sc: sc.activation(
                        out=sq[:], in_=yf[:, mc, :], func=AF.Square))(mc),
                    deps=[a1, st_units[-1] if st_units else None])
                st_units.append(em.t(
                    (lambda mc: lambda t: [
                        t.matmul(ps_st[1][:, 0:512], c_onesc[:], yf[:, mc, :],
                                 start=(mc == 0), stop=(mc == 15)),
                        t.matmul(ps_st[2][:, 0:512], c_onesc[:], sq[:],
                                 start=(mc == 0), stop=(mc == 15))])(mc),
                    deps=[sqd, consts]))
            rrc = em.s(lambda sc: [
                sc.copy(out=rows[:, 0:512], in_=ps_st[1][:, 0:512]),
                sc.copy(out=rows[:, 512:1024], in_=ps_st[2][:, 0:512])],
                deps=[st_units[15]])
            rr = em.v(lambda v: [
                v.tensor_scalar(out=rows[:, 1024:1536], in0=rows[:, 0:512],
                                scalar1=1.0 / DM, scalar2=None, op0=ALU.mult),
                v.tensor_scalar(out=rows[:, 512:1024], in0=rows[:, 512:1024],
                                scalar1=1.0 / DM, scalar2=None, op0=ALU.mult),
                v.tensor_tensor(out=rows[:, 0:512], in0=rows[:, 1024:1536],
                                in1=rows[:, 1024:1536], op=ALU.mult),
                v.tensor_tensor(out=rows[:, 1536:2048], in0=rows[:, 512:1024],
                                in1=rows[:, 0:512], op=ALU.subtract),
                v.tensor_scalar(out=rows[:, 1536:2048], in0=rows[:, 1536:2048],
                                scalar1=EPS5, scalar2=None, op0=ALU.add)],
                deps=[rrc])
            r2 = em.s(lambda sc: sc.activation(out=rows[:, 1536:2048],
                                               in_=rows[:, 1536:2048], func=AF.Sqrt),
                      deps=[rr])
            r3 = em.v(lambda v: v.reciprocal(out=rows[:, 1536:2048],
                                             in_=rows[:, 1536:2048]), deps=[r2])
            bcs = [psmm.next(), psmm.next()]
            r4 = em.t(lambda t: [
                t.matmul(psmm.tile(bcs[0])[:, 0:512], c_onesr[:], rows[:, 1024:1536],
                         start=True, stop=True),
                t.matmul(psmm.tile(bcs[1])[:, 0:512], c_onesr[:], rows[:, 1536:2048],
                         start=True, stop=True)],
                deps=[r3] + [psmm.reader[s_] for s_ in bcs])
            r5 = em.s(lambda sc: [
                sc.copy(out=mean_b[:], in_=psmm.tile(bcs[0])[:, 0:512]),
                sc.copy(out=rstd_b[:], in_=psmm.tile(bcs[1])[:, 0:512])],
                deps=[r4])
            for s_ in bcs:
                psmm.reader[s_] = r5
            outs = []
            for mc in range(16):
                a2 = em.v(
                    (lambda mc: lambda v: [
                        v.tensor_tensor(out=yf[:, mc, :], in0=yf[:, mc, :],
                                        in1=mean_b[:], op=ALU.subtract),
                        v.tensor_tensor(out=yf[:, mc, :], in0=yf[:, mc, :],
                                        in1=rstd_b[:], op=ALU.mult)])(mc),
                    deps=[r5, st_units[15]])
                o1 = em.sync_dma(
                    (lambda mc, th: lambda sy: sy.dma_start(
                        out=out_h.ap()[mc][:, th * TT:(th + 1) * TT],
                        in_=yf[:, mc, :]))(mc, th),
                    deps=[a2])
                outs.append(o1)
            outs_all += outs
            prev['outs'] = list(outs)
            prev['hTr'] = prev[('dmm', 1)]
        em.g(lambda g: g.engine_nop(), deps=outs_all)
        em.realize(block, sems, dma_sems)
    return nc


def host_inputs_F2(resF1, resG, inputs):
    wgu = np.concatenate(
        [np.asarray(rr["wguT"]).view(BF) for rr in resG], axis=2).reshape(2048, 11264)
    wdn = np.concatenate(
        [np.asarray(rr["wdnT"]).view(BF) for rr in resG], axis=2).reshape(5632, 2048)
    wgu_t = np.ascontiguousarray(
        wgu.reshape(16, 128, 88, 128).transpose(2, 0, 1, 3))
    wdn_t = np.ascontiguousarray(
        wdn.reshape(44, 128, 16, 128).transpose(2, 0, 1, 3))
    maps = []
    for c in range(NC):
        maps.append({
            "xn3": np.asarray(resF1[c]["xn3"]).view(BF).reshape(16, 128, TCK),
            "x3": np.asarray(resF1[c]["x3"]).reshape(16, 128, TCK),
            "wgu_t": wgu_t,
            "wdn_t": wdn_t,
            "onesc": np.ones((128, 1), np.float32),
            "onesr": np.ones((1, 128), np.float32),
        })
    return maps


def run_F2(resF1, resG, inputs):
    nc = build_F2()
    maps = host_inputs_F2(resF1, resG, inputs)
    res = run_bass_kernel_spmd(nc, maps, core_ids=list(range(NC)))
    return res.results


# version 05 (final)
# speedup vs baseline: 1.1413x; 1.0643x over previous
from contextlib import ExitStack
import numpy as np
import ml_dtypes
import concourse.bass as bass
import concourse.mybir as mybir
from concourse.bass_utils import run_bass_kernel_spmd

F32 = mybir.dt.float32
BF16 = mybir.dt.bfloat16
F16 = mybir.dt.float16
I32 = mybir.dt.int32
U8 = mybir.dt.uint8
AF = mybir.ActivationFunctionType
ALU = mybir.AluOpType
BF = ml_dtypes.bfloat16

NC = 8
S = 8192
TCK = 1024
HALO = 4
TH = TCK + HALO
DM = 2048
DI = 4096
DSTATE = 16
EPS5 = 1e-5
EPS6 = 1e-6

NHEADS = 64
HEADDIM = 64
INTER = 4096
NGROUPS = 8
NSTATE = 128
CONVD = 6144
I2 = 5632

NCP_TH = [(0, 512), (512, 512), (1024, TH - 1024)]
NCP_TC = [(0, 512), (512, 512)]



class _Unit:
    __slots__ = ("eng", "fn", "deps", "dma", "n_dma", "flagged", "count_after",
                 "sem_idx", "sem_target")

    def __init__(self, eng, fn, deps, dma, n_dma):
        self.eng = eng
        self.fn = fn
        self.deps = [d for d in deps if d is not None]
        self.dma = dma
        self.n_dma = n_dma
        self.flagged = False
        self.count_after = None
        self.sem_idx = None
        self.sem_target = None

    def dep(self, *units):
        for u in units:
            if u is not None:
                self.deps.append(u)
        return self


class Emitter:
    ENGS = ("sync", "gpsimd", "vector", "scalar", "tensor")
    NPOOL = 12

    def __init__(self, nc):
        self.nc = nc
        self.units = {e: [] for e in self.ENGS}
        self.pool_i = 0
        self.pool_totals = [0] * self.NPOOL
        self.pool_last_unit = [None] * self.NPOOL

    def unit(self, eng, fn, deps=(), dma=False, n_dma=1):
        u = _Unit(eng, fn, deps, dma, n_dma)
        if dma:
            i = self.pool_i % self.NPOOL
            self.pool_i += 1
            u.sem_idx = i
            prev = self.pool_last_unit[i]
            if prev is not None:
                u.deps.append(prev)
            self.pool_totals[i] += 16 * n_dma
            u.sem_target = self.pool_totals[i]
            self.pool_last_unit[i] = u
        self.units[eng].append(u)
        return u

    def sync_dma(self, fn, deps=(), n=1):
        return self.unit("sync", fn, deps, dma=True, n_dma=n)

    def gp_dma(self, fn, deps=(), n=1):
        return self.unit("gpsimd", fn, deps, dma=True, n_dma=n)

    def v(self, fn, deps=()):
        return self.unit("vector", fn, deps)

    def s(self, fn, deps=()):
        return self.unit("scalar", fn, deps)

    def t(self, fn, deps=()):
        return self.unit("tensor", fn, deps)

    def g(self, fn, deps=()):
        return self.unit("gpsimd", fn, deps)

    def realize(self, block, sems, dma_sems):
        for e in self.ENGS:
            for u in self.units[e]:
                for d in u.deps:
                    if d.dma or d.eng != u.eng:
                        d.flagged = True
        for e in self.ENGS:
            c = 0
            for u in self.units[e]:
                if u.flagged and not u.dma:
                    c += 1
                u.count_after = c
        em = self

        def mk(engname):
            def body(eng):
                waited = {}
                for u in em.units[engname]:
                    for d in u.deps:
                        if d.dma:
                            key = ("dma", d.sem_idx)
                            tgt = d.sem_target
                            semh = dma_sems[d.sem_idx]
                        else:
                            if d.eng == engname:
                                continue
                            key = d.eng
                            tgt = d.count_after
                            semh = sems[d.eng]
                        if waited.get(key, -1) >= tgt:
                            continue
                        eng.wait_ge(semh, tgt)
                        waited[key] = tgt
                    ins = u.fn(eng)
                    if u.dma:
                        if not isinstance(ins, (list, tuple)):
                            ins = [ins]
                        assert len(ins) == u.n_dma, (
                            f"{engname} dma unit: expected {u.n_dma} dmas, got {len(ins)}")
                        for i_ in ins:
                            i_.then_inc(dma_sems[u.sem_idx], 16)
                    elif u.flagged:
                        if isinstance(ins, (list, tuple)):
                            ins = ins[-1]
                        if ins is None:
                            ins = eng.nop()
                        ins.then_inc(sems[u.eng], 1)
            return body

        block.sync(mk("sync"))
        block.gpsimd(mk("gpsimd"))
        block.vector(mk("vector"))
        block.scalar(mk("scalar"))
        block.tensor(mk("tensor"))


class Slots:

    def __init__(self, tiles):
        self.tiles = tiles
        self.i = 0
        self.reader = [None] * len(tiles)
        self.writer = [None] * len(tiles)

    def next(self):
        s = self.i % len(self.tiles)
        self.i += 1
        return s

    def tile(self, s):
        return self.tiles[s]



def lhsT_tiles(w):
    K, M = w.shape
    assert K % 128 == 0 and M % 128 == 0
    return np.ascontiguousarray(
        w.reshape(K // 128, 128, M // 128, 128).transpose(2, 0, 1, 3))


def percol(v):
    n = v.shape[0]
    return np.ascontiguousarray(v.reshape(n // 128, 128).T)


def tok_slices(x_t, halo):
    D = x_t.shape[0]
    out = []
    for c in range(NC):
        lo = c * TCK - halo
        sl = np.zeros((D, TCK + halo), x_t.dtype)
        if lo < 0:
            sl[:, halo:] = x_t[:, c * TCK:(c + 1) * TCK]
        else:
            sl[:, :] = x_t[:, lo:(c + 1) * TCK]
        out.append(np.ascontiguousarray(sl))
    return out



def build_A():
    nc = bass.Bass("TRN2", target_bir_lowering=False, debug=False)
    xT_h = nc.dram_tensor("xT", [16, 128, TH], F32, kind="ExternalInput")
    wn_h = nc.dram_tensor("wn", [128, 16], F32, kind="ExternalInput")
    cwt_h = nc.dram_tensor("cwt", [128, 128], F32, kind="ExternalInput")
    cb_h = nc.dram_tensor("cb", [128, 32], F32, kind="ExternalInput")
    bdt_h = nc.dram_tensor("bdt", [128, 32], F32, kind="ExternalInput")
    onesc_h = nc.dram_tensor("onesc", [128, 1], F32, kind="ExternalInput")
    onesr_h = nc.dram_tensor("onesr", [1, 128], F32, kind="ExternalInput")
    win_h = nc.dram_tensor("win_t", [64, 16, 128, 128], F32, kind="ExternalInput")
    wxp_h = nc.dram_tensor("wxp_t", [2, 32, 128, 128], F32, kind="ExternalInput")
    wdt_h = nc.dram_tensor("wdt_t", [32, 1, 128, 128], F32, kind="ExternalInput")

    xs_h = nc.dram_tensor("xs16", [32, 128, TCK], BF16, kind="ExternalOutput")
    res_h = nc.dram_tensor("res16", [32, 128, TCK], BF16, kind="ExternalOutput")
    delta_h = nc.dram_tensor("delta", [32, 128, TCK], F32, kind="ExternalOutput")
    bc_h = nc.dram_tensor("bc", [32, TCK], F32, kind="ExternalOutput")

    st = ExitStack()
    with st:
        def SB(name, shape, dt):
            return st.enter_context(nc.sbuf_tensor(name, shape, dt))

        def PSUM(name, shape):
            return st.enter_context(nc.psum_tensor(name, shape, F32))

        c_wn = SB("c_wn", [128, 16], F32)
        c_cwt = SB("c_cwt", [128, 128], F32)
        c_cb = SB("c_cb", [128, 32], F32)
        c_bdt = SB("c_bdt", [128, 32], F32)
        c_onesc = SB("c_onesc", [128, 1], F32)
        c_onesr = SB("c_onesr", [1, 128], F32)
        xbuf0 = SB("xbuf0", [128, TH], F32)
        xbuf1 = SB("xbuf1", [128, TH], F32)
        sq = SB("sq", [128, TH], F32)
        rrow = SB("rrow", [1, TH], F32)
        rstd_b = SB("rstd_b", [128, TH], F32)
        xn16 = SB("xn16", [128, 16, TH], BF16)
        wbuf0 = SB("wbuf0", [128, 2048], BF16)
        wbuf1 = SB("wbuf1", [128, 2048], BF16)
        xsraw = SB("xsraw", [128, TH], F32)
        cacc = SB("cacc", [128, TCK], F32)
        res0 = SB("res0", [128, TCK], BF16)
        res1 = SB("res1", [128, TCK], BF16)
        xs16s = SB("xs16s", [128, 32, TCK], BF16)
        wxp16 = SB("wxp16", [128, 2 * 32 * 128], BF16)
        wdt16 = SB("wdt16", [128, 32 * 128], BF16)
        dtr16 = SB("dtr16", [128, TCK], BF16)
        bcbuf = SB("bcbuf", [32, TCK], F32)
        sp1 = SB("sp1", [128, TCK], F32)
        sp2 = SB("sp2", [128, TCK], F32)
        delt0 = SB("delt0", [128, TCK], F32)
        delt1 = SB("delt1", [128, TCK], F32)
        ps_s0 = PSUM("ps_s0", [1, 512])
        ps_s1 = PSUM("ps_s1", [1, 512])
        ps_s2 = PSUM("ps_s2", [1, 512])
        ps_m0 = PSUM("ps_m0", [128, 512])
        ps_m1 = PSUM("ps_m1", [128, 512])
        ps_m2 = PSUM("ps_m2", [128, 512])
        ps_m3 = PSUM("ps_m3", [128, 512])
        sems = {e: st.enter_context(nc.semaphore("sem_" + e)) for e in Emitter.ENGS}
        dma_sems = [st.enter_context(nc.semaphore(f"dsem{i}")) for i in range(Emitter.NPOOL)]
        block = st.enter_context(nc.Block())
        em = Emitter(nc)
        psum_stats = [ps_s0, ps_s1, ps_s2]
        xbufs = [xbuf0, xbuf1]
        wbufs = [wbuf0, wbuf1]
        ps = Slots([ps_m0, ps_m1, ps_m2, ps_m3])
        resb = [res0, res1]

        consts = em.sync_dma(lambda sy: [
            sy.dma_start(out=c_wn[:], in_=wn_h.ap()),
            sy.dma_start(out=c_cwt[:], in_=cwt_h.ap()),
            sy.dma_start(out=c_cb[:], in_=cb_h.ap()),
            sy.dma_start(out=c_bdt[:], in_=bdt_h.ap()),
            sy.dma_start(out=c_onesc[:], in_=onesc_h.ap()),
            sy.dma_start(out=c_onesr[:], in_=onesr_h.ap()),
        ], n=6)
        ldwx = em.gp_dma(lambda g: g.dma_start(
            out=wxp16[:].rearrange("p (g j) -> p g j", j=128),
            in_=wxp_h.ap().rearrange("m kc p j -> p (m kc) j")))
        ldwd = em.gp_dma(lambda g: g.dma_start(
            out=wdt16[:].rearrange("p (dc j) -> p dc j", j=128),
            in_=wdt_h.ap().rearrange("dc one p j -> p (dc one) j")))

        sq_done = [None] * 16
        stat_done = [None] * 16
        for kc in range(16):
            ld = em.sync_dma(
                (lambda kc: lambda sy: sy.dma_start(
                    out=xbufs[kc % 2][:], in_=xT_h.ap()[kc]))(kc),
                deps=[sq_done[kc - 2] if kc >= 2 else None])
            sq_done[kc] = em.s(
                (lambda kc: lambda sc: sc.activation(
                    out=sq[:], in_=xbufs[kc % 2][:], func=AF.Square))(kc),
                deps=[ld, stat_done[kc - 1] if kc >= 1 else None])
            stat_done[kc] = em.t(
                (lambda kc: lambda t: [
                    t.matmul(psum_stats[i][:, 0:w], c_onesc[:], sq[:, o:o + w],
                             start=(kc == 0), stop=(kc == 15))
                    for i, (o, w) in enumerate(NCP_TH)])(kc),
                deps=[sq_done[kc], consts])

        r1c = em.s(lambda sc: [
            sc.copy(out=rrow[:, o:o + w], in_=psum_stats[i][:, 0:w])
            for i, (o, w) in enumerate(NCP_TH)],
            deps=[stat_done[15]])
        r1 = em.v(lambda v: v.tensor_scalar(out=rrow[:], in0=rrow[:], scalar1=1.0 / DM,
                                            scalar2=EPS5, op0=ALU.mult, op1=ALU.add),
                  deps=[r1c])
        r2 = em.s(lambda sc: sc.activation(out=rrow[:], in_=rrow[:], func=AF.Sqrt),
                  deps=[r1])
        r3 = em.v(lambda v: v.reciprocal(out=rrow[:], in_=rrow[:]), deps=[r2])
        bc_slots = [ps.next() for _ in range(3)]
        r4 = em.t(lambda t: [
            t.matmul(ps.tile(bc_slots[i])[:, 0:w], c_onesr[:], rrow[:, o:o + w],
                     start=True, stop=True)
            for i, (o, w) in enumerate(NCP_TH)], deps=[r3, consts])
        r5 = em.s(lambda sc: [
            sc.copy(out=rstd_b[:, o:o + w], in_=ps.tile(bc_slots[i])[:, 0:w])
            for i, (o, w) in enumerate(NCP_TH)], deps=[r4])
        for s_ in bc_slots:
            ps.reader[s_] = r5

        xn_done = [None] * 16
        for kc in range(16):
            ld = em.sync_dma(
                (lambda kc: lambda sy: sy.dma_start(
                    out=xbufs[kc % 2][:], in_=xT_h.ap()[kc]))(kc),
                deps=[xn_done[kc - 2] if kc >= 2 else sq_done[15 - (1 - kc)]])
            xn_done[kc] = em.v(
                (lambda kc: lambda v: v.scalar_tensor_tensor(
                    out=xn16[:, kc, :], in0=xbufs[kc % 2][:],
                    scalar=c_wn[:, kc:kc + 1], in1=rstd_b[:],
                    op0=ALU.mult, op1=ALU.mult))(kc),
                deps=[ld, r5, consts])

        ldw_units = [None] * 64
        conv_done = [None] * 32
        resdma = [None] * 32
        last_mm_user_of_wbuf = [None, None]
        for mc in range(64):
            ldw_units[mc] = em.gp_dma(
                (lambda mc: lambda g: g.dma_start(
                    out=wbufs[mc % 2][:].rearrange("p (kc j) -> p kc j", j=128),
                    in_=win_h.ap()[mc].rearrange("kc p j -> p kc j")))(mc),
                deps=[last_mm_user_of_wbuf[mc % 2]])
            evacs = []
            for i, (o, w) in enumerate(NCP_TH):
                s_ = ps.next()
                mm = em.t(
                    (lambda mc, o, w, s_: lambda t: [
                        t.matmul(ps.tile(s_)[:, 0:w],
                                 wbufs[mc % 2][:, kc * 128:(kc + 1) * 128],
                                 xn16[:, kc, o:o + w],
                                 start=(kc == 0), stop=(kc == 15))
                        for kc in range(16)])(mc, o, w, s_),
                    deps=[ldw_units[mc], xn_done[15], ps.reader[s_]])
                if mc < 32:
                    ev = em.s(
                        (lambda o, w, s_: lambda sc: sc.copy(
                            out=xsraw[:, o:o + w], in_=ps.tile(s_)[:, 0:w]))(o, w, s_),
                        deps=[mm, conv_done[mc - 1] if (i == 0 and mc >= 1) else None])
                else:
                    def resf(o, w, s_, mc):
                        def fn(sc):
                            if o == 0:
                                return sc.activation(out=resb[mc % 2][:, 0:w - HALO],
                                                     in_=ps.tile(s_)[:, HALO:w],
                                                     func=AF.Silu)
                            return sc.activation(
                                out=resb[mc % 2][:, o - HALO:o - HALO + w],
                                in_=ps.tile(s_)[:, 0:w], func=AF.Silu)
                        return fn
                    ev = em.s(resf(o, w, s_, mc),
                              deps=[mm, resdma[mc - 2 - 32] if (i == 0 and mc >= 34) else None])
                ps.reader[s_] = ev
                evacs.append(ev)
            last_mm_user_of_wbuf[mc % 2] = mm
            if mc < 32:
                cd = em.v(
                    (lambda mc: lambda v: [
                        v.tensor_scalar(out=cacc[:], in0=xsraw[:, 1:1 + TCK],
                                        scalar1=c_cwt[:, mc * 4:mc * 4 + 1],
                                        scalar2=c_cb[:, mc:mc + 1],
                                        op0=ALU.mult, op1=ALU.add),
                        *[v.scalar_tensor_tensor(
                            out=cacc[:], in0=xsraw[:, 1 + wt:1 + wt + TCK],
                            scalar=c_cwt[:, mc * 4 + wt:mc * 4 + wt + 1],
                            in1=cacc[:], op0=ALU.mult, op1=ALU.add)
                          for wt in range(1, 4)]])(mc),
                    deps=evacs + [conv_done[mc - 1] if mc >= 1 else None])
                conv_done[mc] = em.s(
                    (lambda mc: lambda sc: sc.activation(
                        out=xs16s[:, mc, :], in_=cacc[:], func=AF.Silu))(mc),
                    deps=[cd])
            else:
                resdma[mc - 32] = em.sync_dma(
                    (lambda mc: lambda sy: sy.dma_start(
                        out=res_h.ap()[mc - 32], in_=resb[mc % 2][:]))(mc),
                    deps=evacs)

        xs_out = em.sync_dma(lambda sy: [
            sy.dma_start(out=xs_h.ap()[mcc], in_=xs16s[:, mcc, :])
            for mcc in range(32)], deps=list(conv_done), n=32)

        xdbc_ev = {}
        for m2 in range(2):
            for i, (o, w) in enumerate(NCP_TC):
                s_ = ps.next()
                mm = em.t(
                    (lambda m2, o, w, s_: lambda t: [
                        t.matmul(ps.tile(s_)[:, 0:w],
                                 wxp16[:, (m2 * 32 + kc) * 128:(m2 * 32 + kc + 1) * 128],
                                 xs16s[:, kc, o:o + w],
                                 start=(kc == 0), stop=(kc == 31))
                        for kc in range(32)])(m2, o, w, s_),
                    deps=[ldwx, ps.reader[s_]] + list(conv_done))
                if m2 == 0:
                    ev = em.s(
                        (lambda o, w, s_: lambda sc: sc.copy(
                            out=dtr16[:, o:o + w], in_=ps.tile(s_)[:, 0:w]))(o, w, s_),
                        deps=[mm])
                else:
                    ev = em.v(
                        (lambda o, w, s_: lambda v: v.tensor_copy(
                            out=bcbuf[:, o:o + w], in_=ps.tile(s_)[0:32, 0:w]))(o, w, s_),
                        deps=[mm])
                ps.reader[s_] = ev
                xdbc_ev[(m2, i)] = ev
        bc_out = em.sync_dma(
            lambda sy: sy.dma_start(out=bc_h.ap(), in_=bcbuf[:]),
            deps=[xdbc_ev[(1, 0)], xdbc_ev[(1, 1)]])

        deltb = [delt0, delt1]
        add_done = [None] * 32
        ddma = [None] * 32
        for dc in range(32):
            acts = []
            for i, (o, w) in enumerate(NCP_TC):
                s_ = ps.next()
                mm = em.t(
                    (lambda dc, o, w, s_: lambda t: t.matmul(
                        ps.tile(s_)[:, 0:w], wdt16[:, dc * 128:(dc + 1) * 128],
                        dtr16[:, o:o + w], start=True, stop=True))(dc, o, w, s_),
                    deps=[ldwd, xdbc_ev[(0, 0)], xdbc_ev[(0, 1)], ps.reader[s_]])
                act = em.s(
                    (lambda dc, o, w, s_: lambda sc: [
                        sc.activation(out=sp1[:, o:o + w], in_=ps.tile(s_)[:, 0:w],
                                      func=AF.Abs, bias=c_bdt[:, dc:dc + 1]),
                        sc.activation(out=sp1[:, o:o + w], in_=sp1[:, o:o + w],
                                      func=AF.Exp, scale=-1.0),
                        sc.activation(out=sp1[:, o:o + w], in_=sp1[:, o:o + w],
                                      func=AF.Ln, bias=1.0),
                        sc.activation(out=sp2[:, o:o + w], in_=ps.tile(s_)[:, 0:w],
                                      func=AF.Relu, bias=c_bdt[:, dc:dc + 1]),
                    ][-1])(dc, o, w, s_),
                    deps=[mm, add_done[dc - 1] if (i == 0 and dc >= 1) else None])
                ps.reader[s_] = act
                acts.append(act)
            add_done[dc] = em.v(
                (lambda dc: lambda v: v.tensor_tensor(
                    out=deltb[dc % 2][:], in0=sp1[:], in1=sp2[:], op=ALU.add))(dc),
                deps=acts + [ddma[dc - 2] if dc >= 2 else None])
            ddma[dc] = em.sync_dma(
                (lambda dc: lambda sy: sy.dma_start(
                    out=delta_h.ap()[dc], in_=deltb[dc % 2][:]))(dc),
                deps=[add_done[dc]])

        em.g(lambda g: g.engine_nop(),
             deps=[xs_out, bc_out] + ddma + resdma)

        em.realize(block, sems, dma_sems)
    return nc


def host_inputs_A(inputs):
    x = np.asarray(inputs["L0_attn_x"])[0]
    xT = np.ascontiguousarray(x.T)
    xs = tok_slices(xT, HALO)
    w_in = np.asarray(inputs["L0_attn_W_in"])
    w_xp = np.asarray(inputs["L0_attn_W_xp"])
    w_xp_pad = np.zeros((DI, 256), np.float32)
    w_xp_pad[:, :160] = w_xp
    w_dt = np.asarray(inputs["L0_attn_W_dt"])
    conv_w = np.asarray(inputs["L0_attn_conv_w"])
    cwt = np.ascontiguousarray(
        conv_w.T.reshape(32, 128, 4).transpose(1, 0, 2).reshape(128, 128))
    shared = {
        "wn": percol(np.asarray(inputs["L0_attn_w_norm"])),
        "cwt": cwt,
        "cb": percol(np.asarray(inputs["L0_attn_conv_b"])),
        "bdt": percol(np.asarray(inputs["L0_attn_b_dt"])),
        "onesc": np.ones((128, 1), np.float32),
        "onesr": np.ones((1, 128), np.float32),
        "win_t": lhsT_tiles(w_in),
        "wxp_t": lhsT_tiles(w_xp_pad),
        "wdt_t": lhsT_tiles(w_dt),
    }
    maps = []
    for c in range(NC):
        m = dict(shared)
        m["xT"] = xs[c].reshape(16, 128, TH)
        maps.append(m)
    return maps


def run_A(inputs):
    nc = build_A()
    maps = host_inputs_A(inputs)
    res = run_bass_kernel_spmd(nc, maps, core_ids=list(range(NC)))
    return res.results


def kernel(**inputs):
    resG = run_G(inputs)
    resA = run_A(inputs)
    resB = run_B(resA, inputs)
    resC1 = run_C1(resA, resB, inputs)
    resC2 = run_C2(resC1, inputs)
    resD = run_D(resC2, inputs)
    resE = run_E(resD, inputs)
    resF1 = run_F1(resD, resE, resC2, inputs)
    resF2 = run_F2(resF1, resG, inputs)
    out = np.concatenate(
        [np.asarray(rr["outp"]).reshape(DM, TCK) for rr in resF2], axis=1).T[None]
    return np.ascontiguousarray(out.astype(np.float32))



def build_B():
    nc = bass.Bass("TRN2", target_bir_lowering=False, debug=False)
    delta_h = nc.dram_tensor("delta_s", [4, 128, S], F32, kind="ExternalInput")
    xs_h = nc.dram_tensor("xs_s", [4, 128, S], BF16, kind="ExternalInput")
    bf_h = nc.dram_tensor("bfull", [16, S], F32, kind="ExternalInput")
    cf_h = nc.dram_tensor("cfull", [16, S], F32, kind="ExternalInput")
    alog_h = nc.dram_tensor("alog_s", [128, 64], F32, kind="ExternalInput")
    sel_h = nc.dram_tensor("sel16", [16, 16 * 128], BF16, kind="ExternalInput")
    d_h = nc.dram_tensor("d_s", [128, 4], F32, kind="ExternalInput")
    g_h = nc.dram_tensor("y16", [4, 128, S], BF16, kind="ExternalOutput")

    st = ExitStack()
    with st:
        def SB(name, shape, dt):
            return st.enter_context(nc.sbuf_tensor(name, shape, dt))
        deltab = SB("deltab", [128, S], F32)
        xsb = SB("xsb", [128, S], BF16)
        dub = SB("dub", [128, S], BF16)
        yb = SB("yb", [128, S], BF16)
        dab = SB("dab", [128, S], F32)
        dbub = SB("dbub", [128, S], BF16)
        hb = dbub
        bbc = [SB("bbc0", [128, S], BF16), SB("bbc1", [128, S], BF16)]
        asb = SB("asb", [128, 64], F32)
        dsb = SB("dsb", [128, 4], F32)
        bfs = SB("bfs", [16, S], BF16)
        cfs = SB("cfs", [16, S], BF16)
        c_sel = SB("c_sel", [16, 16 * 128], BF16)
        psb = [st.enter_context(nc.psum_tensor(f"psb{i}", [128, 512], F32))
               for i in range(4)]
        sems = {e: st.enter_context(nc.semaphore("sem_" + e)) for e in Emitter.ENGS}
        dma_sems = [st.enter_context(nc.semaphore(f"dsem{i}")) for i in range(Emitter.NPOOL)]
        block = st.enter_context(nc.Block())
        em = Emitter(nc)

        lda = em.sync_dma(lambda sy: [
            sy.dma_start(out=asb[:], in_=alog_h.ap()),
            sy.dma_start(out=dsb[:], in_=d_h.ap()),
            sy.dma_start(out=c_sel[:], in_=sel_h.ap())], n=3)
        ldbc = em.gp_dma(lambda g: [
            g.dma_start(out=bfs[:], in_=bf_h.ap()),
            g.dma_start(out=cfs[:], in_=cf_h.ap())], n=2)
        a1 = em.s(lambda sc: sc.activation(out=asb[:], in_=asb[:], func=AF.Exp),
                  deps=[lda])
        a2 = em.v(lambda v: v.tensor_scalar(out=asb[:], in0=asb[:], scalar1=-1.0,
                                            scalar2=None, op0=ALU.mult), deps=[a1])

        gdma = [None] * 4
        for dc in range(4):
            ldd = em.sync_dma(
                (lambda dc: lambda sy: [
                    sy.dma_start(out=deltab[:], in_=delta_h.ap()[dc]),
                    sy.dma_start(out=xsb[:], in_=xs_h.ap()[dc])])(dc),
                deps=[gdma[dc - 1] if dc >= 1 else None], n=2)
            du = em.v((lambda: lambda v: v.tensor_tensor(
                out=dub[:], in0=deltab[:], in1=xsb[:], op=ALU.mult))(),
                deps=[ldd])
            yinit = em.v((lambda dc: lambda v: v.tensor_scalar(
                out=yb[:], in0=xsb[:], scalar1=dsb[:, dc:dc + 1], scalar2=None,
                op0=ALU.mult))(dc), deps=[ldd, lda,
                                          gdma[dc - 1] if dc >= 1 else None])
            prev_acc = yinit
            prev_dbu = None
            prev_prod = None
            prev_scan = None
            for n in range(DSTATE):
                if dc == 0 and n == 0:
                    prev_psb = [None] * 2
                    prev_cread = {0: None, 1: None}
                    prev_dbu2 = {0: None, 1: None}

                bslot = n % 2
                ev_units = []
                for piece in range(16):
                    pb = psb[piece % 2]
                    mmb = em.t(
                        (lambda n, piece, pb: lambda t: [
                            t.matmul(pb[:, 0:512], c_sel[:, n * 128:(n + 1) * 128],
                                     bfs[:, piece * 512:(piece + 1) * 512],
                                     start=True, stop=True)])(n, piece, pb),
                        deps=[lda, ldbc, prev_psb[piece % 2],
                              prev_dbu2.get(bslot)])
                    evb = em.s(
                        (lambda piece, pb, bslot: lambda sc: sc.copy(
                            out=bbc[bslot][:, piece * 512:(piece + 1) * 512],
                            in_=pb[:, 0:512]))(piece, pb, bslot),
                        deps=[mmb])
                    prev_psb[piece % 2] = evb
                    ev_units.append(evb)
                ldb = ev_units[-1]
                da = em.s(
                    (lambda dc, n: lambda sc: sc.activation(
                        out=dab[:], in_=deltab[:], func=AF.Exp,
                        scale=asb[:, dc * 16 + n:dc * 16 + n + 1]))(dc, n),
                    deps=[ldd, a2, prev_scan])
                dbu = em.v(
                    (lambda bslot: lambda v: v.tensor_tensor(
                        out=dbub[:], in0=dub[:], in1=bbc[bslot][:],
                        op=ALU.mult))(bslot),
                    deps=[du, ldb, prev_scan])
                prev_dbu2[bslot] = dbu
                hsc = em.v(
                    lambda v: v.tensor_tensor_scan(out=hb[:], data0=dab[:],
                                                   data1=dbub[:], initial=0.0,
                                                   op0=ALU.mult, op1=ALU.add),
                    deps=[da, dbu, prev_prod])
                lastu = None
                for piece in range(16):
                    pc = psb[2 + piece % 2]
                    mmc = em.t(
                        (lambda n, piece, pc: lambda t: [
                            t.matmul(pc[:, 0:512], c_sel[:, n * 128:(n + 1) * 128],
                                     cfs[:, piece * 512:(piece + 1) * 512],
                                     start=True, stop=True)])(n, piece, pc),
                        deps=[lda, ldbc, prev_cread[piece % 2]])
                    pu = em.v(
                        (lambda piece, pc: lambda v: [
                            v.tensor_tensor(out=hb[:, piece * 512:(piece + 1) * 512],
                                            in0=hb[:, piece * 512:(piece + 1) * 512],
                                            in1=pc[:, 0:512], op=ALU.mult),
                            v.tensor_tensor(out=yb[:, piece * 512:(piece + 1) * 512],
                                            in0=yb[:, piece * 512:(piece + 1) * 512],
                                            in1=hb[:, piece * 512:(piece + 1) * 512],
                                            op=ALU.add)])(piece, pc),
                        deps=[mmc, hsc])
                    prev_cread[piece % 2] = pu
                    lastu = pu
                prod = lastu
                prev_acc = lastu
                prev_dbu = dbu
                prev_prod = prev_acc
                prev_scan = hsc
            gdma[dc] = em.sync_dma(
                (lambda dc: lambda sy: sy.dma_start(out=g_h.ap()[dc], in_=yb[:]))(dc),
                deps=[prev_acc])
        em.g(lambda g: g.engine_nop(), deps=gdma)
        em.realize(block, sems, dma_sems)
    return nc


def host_inputs_B(resA, inputs):
    delta_all = np.concatenate(
        [np.asarray(r["delta"]).reshape(DI, TCK) for r in resA], axis=1)
    xs_all = np.concatenate(
        [np.asarray(r["xs16"]).view(BF).reshape(DI, TCK) for r in resA], axis=1)
    res_all = np.concatenate(
        [np.asarray(r["res16"]).view(BF).reshape(DI, TCK) for r in resA], axis=1)
    bc_all = np.concatenate(
        [np.asarray(r["bc"]).reshape(32, TCK) for r in resA], axis=1)
    alog = np.asarray(inputs["L0_attn_A_log"])
    dvec = np.asarray(inputs["L0_attn_D"])
    sel16 = np.zeros((16, 16, 128), np.float32)
    for n_ in range(16):
        sel16[n_, n_, :] = 1.0
    sel16 = np.ascontiguousarray(sel16.transpose(1, 0, 2).reshape(16, 16 * 128)).astype(BF)
    maps = []
    for c in range(NC):
        sl = slice(c * 512, (c + 1) * 512)
        m = {
            "delta_s": np.ascontiguousarray(delta_all[sl]).reshape(4, 128, S),
            "xs_s": np.ascontiguousarray(xs_all[sl]).reshape(4, 128, S),
            "bfull": np.ascontiguousarray(bc_all[0:16]),
            "cfull": np.ascontiguousarray(bc_all[16:32]),
            "alog_s": np.ascontiguousarray(
                alog[sl].reshape(4, 128, 16).transpose(1, 0, 2).reshape(128, 64)),
            "sel16": sel16,
            "d_s": np.ascontiguousarray(dvec[sl].reshape(4, 128).T),
        }
        maps.append(m)
    return maps


def run_B(resA, inputs):
    nc = build_B()
    maps = host_inputs_B(resA, inputs)
    res = run_bass_kernel_spmd(nc, maps, core_ids=list(range(NC)))
    return res.results



def build_C1():
    nc = bass.Bass("TRN2", target_bir_lowering=False, debug=False)
    y_h = nc.dram_tensor("y16", [32, 128, TCK], BF16, kind="ExternalInput")
    res_h = nc.dram_tensor("res16", [32, 128, TCK], BF16, kind="ExternalInput")
    xT_h = nc.dram_tensor("xT", [16, 128, TCK], F32, kind="ExternalInput")
    wout_h = nc.dram_tensor("wout_t", [16, 32, 128, 128], F32, kind="ExternalInput")
    onesc_h = nc.dram_tensor("onesc", [128, 1], F32, kind="ExternalInput")
    onesr_h = nc.dram_tensor("onesr", [1, 128], F32, kind="ExternalInput")
    x1_h = nc.dram_tensor("x1", [16, 128, TCK], F32, kind="ExternalOutput")
    xn1_h = nc.dram_tensor("xn1", [16, 128, TCK], BF16, kind="ExternalOutput")

    st = ExitStack()
    with st:
        def SB(name, shape, dt):
            return st.enter_context(nc.sbuf_tensor(name, shape, dt))

        def PSUM(name, shape, dt=F32):
            return st.enter_context(nc.psum_tensor(name, shape, dt))
        c_onesc = SB("c_onesc", [128, 1], F32)
        c_onesr = SB("c_onesr", [1, 128], F32)
        ybuf = [SB("ybuf0", [128, TCK], BF16), SB("ybuf1", [128, TCK], BF16)]
        rbuf = [SB("rbuf0", [128, TCK], BF16), SB("rbuf1", [128, TCK], BF16)]
        gated = SB("gated", [128, 32, TCK], BF16)
        xs_ = SB("x1pre", [128, 16, TCK], F32)
        wbuf = [SB("wbuf0", [128, 32 * 128], BF16), SB("wbuf1", [128, 32 * 128], BF16)]
        sq = SB("sq", [128, TCK], F32)
        rows = SB("rows", [1, 4 * TCK], F32)
        mean_b = SB("mean_b", [128, TCK], F32)
        rstd_b = SB("rstd_b", [128, TCK], F32)
        xn1b = SB("xn1b", [128, TCK], BF16)
        psmm = Slots([PSUM(f"ps_m{i}", [128, 512]) for i in range(4)])
        ps_st = [PSUM(f"ps_s{i}", [1, 512]) for i in range(4)]
        sems = {e: st.enter_context(nc.semaphore("sem_" + e)) for e in Emitter.ENGS}
        dma_sems = [st.enter_context(nc.semaphore(f"dsem{i}")) for i in range(Emitter.NPOOL)]
        block = st.enter_context(nc.Block())
        em = Emitter(nc)

        consts = em.sync_dma(lambda sy: [
            sy.dma_start(out=c_onesc[:], in_=onesc_h.ap()),
            sy.dma_start(out=c_onesr[:], in_=onesr_h.ap())], n=2)
        ldx = em.sync_dma(lambda sy: sy.dma_start(
            out=xs_[:], in_=xT_h.ap().rearrange("kc p t -> p kc t")))

        gat = [None] * 32
        for dc in range(32):
            ld = em.sync_dma(
                (lambda dc: lambda sy: [
                    sy.dma_start(out=ybuf[dc % 2][:], in_=y_h.ap()[dc]),
                    sy.dma_start(out=rbuf[dc % 2][:], in_=res_h.ap()[dc])])(dc),
                deps=[gat[dc - 2] if dc >= 2 else None], n=2)
            gat[dc] = em.v(
                (lambda dc: lambda v: v.tensor_tensor(
                    out=gated[:, dc, :], in0=ybuf[dc % 2][:], in1=rbuf[dc % 2][:],
                    op=ALU.mult))(dc),
                deps=[ld])

        ldw = [None] * 16
        stat_units = []
        ev_units = []
        for mc in range(16):
            ldw[mc] = em.gp_dma(
                (lambda mc: lambda g: g.dma_start(
                    out=wbuf[mc % 2][:].rearrange("p (kc j) -> p kc j", j=128),
                    in_=wout_h.ap()[mc].rearrange("kc p j -> p kc j")))(mc),
                deps=[ev_units[-3] if mc >= 2 else None])
            for i, (o, w) in enumerate(NCP_TC):
                s_ = psmm.next()
                mm = em.t(
                    (lambda mc, o, w, s_: lambda t: [
                        t.matmul(psmm.tile(s_)[:, 0:w],
                                 wbuf[mc % 2][:, kc * 128:(kc + 1) * 128],
                                 gated[:, kc, o:o + w],
                                 start=(kc == 0), stop=(kc == 31))
                        for kc in range(32)])(mc, o, w, s_),
                    deps=[ldw[mc], psmm.reader[s_]] + ([gat[31]] if mc == 0 else []))
                ev = em.v(
                    (lambda mc, o, w, s_: lambda v: v.scalar_tensor_tensor(
                        out=xs_[:, mc, o:o + w], in0=xs_[:, mc, o:o + w],
                        scalar=2.0, in1=psmm.tile(s_)[:, 0:w],
                        op0=ALU.mult, op1=ALU.add))(mc, o, w, s_),
                    deps=[mm, ldx])
                psmm.reader[s_] = ev
                ev_units.append(ev)
            sqd = em.s(
                (lambda mc: lambda sc: sc.activation(
                    out=sq[:], in_=xs_[:, mc, :], func=AF.Square))(mc),
                deps=[ev_units[-1], ev_units[-2],
                      stat_units[-1] if mc >= 1 else None])
            stat_units.append(em.t(
                (lambda mc: lambda t: [
                    t.matmul(ps_st[0][:, 0:512], c_onesc[:], xs_[:, mc, 0:512],
                             start=(mc == 0), stop=(mc == 15)),
                    t.matmul(ps_st[1][:, 0:512], c_onesc[:], xs_[:, mc, 512:1024],
                             start=(mc == 0), stop=(mc == 15)),
                    t.matmul(ps_st[2][:, 0:512], c_onesc[:], sq[:, 0:512],
                             start=(mc == 0), stop=(mc == 15)),
                    t.matmul(ps_st[3][:, 0:512], c_onesc[:], sq[:, 512:1024],
                             start=(mc == 0), stop=(mc == 15))])(mc),
                deps=[sqd, consts]))

        rrc = em.s(lambda sc: [
            sc.copy(out=rows[:, 0:512], in_=ps_st[0][:, 0:512]),
            sc.copy(out=rows[:, 512:1024], in_=ps_st[1][:, 0:512]),
            sc.copy(out=rows[:, 1024:1536], in_=ps_st[2][:, 0:512]),
            sc.copy(out=rows[:, 1536:2048], in_=ps_st[3][:, 0:512])],
            deps=[stat_units[15]])
        rr = em.v(lambda v: [
            v.tensor_scalar(out=rows[:, 2048:3072], in0=rows[:, 0:1024],
                            scalar1=1.0 / DM, scalar2=None, op0=ALU.mult),
            v.tensor_scalar(out=rows[:, 1024:2048], in0=rows[:, 1024:2048],
                            scalar1=1.0 / DM, scalar2=None, op0=ALU.mult),
            v.tensor_tensor(out=rows[:, 0:1024], in0=rows[:, 2048:3072],
                            in1=rows[:, 2048:3072], op=ALU.mult),
            v.tensor_tensor(out=rows[:, 3072:4096], in0=rows[:, 1024:2048],
                            in1=rows[:, 0:1024], op=ALU.subtract),
            v.tensor_scalar(out=rows[:, 3072:4096], in0=rows[:, 3072:4096],
                            scalar1=EPS5, scalar2=None, op0=ALU.add)],
            deps=[rrc])
        r2 = em.s(lambda sc: sc.activation(out=rows[:, 3072:4096],
                                           in_=rows[:, 3072:4096], func=AF.Sqrt),
                  deps=[rr])
        r3 = em.v(lambda v: v.reciprocal(out=rows[:, 3072:4096],
                                         in_=rows[:, 3072:4096]), deps=[r2])
        bcs = [psmm.next() for _ in range(4)]
        r4 = em.t(lambda t: [
            t.matmul(psmm.tile(bcs[0])[:, 0:512], c_onesr[:], rows[:, 2048:2560],
                     start=True, stop=True),
            t.matmul(psmm.tile(bcs[1])[:, 0:512], c_onesr[:], rows[:, 2560:3072],
                     start=True, stop=True),
            t.matmul(psmm.tile(bcs[2])[:, 0:512], c_onesr[:], rows[:, 3072:3584],
                     start=True, stop=True),
            t.matmul(psmm.tile(bcs[3])[:, 0:512], c_onesr[:], rows[:, 3584:4096],
                     start=True, stop=True)],
            deps=[r3] + [psmm.reader[s_] for s_ in bcs])
        r5 = em.s(lambda sc: [
            sc.copy(out=mean_b[:, 0:512], in_=psmm.tile(bcs[0])[:, 0:512]),
            sc.copy(out=mean_b[:, 512:1024], in_=psmm.tile(bcs[1])[:, 0:512]),
            sc.copy(out=rstd_b[:, 0:512], in_=psmm.tile(bcs[2])[:, 0:512]),
            sc.copy(out=rstd_b[:, 512:1024], in_=psmm.tile(bcs[3])[:, 0:512])],
            deps=[r4])
        for s_ in bcs:
            psmm.reader[s_] = r5

        outs = []
        for mc in range(16):
            ap1 = em.v(
                (lambda mc: lambda v: [
                    v.tensor_tensor(out=xs_[:, mc, :], in0=xs_[:, mc, :],
                                    in1=mean_b[:], op=ALU.subtract),
                    v.tensor_tensor(out=xs_[:, mc, :], in0=xs_[:, mc, :],
                                    in1=rstd_b[:], op=ALU.mult)])(mc),
                deps=[r5, stat_units[15]])
            cp = em.s(
                (lambda mc: lambda sc: sc.copy(out=xn1b[:], in_=xs_[:, mc, :]))(mc),
                deps=[ap1, outs[-1] if mc >= 1 else None])
            o1 = em.sync_dma(
                (lambda mc: lambda sy: [
                    sy.dma_start(out=x1_h.ap()[mc], in_=xs_[:, mc, :]),
                    sy.dma_start(out=xn1_h.ap()[mc], in_=xn1b[:])])(mc),
                deps=[cp], n=2)
            outs.append(o1)
        em.g(lambda g: g.engine_nop(), deps=outs)
        em.realize(block, sems, dma_sems)
    return nc


def host_inputs_C1(resA, resB, inputs):
    y_all = np.concatenate(
        [np.asarray(r["y16"]).view(BF).reshape(512, S) for r in resB], axis=0)
    x = np.asarray(inputs["L0_attn_x"])[0]
    xT = np.ascontiguousarray(x.T)
    wout = lhsT_tiles(np.asarray(inputs["L0_attn_W_out"]))
    maps = []
    for c in range(NC):
        cols = slice(c * TCK, (c + 1) * TCK)
        maps.append({
            "y16": np.ascontiguousarray(y_all[:, cols]).reshape(32, 128, TCK),
            "res16": np.asarray(resA[c]["res16"]).view(BF).reshape(32, 128, TCK),
            "xT": np.ascontiguousarray(xT[:, cols]).reshape(16, 128, TCK),
            "wout_t": wout,
            "onesc": np.ones((128, 1), np.float32),
            "onesr": np.ones((1, 128), np.float32),
        })
    return maps


def run_C1(resA, resB, inputs):
    nc = build_C1()
    maps = host_inputs_C1(resA, resB, inputs)
    res = run_bass_kernel_spmd(nc, maps, core_ids=list(range(NC)))
    return res.results



def build_C2():
    nc = bass.Bass("TRN2", target_bir_lowering=False, debug=False)
    xn1_h = nc.dram_tensor("xn1", [16, 128, TCK], BF16, kind="ExternalInput")
    x1_h = nc.dram_tensor("x1", [16, 128, TCK], F32, kind="ExternalInput")
    wg_h = nc.dram_tensor("wg", [DM, 8192], F32, kind="ExternalInput")
    wu_h = nc.dram_tensor("wu", [DM, 8192], F32, kind="ExternalInput")
    wdn_h = nc.dram_tensor("wdn_t", [16, 64, 128, 128], F32, kind="ExternalInput")
    id_h = nc.dram_tensor("ident16", [128, 128], BF16, kind="ExternalInput")
    onesc_h = nc.dram_tensor("onesc", [128, 1], F32, kind="ExternalInput")
    onesr_h = nc.dram_tensor("onesr", [1, 128], F32, kind="ExternalInput")
    x2_h = nc.dram_tensor("x2", [16, 128, TCK], F32, kind="ExternalOutput")

    TT = 512
    st = ExitStack()
    with st:
        def SB(name, shape, dt):
            return st.enter_context(nc.sbuf_tensor(name, shape, dt))

        def PSUM(name, shape, dt=F32):
            return st.enter_context(nc.psum_tensor(name, shape, dt))
        c_id = SB("c_id", [128, 128], BF16)
        c_onesc = SB("c_onesc", [128, 1], F32)
        c_onesr = SB("c_onesr", [1, 128], F32)
        xn1 = SB("xn1s", [128, 16, TCK], BF16)
        hT = SB("hT", [128, 64, TT], BF16)
        wgb = SB("wgb", [128, 16, 512], BF16)
        wub = SB("wub", [128, 16, 512], BF16)
        wdb = [SB("wdb0", [128, 64, 128], BF16)]
        gs = SB("gs", [128, 4, 512], BF16)
        habs = SB("habs", [128, 2048], BF16)
        hsp = habs
        q = SB("q", [128, 4, 512], BF16)
        msk = SB("msk", [128, 2048], U8)
        zero16 = SB("zero16", [128, 512], BF16)
        x1h = SB("x1h", [128, 16, TT], F32)
        sq = SB("sq", [128, TT], F32)
        rows = SB("rows", [1, 4 * TT], F32)
        mean_b = SB("mean_b", [128, TT], F32)
        rstd_b = SB("rstd_b", [128, TT], F32)
        psg = Slots([PSUM(f"psg{i}", [128, 512]) for i in range(4)])
        ps_tr = [PSUM("pstr0", [128, 1024], BF16), PSUM("pstr1", [128, 1024], BF16)]
        ps_st = [PSUM("ps_su", [1, 512]), PSUM("ps_sq", [1, 512])]
        sems = {e: st.enter_context(nc.semaphore("sem_" + e)) for e in Emitter.ENGS}
        dma_sems = [st.enter_context(nc.semaphore(f"dsem{i}")) for i in range(Emitter.NPOOL)]
        block = st.enter_context(nc.Block())
        em = Emitter(nc)

        consts = em.sync_dma(lambda sy: [
            sy.dma_start(out=c_id[:], in_=id_h.ap()),
            sy.dma_start(out=c_onesc[:], in_=onesc_h.ap()),
            sy.dma_start(out=c_onesr[:], in_=onesr_h.ap())], n=3)
        ldxn = em.sync_dma(lambda sy: sy.dma_start(
            out=xn1[:], in_=xn1_h.ap().rearrange("kc p t -> p kc t")))
        z0 = em.v(lambda v: v.memset(zero16[:], 0.0))

        x2outs = []
        prev_ldw = {}
        prev_half = {}
        for th in range(2):
            tsl = slice(th * TT, (th + 1) * TT)
            ldx1 = em.sync_dma(
                (lambda th: lambda sy: sy.dma_start(
                    out=x1h[:],
                    in_=x1_h.ap().rearrange("kc p t -> p kc t")[:, :, th * TT:(th + 1) * TT]))(th),
                deps=prev_half.get('x2dma_all', []))
            tr_units = []
            for ncc in range(16):
                ldwg = em.gp_dma(
                    (lambda ncc: lambda g: g.dma_start(
                        out=wgb[:],
                        in_=wg_h.ap().rearrange("(kc p) n -> p kc n", p=128)
                        [:, :, ncc * 512:(ncc + 1) * 512]))(ncc),
                    deps=[prev_ldw.get('g')])
                gmm = []
                for tc in range(4):
                    s_ = psg.next()
                    mm = em.t(
                        (lambda ncc, tc, th, s_: lambda t: [
                            t.matmul(psg.tile(s_)[:, 0:512],
                                     xn1[:, kc, th * TT + tc * 128: th * TT + (tc + 1) * 128],
                                     wgb[:, kc, :],
                                     start=(kc == 0), stop=(kc == 15))
                            for kc in range(16)])(ncc, tc, th, s_),
                        deps=[ldwg, ldxn, psg.reader[s_]])
                    gmm.append((mm, s_))
                gevac = em.s(
                    (lambda gmm: lambda sc: [
                        sc.activation(out=gs[:, tc, :], in_=psg.tile(s_)[:, 0:512],
                                      func=AF.Silu)
                        for tc, (mm, s_) in enumerate(gmm)])(gmm),
                    deps=[mm for mm, _ in gmm])
                for _, s_ in gmm:
                    psg.reader[s_] = gevac
                prev_ldw['g'] = gmm[-1][0]
                ldwu = em.gp_dma(
                    (lambda ncc: lambda g: g.dma_start(
                        out=wub[:],
                        in_=wu_h.ap().rearrange("(kc p) n -> p kc n", p=128)
                        [:, :, ncc * 512:(ncc + 1) * 512]))(ncc),
                    deps=[prev_ldw.get('u')])
                umm = []
                for tc in range(4):
                    s_ = psg.next()
                    mm = em.t(
                        (lambda ncc, tc, th, s_: lambda t: [
                            t.matmul(psg.tile(s_)[:, 0:512],
                                     xn1[:, kc, th * TT + tc * 128: th * TT + (tc + 1) * 128],
                                     wub[:, kc, :],
                                     start=(kc == 0), stop=(kc == 15))
                            for kc in range(16)])(ncc, tc, th, s_),
                        deps=[ldwu, ldxn, psg.reader[s_]])
                    umm.append((mm, s_))
                prev_ldw['u'] = umm[-1][0]
                hvs = []
                for tc in range(4):
                    mm, s_ = umm[tc]
                    hv = em.v(
                        (lambda tc, s_: lambda v: v.tensor_tensor(
                            out=gs[:, tc, :], in0=gs[:, tc, :],
                            in1=psg.tile(s_)[:, 0:512], op=ALU.mult))(tc, s_),
                        deps=[mm, gevac])
                    psg.reader[s_] = hv
                    hvs.append(hv)
                gsf = gs[:].rearrange("p tc t -> p (tc t)")
                ab = em.s(
                    lambda sc: sc.activation(out=habs[:], in_=gsf, func=AF.Abs),
                    deps=hvs + [tr_units[-1] if tr_units else None])
                h4 = habs[:].rearrange("p (g4 j) -> p g4 j", j=4)
                spar = em.v(
                    (lambda h4, gsf: lambda v: [
                        v.tensor_tensor(out=q[:, 0, :], in0=h4[:, :, 0], in1=h4[:, :, 1], op=ALU.max),
                        v.tensor_tensor(out=q[:, 1, :], in0=h4[:, :, 0], in1=h4[:, :, 1], op=ALU.min),
                        v.tensor_tensor(out=q[:, 2, :], in0=h4[:, :, 2], in1=h4[:, :, 3], op=ALU.max),
                        v.tensor_tensor(out=q[:, 3, :], in0=h4[:, :, 2], in1=h4[:, :, 3], op=ALU.min),
                        v.tensor_tensor(out=q[:, 0, :], in0=q[:, 0, :], in1=q[:, 2, :], op=ALU.min),
                        v.tensor_tensor(out=q[:, 1, :], in0=q[:, 1, :], in1=q[:, 3, :], op=ALU.max),
                        v.tensor_tensor(out=q[:, 2, :], in0=q[:, 0, :], in1=q[:, 1, :], op=ALU.max),
                        v.tensor_tensor(out=msk[:].rearrange("p (g4 j) -> p g4 j", j=4),
                                        in0=h4,
                                        in1=q[:, 2, :].broadcast_to([128, 512, 4]),
                                        op=ALU.is_ge),
                        v.select(out=hsp[:].rearrange("p (g4 j) -> p g4 j", j=4),
                                 mask=msk[:].rearrange("p (g4 j) -> p g4 j", j=4),
                                 on_true=gsf.rearrange("p (g4 j) -> p g4 j", j=4),
                                 on_false=zero16[:].broadcast_to([128, 512, 4]))])(h4, gsf),
                    deps=[ab, z0])
                for half16 in range(2):
                    trs = half16
                    trm = em.t(
                        (lambda half16, trs: lambda t: [
                            t.transpose(ps_tr[trs][:, gg * 128:(gg + 1) * 128],
                                        hsp[:, half16 * 1024 + gg * 128:
                                            half16 * 1024 + (gg + 1) * 128], c_id[:])
                            for gg in range(8)])(half16, trs),
                        deps=[spar, consts,
                              tr_units[-2] if len(tr_units) >= 2 else None])
                    trev = em.v(
                        (lambda ncc, half16, trs: lambda v: [
                            v.tensor_copy(
                                out=hT[:, ncc * 4 + ((half16 * 1024 + gg * 128) % 512) // 128,
                                       ((half16 * 1024 + gg * 128) // 512) * 128:
                                       ((half16 * 1024 + gg * 128) // 512) * 128 + 128],
                                in_=ps_tr[trs][:, gg * 128:(gg + 1) * 128])
                            for gg in range(8)])(ncc, half16, trs),
                        deps=[trm, prev_half.get('down_last')])
                    tr_units.append(trev)
            stat_units = []
            ev_units = []
            for mc in range(16):
                ldwd = em.gp_dma(
                    (lambda mc: lambda g: g.dma_start(
                        out=wdb[0][:],
                        in_=wdn_h.ap()[mc].rearrange("kc p j -> p kc j")))(mc),
                    deps=[prev_ldw.get('d')])
                for i in range(1):
                    s_ = psg.next()
                    mm = em.t(
                        (lambda mc, s_: lambda t: [
                            t.matmul(psg.tile(s_)[:, 0:512], wdb[0][:, kc, :],
                                     hT[:, kc, :],
                                     start=(kc == 0), stop=(kc == 63))
                            for kc in range(64)])(mc, s_),
                        deps=[ldwd, psg.reader[s_]] + tr_units[-2:])
                    prev_ldw['d'] = mm
                    ev = em.v(
                        (lambda mc, s_: lambda v: v.tensor_tensor(
                            out=x1h[:, mc, :], in0=x1h[:, mc, :],
                            in1=psg.tile(s_)[:, 0:512], op=ALU.add))(mc, s_),
                        deps=[mm, ldx1])
                    psg.reader[s_] = ev
                    ev_units.append(ev)
                sqd = em.s(
                    (lambda mc: lambda sc: sc.activation(
                        out=sq[:], in_=x1h[:, mc, :], func=AF.Square))(mc),
                    deps=[ev_units[-1], stat_units[-1] if stat_units else None])
                stat_units.append(em.t(
                    (lambda mc: lambda t: [
                        t.matmul(ps_st[0][:, 0:512], c_onesc[:], x1h[:, mc, :],
                                 start=(mc == 0), stop=(mc == 15)),
                        t.matmul(ps_st[1][:, 0:512], c_onesc[:], sq[:],
                                 start=(mc == 0), stop=(mc == 15))])(mc),
                    deps=[sqd, consts]))
            rrc = em.s(
                lambda sc: [
                    sc.copy(out=rows[:, 0:512], in_=ps_st[0][:, 0:512]),
                    sc.copy(out=rows[:, 512:1024], in_=ps_st[1][:, 0:512])],
                deps=[stat_units[15]])
            rr = em.v(
                lambda v: [
                    v.tensor_scalar(out=rows[:, 1024:1536], in0=rows[:, 0:512],
                                    scalar1=1.0 / DM, scalar2=None, op0=ALU.mult),
                    v.tensor_scalar(out=rows[:, 512:1024], in0=rows[:, 512:1024],
                                    scalar1=1.0 / DM, scalar2=None, op0=ALU.mult),
                    v.tensor_tensor(out=rows[:, 0:512], in0=rows[:, 1024:1536],
                                    in1=rows[:, 1024:1536], op=ALU.mult),
                    v.tensor_tensor(out=rows[:, 1536:2048], in0=rows[:, 512:1024],
                                    in1=rows[:, 0:512], op=ALU.subtract),
                    v.tensor_scalar(out=rows[:, 1536:2048], in0=rows[:, 1536:2048],
                                    scalar1=EPS5, scalar2=None, op0=ALU.add)],
                deps=[rrc])
            r2 = em.s(lambda sc: sc.activation(out=rows[:, 1536:2048],
                                               in_=rows[:, 1536:2048], func=AF.Sqrt),
                      deps=[rr])
            r3 = em.v(lambda v: v.reciprocal(out=rows[:, 1536:2048],
                                             in_=rows[:, 1536:2048]), deps=[r2])
            bcs = [psg.next(), psg.next()]
            r4 = em.t(lambda t: [
                t.matmul(psg.tile(bcs[0])[:, 0:512], c_onesr[:], rows[:, 1024:1536],
                         start=True, stop=True),
                t.matmul(psg.tile(bcs[1])[:, 0:512], c_onesr[:], rows[:, 1536:2048],
                         start=True, stop=True)],
                deps=[r3] + [psg.reader[s_] for s_ in bcs])
            r5 = em.s(lambda sc: [
                sc.copy(out=mean_b[:], in_=psg.tile(bcs[0])[:, 0:512]),
                sc.copy(out=rstd_b[:], in_=psg.tile(bcs[1])[:, 0:512])],
                deps=[r4])
            for s_ in bcs:
                psg.reader[s_] = r5
            outs_th = []
            for mc in range(16):
                ap1 = em.v(
                    (lambda mc: lambda v: [
                        v.tensor_tensor(out=x1h[:, mc, :], in0=x1h[:, mc, :],
                                        in1=mean_b[:], op=ALU.subtract),
                        v.tensor_tensor(out=x1h[:, mc, :], in0=x1h[:, mc, :],
                                        in1=rstd_b[:], op=ALU.mult)])(mc),
                    deps=[r5])
                o1 = em.sync_dma(
                    (lambda mc, th: lambda sy: sy.dma_start(
                        out=x2_h.ap()[mc][:, th * TT:(th + 1) * TT],
                        in_=x1h[:, mc, :]))(mc, th),
                    deps=[ap1])
                outs_th.append(o1)
            x2outs += outs_th
            prev_half['x2dma_all'] = list(outs_th)
            prev_half['down_last'] = prev_ldw.get('d')
        em.g(lambda g: g.engine_nop(), deps=x2outs)
        em.realize(block, sems, dma_sems)
    return nc


def host_inputs_C2(resC1, inputs):
    wdn = lhsT_tiles(np.asarray(inputs["L0_ffn_w_down"]))
    ident = np.eye(128, dtype=np.float32).astype(BF)
    maps = []
    for c in range(NC):
        maps.append({
            "xn1": np.asarray(resC1[c]["xn1"]).view(BF).reshape(16, 128, TCK),
            "x1": np.asarray(resC1[c]["x1"]).reshape(16, 128, TCK),
            "wg": np.asarray(inputs["L0_ffn_w_gate"]),
            "wu": np.asarray(inputs["L0_ffn_w_up"]),
            "wdn_t": wdn,
            "ident16": ident,
            "onesc": np.ones((128, 1), np.float32),
            "onesr": np.ones((1, 128), np.float32),
        })
    return maps


def run_C2(resC1, inputs):
    nc = build_C2()
    maps = host_inputs_C2(resC1, inputs)
    res = run_bass_kernel_spmd(nc, maps, core_ids=list(range(NC)))
    return res.results



def build_D():
    nc = bass.Bass("TRN2", target_bir_lowering=False, debug=False)
    x2_h = nc.dram_tensor("x2T", [16, 128, TH], F32, kind="ExternalInput")
    wl_h = nc.dram_tensor("wl_t", [81, 16, 128, 128], F32, kind="ExternalInput")
    cwt_h = nc.dram_tensor("cwt2", [128, 192], F32, kind="ExternalInput")
    cb_h = nc.dram_tensor("cb2", [128, 48], F32, kind="ExternalInput")
    dtb_h = nc.dram_tensor("dtb", [64, 1], F32, kind="ExternalInput")
    alog_h = nc.dram_tensor("alog2", [64, 1], F32, kind="ExternalInput")

    gate_h = nc.dram_tensor("gate16", [32, 128, TCK], BF16, kind="ExternalOutput")
    xbc_h = nc.dram_tensor("xbc16", [48, 128, TCK], BF16, kind="ExternalOutput")
    dt2_h = nc.dram_tensor("dt2", [64, TCK], F32, kind="ExternalOutput")
    a2_h = nc.dram_tensor("a2", [64, TCK], F32, kind="ExternalOutput")

    st = ExitStack()
    with st:
        def SB(name, shape, dt):
            return st.enter_context(nc.sbuf_tensor(name, shape, dt))

        def PSUM(name, shape, dt=F32):
            return st.enter_context(nc.psum_tensor(name, shape, dt))
        c_cwt = SB("c_cwt", [128, 192], F32)
        c_cb = SB("c_cb", [128, 48], F32)
        c_dtb = SB("c_dtb", [64, 1], F32)
        c_al = SB("c_al", [64, 1], F32)
        x2n = SB("x2n", [128, 16, TH], BF16)
        wbuf = [SB("wbuf0", [128, 2048], BF16), SB("wbuf1", [128, 2048], BF16)]
        xraw = SB("xraw", [128, TH], F32)
        cacc = SB("cacc", [128, TCK], F32)
        ob = [SB("ob0", [128, TCK], BF16), SB("ob1", [128, TCK], BF16)]
        sp1 = SB("sp1", [128, TCK], F32)
        sp2 = SB("sp2", [128, TCK], F32)
        dtb_ = SB("dtbuf", [128, TCK], F32)
        a2b = SB("a2buf", [128, TCK], F32)
        psmm = Slots([PSUM(f"ps{i}", [128, 512]) for i in range(4)])
        sems = {e: st.enter_context(nc.semaphore("sem_" + e)) for e in Emitter.ENGS}
        dma_sems = [st.enter_context(nc.semaphore(f"dsem{i}")) for i in range(Emitter.NPOOL)]
        block = st.enter_context(nc.Block())
        em = Emitter(nc)

        consts = em.sync_dma(lambda sy: [
            sy.dma_start(out=c_cwt[:], in_=cwt_h.ap()),
            sy.dma_start(out=c_cb[:], in_=cb_h.ap()),
            sy.dma_start(out=c_dtb[:], in_=dtb_h.ap()),
            sy.dma_start(out=c_al[:], in_=alog_h.ap())], n=4)
        ldx = em.gp_dma(lambda g: g.dma_start(
            out=x2n[:], in_=x2_h.ap().rearrange("kc p t -> p kc t")))
        a1u = em.s(lambda sc: sc.activation(out=c_al[:], in_=c_al[:], func=AF.Exp),
                   deps=[consts])
        a2u = em.v(lambda v: v.tensor_scalar(out=c_al[:], in0=c_al[:], scalar1=-1.0,
                                             scalar2=None, op0=ALU.mult), deps=[a1u])

        ldw = [None] * 81
        out_units = []
        prev_gate_dma = {}
        prev_conv = [None] * 81
        last_wbuf_mm = [None, None]
        for mc in range(81):
            ldw[mc] = em.gp_dma(
                (lambda mc: lambda g: g.dma_start(
                    out=wbuf[mc % 2][:].rearrange("p (kc j) -> p kc j", j=128),
                    in_=wl_h.ap()[mc].rearrange("kc p j -> p kc j")))(mc),
                deps=[last_wbuf_mm[mc % 2]])
            evacs = []
            mm = None
            for i, (o, w) in enumerate(NCP_TH):
                s_ = psmm.next()
                mm = em.t(
                    (lambda mc, o, w, s_: lambda t: [
                        t.matmul(psmm.tile(s_)[:, 0:w],
                                 wbuf[mc % 2][:, kc * 128:(kc + 1) * 128],
                                 x2n[:, kc, o:o + w],
                                 start=(kc == 0), stop=(kc == 15))
                        for kc in range(16)])(mc, o, w, s_),
                    deps=[ldw[mc], ldx, psmm.reader[s_]])
                if mc < 32:
                    def gfn(o, w, s_, mc):
                        def fn(sc):
                            if o == 0:
                                return sc.activation(out=ob[mc % 2][:, 0:w - HALO],
                                                     in_=psmm.tile(s_)[:, HALO:w],
                                                     func=AF.Silu)
                            return sc.activation(out=ob[mc % 2][:, o - HALO:o - HALO + w],
                                                 in_=psmm.tile(s_)[:, 0:w], func=AF.Silu)
                        return fn
                    ev = em.s(gfn(o, w, s_, mc),
                              deps=[mm, prev_gate_dma.get(mc % 2)])
                elif mc < 80:
                    ev = em.s(
                        (lambda o, w, s_: lambda sc: sc.copy(
                            out=xraw[:, o:o + w], in_=psmm.tile(s_)[:, 0:w]))(o, w, s_),
                        deps=[mm, prev_conv[mc - 1] if i == 0 else None])
                else:
                    def dfn(o, w, s_):
                        def fn(sc):
                            if o == 0:
                                src = psmm.tile(s_)[0:64, HALO:w]
                                dsl = slice(0, w - HALO)
                            else:
                                src = psmm.tile(s_)[0:64, 0:w]
                                dsl = slice(o - HALO, o - HALO + w)
                            sc.activation(out=sp1[0:64, dsl], in_=src,
                                          func=AF.Abs, bias=c_dtb[:, 0:1])
                            sc.activation(out=sp1[0:64, dsl], in_=sp1[0:64, dsl],
                                          func=AF.Exp, scale=-1.0)
                            sc.activation(out=sp1[0:64, dsl], in_=sp1[0:64, dsl],
                                          func=AF.Ln, bias=1.0)
                            return sc.activation(out=sp2[0:64, dsl], in_=src,
                                                 func=AF.Relu, bias=c_dtb[:, 0:1])
                        return fn
                    ev = em.s(dfn(o, w, s_), deps=[mm])
                psmm.reader[s_] = ev
                evacs.append(ev)
            last_wbuf_mm[mc % 2] = mm
            if mc < 32:
                d1 = em.sync_dma(
                    (lambda mc: lambda sy: sy.dma_start(
                        out=gate_h.ap()[mc], in_=ob[mc % 2][:]))(mc),
                    deps=evacs)
                prev_gate_dma[mc % 2] = d1
                out_units.append(d1)
                prev_conv[mc] = evacs[-1]
            elif mc < 80:
                dc2 = mc - 32
                cd = em.v(
                    (lambda dc2: lambda v: [
                        v.tensor_scalar(out=cacc[:], in0=xraw[:, 1:1 + TCK],
                                        scalar1=c_cwt[:, dc2 * 4:dc2 * 4 + 1],
                                        scalar2=c_cb[:, dc2:dc2 + 1],
                                        op0=ALU.mult, op1=ALU.add),
                        *[v.scalar_tensor_tensor(
                            out=cacc[:], in0=xraw[:, 1 + wt:1 + wt + TCK],
                            scalar=c_cwt[:, dc2 * 4 + wt:dc2 * 4 + wt + 1],
                            in1=cacc[:], op0=ALU.mult, op1=ALU.add)
                          for wt in range(1, 4)]])(dc2),
                    deps=evacs + [prev_conv[mc - 1], consts])
                sil = em.s(
                    (lambda mc: lambda sc: sc.activation(
                        out=ob[mc % 2][:], in_=cacc[:], func=AF.Silu))(mc),
                    deps=[cd, prev_gate_dma.get(mc % 2)])
                prev_conv[mc] = sil
                d1 = em.sync_dma(
                    (lambda mc: lambda sy: sy.dma_start(
                        out=xbc_h.ap()[mc - 32], in_=ob[mc % 2][:]))(mc),
                    deps=[sil])
                prev_gate_dma[mc % 2] = d1
                out_units.append(d1)
            else:
                add = em.v(lambda v: v.tensor_tensor(
                    out=dtb_[0:64, :], in0=sp1[0:64, :], in1=sp2[0:64, :],
                    op=ALU.add), deps=evacs)
                a2c = em.v(lambda v: v.tensor_scalar(
                    out=a2b[0:64, :], in0=dtb_[0:64, :], scalar1=c_al[:, 0:1],
                    scalar2=None, op0=ALU.mult), deps=[add, a2u])
                d1 = em.sync_dma(lambda sy: [
                    sy.dma_start(out=dt2_h.ap(), in_=dtb_[0:64, :]),
                    sy.dma_start(out=a2_h.ap(), in_=a2b[0:64, :])],
                    deps=[add, a2c], n=2)
                out_units.append(d1)
        em.g(lambda g: g.engine_nop(), deps=out_units)
        em.realize(block, sems, dma_sems)
    return nc


def host_inputs_D(resC2, inputs):
    x2_all = np.concatenate(
        [np.asarray(r["x2"]).reshape(DM, TCK) for r in resC2], axis=1)
    wdt_pad = np.zeros((DM, 128), np.float32)
    wdt_pad[:, :64] = np.asarray(inputs["L1_attn_Wdt"])
    wl = np.concatenate([np.asarray(inputs["L1_attn_Wz"]),
                         np.asarray(inputs["L1_attn_Wx"]),
                         np.asarray(inputs["L1_attn_WB"]),
                         np.asarray(inputs["L1_attn_WC"]),
                         wdt_pad], axis=1)
    wl_t = lhsT_tiles(wl)
    conv_w = np.asarray(inputs["L1_attn_conv_w"])
    cwt2 = np.ascontiguousarray(
        conv_w.reshape(48, 128, 4).transpose(1, 0, 2).reshape(128, 192))
    cb2 = percol(np.asarray(inputs["L1_attn_conv_b"]))
    x2slices = tok_slices(x2_all, HALO)
    maps = []
    for c in range(NC):
        maps.append({
            "x2T": x2slices[c].reshape(16, 128, TH),
            "wl_t": wl_t,
            "cwt2": cwt2,
            "cb2": cb2,
            "dtb": np.asarray(inputs["L1_attn_dt_bias"]).reshape(64, 1),
            "alog2": np.asarray(inputs["L1_attn_A_log"]).reshape(64, 1),
        })
    return maps


def run_D(resC2, inputs):
    nc = build_D()
    maps = host_inputs_D(resC2, inputs)
    res = run_bass_kernel_spmd(nc, maps, core_ids=list(range(NC)))
    return res.results



def build_E():
    nc = bass.Bass("TRN2", target_bir_lowering=False, debug=False)
    xg_h = nc.dram_tensor("xg16", [8, 64, S], BF16, kind="ExternalInput")
    bg_h = nc.dram_tensor("bg16", [128, S], BF16, kind="ExternalInput")
    cg_h = nc.dram_tensor("cg16", [128, S], BF16, kind="ExternalInput")
    ag_h = nc.dram_tensor("ag8", [8, S], F32, kind="ExternalInput")
    dt_h = nc.dram_tensor("dt8", [8, S], F32, kind="ExternalInput")
    sel_h = nc.dram_tensor("sel8", [8, 8 * 128], F32, kind="ExternalInput")
    i8_h = nc.dram_tensor("i8", [8, 8], F32, kind="ExternalInput")
    ones8_h = nc.dram_tensor("ones8", [8, 128], F32, kind="ExternalInput")
    tri_h = nc.dram_tensor("tri", [128, 128], F32, kind="ExternalInput")
    dgb_h = nc.dram_tensor("dgb", [64, 8], F32, kind="ExternalInput")
    id_h = nc.dram_tensor("ident16", [128, 128], BF16, kind="ExternalInput")
    yg_h = nc.dram_tensor("yg16", [8, 64, S], BF16, kind="ExternalOutput")

    Q = 128
    NQTR = 4
    TTE = S // NQTR
    NCH = TTE // Q
    st = ExitStack()
    with st:
        def SB(name, shape, dt):
            return st.enter_context(nc.sbuf_tensor(name, shape, dt))

        def PSUM(name, shape, dt=F32):
            return st.enter_context(nc.psum_tensor(name, shape, dt))
        xg = SB("xg", [64, 8, TTE], BF16)
        bg = SB("bg", [128, TTE], BF16)
        cg = SB("cg", [128, TTE], BF16)
        yg = SB("yg", [64, 8, TTE], BF16)
        ag8 = SB("ag8s", [8, TTE], F32)
        dt8 = SB("dt8s", [8, TTE], F32)
        zr8 = SB("zr8", [8, TTE], BF16)
        cumx = SB("cumx", [8, TTE + 1], F32)
        carry = SB("carry", [8, 1], F32)
        c_sel = SB("c_sel", [8, 8 * 128], F32)
        c_i8 = SB("c_i8", [8, 8], F32)
        c_on8 = SB("c_on8", [8, 128], F32)
        c_tri = SB("c_tri", [128, 128], F32)
        c_dgb = SB("c_dgb", [64, 8], F32)
        c_id = SB("c_id", [128, 128], BF16)
        diag8 = SB("diag8", [8, 16], F32)
        cds = SB("cds", [128, 24], F32)
        wcols = SB("wcols", [128, 8], F32)
        wca = SB("wca", [128, 8], F32)
        wcb = SB("wcb", [128, 8], F32)
        pcol = SB("pcol", [128, 1], F32)
        zb = SB("zb", [128, 8, Q], F32)
        z0b = SB("z0b", [128, 8, Q], F32)
        st16 = SB("st16", [128, 8 * Q], BF16)
        cl16 = SB("cl16", [128, 8 * Q], BF16)
        bt16 = SB("bt16", [128, 128], BF16)
        xt16 = SB("xt16", [128, 8, 64], BF16)
        bw16 = SB("bw16", [128, 8, 128], BF16)
        ht = SB("ht", [128, 512], F32)
        ht16 = SB("ht16", [128, 512], BF16)
        ps_tr = [PSUM("pstr0", [128, 512]), PSUM("pstr1", [128, 512])]
        ps_cb = [PSUM("pscb0", [128, 512]), PSUM("pscb1", [128, 512])]
        ps_yc = [PSUM("psyc0", [128, 512]), PSUM("psyc1", [128, 512])]
        ps_tp = [PSUM("pstp0", [128, 1024], BF16), PSUM("pstp1", [128, 1024], BF16)]
        sems = {e: st.enter_context(nc.semaphore("sem_" + e)) for e in Emitter.ENGS}
        dma_sems = [st.enter_context(nc.semaphore(f"dsem{i}")) for i in range(Emitter.NPOOL)]
        block = st.enter_context(nc.Block())
        em = Emitter(nc)

        consts = em.sync_dma(lambda sy: [
            sy.dma_start(out=c_sel[:], in_=sel_h.ap()),
            sy.dma_start(out=c_i8[:], in_=i8_h.ap()),
            sy.dma_start(out=c_on8[:], in_=ones8_h.ap()),
            sy.dma_start(out=c_tri[:], in_=tri_h.ap()),
            sy.dma_start(out=c_dgb[:], in_=dgb_h.ap()),
            sy.dma_start(out=c_id[:], in_=id_h.ap()),
        ], n=6)
        init = em.v(lambda v: [
            v.memset(zr8[:], 0.0),
            v.memset(carry[:], 0.0),
            v.memset(ht[:], 0.0)], deps=[])

        prev = {}
        gc = 0
        for h2 in range(NQTR):
            hsl = slice(h2 * TTE, (h2 + 1) * TTE)
            lds = em.sync_dma(
                (lambda h2: lambda sy: [
                    sy.dma_start(out=xg[:],
                                 in_=xg_h.ap().rearrange("hh p t -> p hh t")
                                 [:, :, h2 * TTE:(h2 + 1) * TTE]),
                    sy.dma_start(out=bg[:], in_=bg_h.ap()[:, h2 * TTE:(h2 + 1) * TTE]),
                    sy.dma_start(out=cg[:], in_=cg_h.ap()[:, h2 * TTE:(h2 + 1) * TTE]),
                    sy.dma_start(out=ag8[:], in_=ag_h.ap()[:, h2 * TTE:(h2 + 1) * TTE]),
                    sy.dma_start(out=dt8[:], in_=dt_h.ap()[:, h2 * TTE:(h2 + 1) * TTE]),
                ])(h2),
                deps=[prev.get('ydma'), prev.get('lastcompute')], n=5)
            cum = em.v(
                lambda v: [
                    v.tensor_copy(out=cumx[:, 0:1], in_=carry[:]),
                    v.tensor_tensor_scan(
                        out=cumx[:, 1:TTE + 1], data0=ag8[:], data1=zr8[:],
                        initial=carry[:, 0:1], op0=ALU.add, op1=ALU.add)],
                deps=[lds, init, prev.get('carry_upd')])
            ccp = em.v(lambda v: v.tensor_copy(out=carry[:], in_=cumx[:, TTE:TTE + 1]),
                       deps=[cum])
            prev['carry_upd'] = ccp

            for c in range(NCH):
                cs = c * Q
                tp = ps_tp[gc % 2]
                pcb = ps_cb[gc % 2]
                trm = em.t(
                    (lambda cs, tp: lambda t: [
                        t.transpose(tp[:, 0:128], bg[:, cs:cs + Q], c_id[:])] + [
                        t.transpose(tp[:, 128 + h * 64: 128 + (h + 1) * 64],
                                    xg[0:64, h, cs:cs + Q], c_id[0:64, 0:64])
                        for h in range(8)])(cs, tp),
                    deps=[lds, consts, prev.get(('tpr', gc % 2))])
                trev = em.v(
                    (lambda tp: lambda v: [
                        v.tensor_copy(out=bt16[:], in_=tp[:, 0:128])] + [
                        v.tensor_copy(out=xt16[:, h, :],
                                      in_=tp[:, 128 + h * 64:128 + (h + 1) * 64])
                        for h in range(8)])(tp),
                    deps=[trm, prev.get('dk3'), prev.get('ymm_last')])
                prev[('tpr', gc % 2)] = trev
                dg = em.v(
                    (lambda cs, gc: lambda v: v.tensor_scalar(
                        out=diag8[:, (gc % 2) * 8:(gc % 2) * 8 + 8], in0=c_i8[:],
                        scalar1=cumx[:, cs + Q:cs + Q + 1], scalar2=None,
                        op0=ALU.mult))(cs, gc),
                    deps=[cum, consts, prev.get(('dgr', gc % 2))])
                cbm = em.t(
                    (lambda cs, pcb, gc: lambda t: [
                        t.matmul(pcb[:, 0:128], bg[:, cs:cs + Q], cg[:, cs:cs + Q],
                                 start=True, stop=True),
                        t.matmul(pcb[:, 128:136], cumx[:, cs + 1:cs + 1 + Q], c_i8[:],
                                 start=True, stop=True),
                        t.matmul(pcb[:, 136:144], dt8[:, cs:cs + Q], c_i8[:],
                                 start=True, stop=True),
                        t.matmul(pcb[:, 144:152], c_on8[:],
                                 diag8[:, (gc % 2) * 8:(gc % 2) * 8 + 8],
                                 start=True, stop=True)])(cs, pcb, gc),
                    deps=[cum, dg, prev.get(('cbr', gc % 2))])
                prev[('dgr', gc % 2)] = cbm
                wc0 = em.s(
                    (lambda pcb: lambda sc: sc.copy(out=cds[:], in_=pcb[:, 128:152]))(pcb),
                    deps=[cbm, prev.get('wcr')])
                wc = em.v(
                    lambda v: v.tensor_tensor(out=wca[:], in0=cds[:, 16:24],
                                              in1=cds[:, 0:8], op=ALU.subtract),
                    deps=[wc0])
                wc2 = em.s(lambda sc: sc.activation(out=wcb[:], in_=wca[:],
                                                    func=AF.Exp), deps=[wc])
                wc3 = em.v(lambda v: v.tensor_tensor(out=wcols[:], in0=wcb[:],
                                                     in1=cds[:, 8:16], op=ALU.mult),
                           deps=[wc2, prev.get('wcolr')])
                zunits = []
                for h in range(8):
                    ptr = ps_tr[h % 2]
                    tr = em.t(
                        (lambda h, cs, ptr: lambda t: t.matmul(
                            ptr[:, 0:129], c_sel[:, h * 128:(h + 1) * 128],
                            cumx[:, cs:cs + 129], start=True, stop=True))(h, cs, ptr),
                        deps=[cum, prev.get(('trr', h % 2)), consts])
                    zu = em.v(
                        (lambda h, ptr: lambda v: [
                            v.tensor_copy(out=pcol[:], in_=ptr[:, 0:1]),
                            v.tensor_scalar(out=zb[:, h, :], in0=ptr[:, 1:129],
                                            scalar1=cds[:, h:h + 1], scalar2=None,
                                            op0=ALU.subtract),
                            v.tensor_scalar(out=z0b[:, h, :], in0=ptr[:, 1:129],
                                            scalar1=pcol[:, 0:1], scalar2=None,
                                            op0=ALU.subtract)])(h, ptr),
                        deps=[tr, wc, prev.get('dk2')])
                    prev[('trr', h % 2)] = zu
                    zunits.append(zu)
                dk1 = em.v(lambda v: v.tensor_scalar(
                    out=zb[:].rearrange("p h q -> p (h q)"),
                    in0=zb[:].rearrange("p h q -> p (h q)"),
                    scalar1=0.0, scalar2=None, op0=ALU.min), deps=zunits)
                dk2 = em.s(lambda sc: [
                    sc.activation(out=zb[:].rearrange("p h q -> p (h q)"),
                                  in_=zb[:].rearrange("p h q -> p (h q)"),
                                  func=AF.Exp),
                    sc.activation(out=z0b[:].rearrange("p h q -> p (h q)"),
                                  in_=z0b[:].rearrange("p h q -> p (h q)"),
                                  func=AF.Exp)],
                    deps=[dk1])
                prev['dk2'] = dk2
                dk3 = em.v(
                    (lambda pcb, cs: lambda v: [
                        v.tensor_tensor(out=zb[:], in0=zb[:],
                                        in1=c_tri[:].rearrange("p q -> p () q")
                                        .broadcast_to([128, 8, Q]),
                                        op=ALU.mult),
                        v.tensor_tensor(out=zb[:], in0=zb[:],
                                        in1=pcb[:, 0:128].rearrange("p q -> p () q")
                                        .broadcast_to([128, 8, Q]),
                                        op=ALU.mult),
                        v.tensor_tensor(out=st16[:].rearrange("p (h q) -> p h q", q=Q),
                                        in0=zb[:],
                                        in1=cds[:, 8:16].rearrange("p h -> p h ()")
                                        .broadcast_to([128, 8, Q]),
                                        op=ALU.mult),
                        v.tensor_tensor(out=cl16[:].rearrange("p (h q) -> p h q", q=Q),
                                        in0=z0b[:],
                                        in1=cg[:, cs:cs + Q].rearrange("p q -> p () q")
                                        .broadcast_to([128, 8, Q]),
                                        op=ALU.mult),
                        v.tensor_tensor(out=bw16[:],
                                        in0=bt16[:].rearrange("p n -> p () n")
                                        .broadcast_to([128, 8, 128]),
                                        in1=wcols[:].rearrange("p h -> p h ()")
                                        .broadcast_to([128, 8, 128]),
                                        op=ALU.mult)])(pcb, cs),
                    deps=[dk2, wc3, trev, prev.get('ymm_last2')])
                prev[('cbr', gc % 2)] = dk3
                prev['dk3'] = dk3
                prev['wcolr'] = dk3
                prev['wcr'] = dk3
                htc = em.v(lambda v: v.tensor_copy(out=ht16[:], in_=ht[:]),
                           deps=[prev.get('htu_last', init), prev.get('yil')])
                ytu = None
                htu = None
                ymm = None
                for h in range(8):
                    pyc = ps_yc[h % 2]
                    ymm = em.t(
                        (lambda h, gc, cs, pyc: lambda t: [
                            t.matmul(pyc[0:64, 0:128], xt16[:, h, :],
                                     st16[:, h * Q:(h + 1) * Q],
                                     start=True, stop=(gc == 0))] + ([
                            t.matmul(pyc[0:64, 0:128],
                                     ht16[:, h * 64:(h + 1) * 64],
                                     cl16[:, h * Q:(h + 1) * Q],
                                     start=False, stop=True)] if gc > 0 else []) + [
                            t.matmul(pyc[:, 128:192], bw16[:, h, :], xt16[:, h, :],
                                     start=True, stop=True)])(h, gc, cs, pyc),
                        deps=[dk3, htc, trev, prev.get(('ycr', h % 2))])
                    ytu = em.v(
                        (lambda h, cs, pyc: lambda v: v.scalar_tensor_tensor(
                            out=yg[0:64, h, cs:cs + Q],
                            in0=xg[0:64, h, cs:cs + Q],
                            scalar=c_dgb[0:64, h:h + 1],
                            in1=pyc[0:64, 0:128],
                            op0=ALU.mult, op1=ALU.add))(h, cs, pyc),
                        deps=[ymm, consts])
                    htu = em.v(
                        (lambda h, pyc: lambda v: v.scalar_tensor_tensor(
                            out=ht[:, h * 64:(h + 1) * 64],
                            in0=ht[:, h * 64:(h + 1) * 64],
                            scalar=z0b[:, h, Q - 1:Q],
                            in1=pyc[:, 128:192],
                            op0=ALU.mult, op1=ALU.add))(h, pyc),
                        deps=[ymm, htc, init])
                    prev[('ycr', h % 2)] = htu
                prev['htu_last'] = htu
                prev['yil'] = htu
                prev['ymm_last'] = ymm
                prev['ymm_last2'] = ymm
                prev['str'] = ytu
                prev['lastcompute'] = htu
                gc += 1
            ydma = em.sync_dma(
                (lambda h2: lambda sy: sy.dma_start(
                    out=yg_h.ap().rearrange("hh p t -> p hh t")
                    [:, :, h2 * TTE:(h2 + 1) * TTE], in_=yg[:]))(h2),
                deps=[prev['str'], prev['htu_last']], n=1)
            prev['ydma'] = ydma
        em.g(lambda g: g.engine_nop(), deps=[prev['ydma']])
        em.realize(block, sems, dma_sems)
    return nc


def host_inputs_E(resD, inputs):
    xbc_all = np.concatenate(
        [np.asarray(r["xbc16"]).view(BF).reshape(CONVD, TCK) for r in resD], axis=1)
    dt_all = np.concatenate([np.asarray(r["dt2"]) for r in resD], axis=1)
    a2_all = np.concatenate([np.asarray(r["a2"]) for r in resD], axis=1)
    sel = np.zeros((8, 8, 128), np.float32)
    for h in range(8):
        sel[h, h, :] = 1.0
    dvec = np.asarray(inputs["L1_attn_D"])
    maps = []
    for g in range(NC):
        maps.append({
            "xg16": np.ascontiguousarray(
                xbc_all[g * 512:(g + 1) * 512]).reshape(8, 64, S),
            "bg16": np.ascontiguousarray(xbc_all[4096 + g * 128: 4096 + (g + 1) * 128]),
            "cg16": np.ascontiguousarray(xbc_all[5120 + g * 128: 5120 + (g + 1) * 128]),
            "ag8": np.ascontiguousarray(a2_all[g * 8:(g + 1) * 8]),
            "dt8": np.ascontiguousarray(dt_all[g * 8:(g + 1) * 8]),
            "sel8": np.ascontiguousarray(sel.transpose(1, 0, 2).reshape(8, 8 * 128)),
            "i8": np.eye(8, dtype=np.float32),
            "ones8": np.ones((8, 128), np.float32),
            "tri": np.triu(np.ones((128, 128), np.float32)),
            "dgb": np.tile(dvec[g * 8:(g + 1) * 8], (64, 1)).astype(np.float32),
            "ident16": np.eye(128, dtype=np.float32).astype(BF),
        })
    return maps


def run_E(resD, inputs):
    nc = build_E()
    maps = host_inputs_E(resD, inputs)
    res = run_bass_kernel_spmd(nc, maps, core_ids=list(range(NC)))
    return res.results



def build_G():
    nc = bass.Bass("TRN2", target_bir_lowering=False, debug=False)
    gpk_h = nc.dram_tensor("gup_pk", [11, 128, 256], I32, kind="ExternalInput")
    gsc_h = nc.dram_tensor("gup_sc", [11, 128, 16], F16, kind="ExternalInput")
    dpk_h = nc.dram_tensor("dn_pk", [2, 128, 704], I32, kind="ExternalInput")
    dsc_h = nc.dram_tensor("dn_sc", [2, 128, 44], F16, kind="ExternalInput")
    id_h = nc.dram_tensor("ident16", [128, 128], BF16, kind="ExternalInput")
    wgu_h = nc.dram_tensor("wguT", [16, 128, 1408], BF16, kind="ExternalOutput")
    wdn_h = nc.dram_tensor("wdnT", [44, 128, 256], BF16, kind="ExternalOutput")

    st = ExitStack()
    with st:
        def SB(name, shape, dt):
            return st.enter_context(nc.sbuf_tensor(name, shape, dt))
        c_id = SB("c_id", [128, 128], BF16)
        pk = SB("pk", [128, 704], I32)
        sc16 = SB("sc16", [128, 44], F16)
        sc32 = SB("sc32", [128, 44], F32)
        t1 = SB("t1", [128, 704], I32)
        fव = SB("fv", [128, 704], F32)
        tv = SB("tv", [128, 704], F32)
        mv = SB("mv", [128, 704], F32)
        sg = SB("sg", [128, 704], F32)
        rv = SB("rv", [128, 704], F32)
        bv = SB("bv", [128, 704], F32)
        av = SB("av", [128, 704], F32)
        wrow = SB("wrow", [128, 5632], BF16)
        trev_t = [SB("trev_t0", [128, 128], BF16), SB("trev_t1", [128, 128], BF16),
                  SB("trev_t2", [128, 128], BF16), SB("trev_t3", [128, 128], BF16)]
        ps_tp = [st.enter_context(nc.psum_tensor("pstp0", [128, 1024], BF16)),
                 st.enter_context(nc.psum_tensor("pstp1", [128, 1024], BF16))]
        sems = {e: st.enter_context(nc.semaphore("sem_" + e)) for e in Emitter.ENGS}
        dma_sems = [st.enter_context(nc.semaphore(f"dsem{i}")) for i in range(Emitter.NPOOL)]
        block = st.enter_context(nc.Block())
        em = Emitter(nc)
        fv = fव

        consts = em.sync_dma(lambda sy: sy.dma_start(out=c_id[:], in_=id_h.ap()))
        outs = []
        prev = {}
        for part, (nrc, nwords, ngrp, pk_ap, sc_ap, out_ap) in enumerate([
                (11, 256, 16, gpk_h, gsc_h, wgu_h), (2, 704, 44, dpk_h, dsc_h, wdn_h)]):
            nk = nwords * 8 // 128
            for rc in range(nrc):
                ld = em.sync_dma(
                    (lambda rc, pk_ap, sc_ap, nwords, ngrp: lambda sy: [
                        sy.dma_start(out=pk[:, 0:nwords], in_=pk_ap.ap()[rc]),
                        sy.dma_start(out=sc16[:, 0:ngrp], in_=sc_ap.ap()[rc])])(
                            rc, pk_ap, sc_ap, nwords, ngrp),
                    deps=[prev.get('lastk')], n=2)
                scc = em.s(
                    (lambda ngrp: lambda sc: sc.copy(out=sc32[:, 0:ngrp],
                                                     in_=sc16[:, 0:ngrp]))(ngrp),
                    deps=[ld])
                kunits = []
                for k in range(8):
                    u = em.v(
                        (lambda k, nwords, ngrp: lambda v: [
                            v.tensor_scalar(out=t1[:, 0:nwords], in0=pk[:, 0:nwords],
                                            scalar1=4 * k, scalar2=15,
                                            op0=ALU.logical_shift_right,
                                            op1=ALU.bitwise_and),
                            v.tensor_copy(out=fv[:, 0:nwords], in_=t1[:, 0:nwords]),
                            v.tensor_single_scalar(out=tv[:, 0:nwords],
                                                   in_=fv[:, 0:nwords],
                                                   scalar=7.5, op=ALU.is_ge),
                            v.scalar_tensor_tensor(out=mv[:, 0:nwords],
                                                   in0=tv[:, 0:nwords], scalar=-8.0,
                                                   in1=fv[:, 0:nwords],
                                                   op0=ALU.mult, op1=ALU.add),
                            v.tensor_scalar(out=sg[:, 0:nwords], in0=tv[:, 0:nwords],
                                            scalar1=-2.0, scalar2=1.0,
                                            op0=ALU.mult, op1=ALU.add)])(k, nwords, ngrp),
                        deps=[ld, prev.get('kread')])
                    ru = em.v(
                        (lambda nwords: lambda v: v.tensor_scalar(
                            out=rv[:, 0:nwords], in0=mv[:, 0:nwords],
                            scalar1=-6.0, scalar2=0.0,
                            op0=ALU.add, op1=ALU.max))(nwords),
                        deps=[u])
                    u2 = em.v(
                        (lambda k, nwords, ngrp: lambda v: [
                            v.scalar_tensor_tensor(out=bv[:, 0:nwords],
                                                   in0=mv[:, 0:nwords], scalar=2.0,
                                                   in1=rv[:, 0:nwords],
                                                   op0=ALU.subtract, op1=ALU.add),
                            v.tensor_scalar(out=av[:, 0:nwords], in0=mv[:, 0:nwords],
                                            scalar1=0.5, scalar2=None, op0=ALU.mult),
                            v.tensor_tensor(out=bv[:, 0:nwords], in0=av[:, 0:nwords],
                                            in1=bv[:, 0:nwords], op=ALU.max),
                            v.tensor_tensor(out=bv[:, 0:nwords], in0=bv[:, 0:nwords],
                                            in1=sg[:, 0:nwords], op=ALU.mult),
                            v.tensor_tensor(
                                out=wrow[:].rearrange("p (j k8) -> p j k8", k8=8)
                                [:, 0:nwords, k].rearrange("p (g j2) -> p g j2", j2=16),
                                in0=bv[:, 0:nwords].rearrange("p (g j2) -> p g j2", j2=16),
                                in1=sc32[:, 0:ngrp].rearrange("p g -> p g 1")
                                if False else
                                sc32[:, 0:ngrp].broadcast_to([128, ngrp, 16]),
                                op=ALU.mult)])(k, nwords, ngrp),
                        deps=[ru, scc, prev.get('wrowread')])
                    kunits.append(u2)
                prev['kread'] = kunits[-1]
                prev['lastk'] = kunits[-1]
                trus = []
                tr_mm = []
                for kc in range(nk):
                    tp = ps_tp[kc % 2]
                    tb = trev_t[kc % 4]
                    tru = em.t(
                        (lambda kc, tp: lambda t: t.transpose(
                            tp[:, 0:128], wrow[:, kc * 128:(kc + 1) * 128],
                            c_id[:]))(kc, tp),
                        deps=kunits + [consts, prev.get(('tprd', kc % 2))])
                    tr_mm.append(tru)
                    tre = em.v(
                        (lambda tp, tb: lambda v: v.tensor_copy(out=tb[:],
                                                                in_=tp[:, 0:128]))(tp, tb),
                        deps=[tru, prev.get(('trd', kc % 4))])
                    prev[('tprd', kc % 2)] = tre
                    od = em.sync_dma(
                        (lambda rc, kc, out_ap, tb: lambda sy: sy.dma_start(
                            out=out_ap.ap()[kc][:, rc * 128:(rc + 1) * 128],
                            in_=tb[:]))(rc, kc, out_ap, tb),
                        deps=[tre])
                    prev[('trd', kc % 4)] = od
                    trus.append(od)
                    outs.append(od)
                prev['wrowread'] = tr_mm[-1]
        em.g(lambda g: g.engine_nop(), deps=outs[-12:])
        em.realize(block, sems, dma_sems)
    return nc


def host_inputs_G(inputs):
    gpk = np.asarray(inputs["L1_ffn_gate_up_packed"])
    gsc = np.asarray(inputs["L1_ffn_gate_up_scales"])
    dpk = np.asarray(inputs["L1_ffn_down_packed"])
    dsc = np.asarray(inputs["L1_ffn_down_scales"])
    ident = np.eye(128, dtype=np.float32).astype(BF)
    maps = []
    for c in range(NC):
        g_sl = slice(c * 1408, (c + 1) * 1408)
        d_sl = slice(c * 256, (c + 1) * 256)
        maps.append({
            "gup_pk": np.ascontiguousarray(gpk[g_sl]).reshape(11, 128, 256),
            "gup_sc": np.ascontiguousarray(gsc[:, g_sl].T).reshape(11, 128, 16),
            "dn_pk": np.ascontiguousarray(dpk[d_sl]).reshape(2, 128, 704),
            "dn_sc": np.ascontiguousarray(dsc[:, d_sl].T).reshape(2, 128, 44),
            "ident16": ident,
        })
    return maps


def run_G(inputs):
    nc = build_G()
    maps = host_inputs_G(inputs)
    res = run_bass_kernel_spmd(nc, maps, core_ids=list(range(NC)))
    return res.results



def build_F1():
    nc = bass.Bass("TRN2", target_bir_lowering=False, debug=False)
    so_h = nc.dram_tensor("so16", [32, 128, TCK], BF16, kind="ExternalInput")
    ga_h = nc.dram_tensor("gate16", [32, 128, TCK], BF16, kind="ExternalInput")
    x2_h = nc.dram_tensor("x2T", [16, 128, TCK], F32, kind="ExternalInput")
    nw_h = nc.dram_tensor("nw", [128, 32], F32, kind="ExternalInput")
    wout_h = nc.dram_tensor("wout2_t", [16, 32, 128, 128], F32, kind="ExternalInput")
    onesc_h = nc.dram_tensor("onesc", [128, 1], F32, kind="ExternalInput")
    onesr_h = nc.dram_tensor("onesr", [1, 128], F32, kind="ExternalInput")
    x3_h = nc.dram_tensor("x3", [16, 128, TCK], F32, kind="ExternalOutput")
    xn3_h = nc.dram_tensor("xn3", [16, 128, TCK], BF16, kind="ExternalOutput")

    st = ExitStack()
    with st:
        def SB(name, shape, dt):
            return st.enter_context(nc.sbuf_tensor(name, shape, dt))

        def PSUM(name, shape):
            return st.enter_context(nc.psum_tensor(name, shape, F32))
        c_onesc = SB("c_onesc", [128, 1], F32)
        c_onesr = SB("c_onesr", [1, 128], F32)
        c_nw = SB("c_nw", [128, 32], F32)
        sbuf_a = [SB("sba0", [128, TCK], BF16), SB("sba1", [128, TCK], BF16)]
        sbuf_b = [SB("sbb0", [128, TCK], BF16), SB("sbb1", [128, TCK], BF16)]
        g16 = SB("g16", [128, 32, TCK], BF16)
        xs_ = SB("x3pre", [128, 16, TCK], F32)
        wbuf = [SB("wbuf0", [128, 32 * 128], BF16), SB("wbuf1", [128, 32 * 128], BF16)]
        sq = SB("sq", [128, TCK], F32)
        rows = SB("rows", [1, 4 * TCK], F32)
        rrow2 = SB("rrow2", [1, TCK], F32)
        mean_b = SB("mean_b", [128, TCK], F32)
        rstd_b = SB("rstd_b", [128, TCK], F32)
        rstd2_b = SB("rstd2_b", [128, TCK], F32)
        xn3b = SB("xn3b", [128, TCK], BF16)
        psmm = Slots([PSUM(f"ps_m{i}", [128, 512]) for i in range(4)])
        ps_st = [PSUM(f"ps_s{i}", [1, 512]) for i in range(4)]
        sems = {e: st.enter_context(nc.semaphore("sem_" + e)) for e in Emitter.ENGS}
        dma_sems = [st.enter_context(nc.semaphore(f"dsem{i}")) for i in range(Emitter.NPOOL)]
        block = st.enter_context(nc.Block())
        em = Emitter(nc)

        consts = em.sync_dma(lambda sy: [
            sy.dma_start(out=c_onesc[:], in_=onesc_h.ap()),
            sy.dma_start(out=c_onesr[:], in_=onesr_h.ap()),
            sy.dma_start(out=c_nw[:], in_=nw_h.ap())], n=3)
        ldx = em.sync_dma(lambda sy: sy.dma_start(
            out=xs_[:], in_=x2_h.ap().rearrange("kc p t -> p kc t")))

        gat = [None] * 32
        sq_st = []
        for dc in range(32):
            ld = em.sync_dma(
                (lambda dc: lambda sy: [
                    sy.dma_start(out=sbuf_a[dc % 2][:], in_=so_h.ap()[dc]),
                    sy.dma_start(out=sbuf_b[dc % 2][:], in_=ga_h.ap()[dc])])(dc),
                deps=[gat[dc - 2] if dc >= 2 else None], n=2)
            gat[dc] = em.v(
                (lambda dc: lambda v: v.tensor_tensor(
                    out=g16[:, dc, :], in0=sbuf_a[dc % 2][:], in1=sbuf_b[dc % 2][:],
                    op=ALU.mult))(dc),
                deps=[ld])
            sqd = em.s(
                (lambda dc: lambda sc: sc.activation(
                    out=sq[:], in_=g16[:, dc, :], func=AF.Square))(dc),
                deps=[gat[dc], sq_st[-1] if sq_st else None])
            sq_st.append(em.t(
                (lambda dc: lambda t: [
                    t.matmul(ps_st[0][:, 0:512], c_onesc[:], sq[:, 0:512],
                             start=(dc == 0), stop=(dc == 31)),
                    t.matmul(ps_st[1][:, 0:512], c_onesc[:], sq[:, 512:1024],
                             start=(dc == 0), stop=(dc == 31))])(dc),
                deps=[sqd, consts]))
        g1c = em.s(lambda sc: [
            sc.copy(out=rrow2[:, 0:512], in_=ps_st[0][:, 0:512]),
            sc.copy(out=rrow2[:, 512:1024], in_=ps_st[1][:, 0:512])],
            deps=[sq_st[31]])
        g1 = em.v(lambda v: v.tensor_scalar(out=rrow2[:], in0=rrow2[:],
                                            scalar1=1.0 / INTER, scalar2=EPS5,
                                            op0=ALU.mult, op1=ALU.add), deps=[g1c])
        g2 = em.s(lambda sc: sc.activation(out=rrow2[:], in_=rrow2[:], func=AF.Sqrt),
                  deps=[g1])
        g3 = em.v(lambda v: v.reciprocal(out=rrow2[:], in_=rrow2[:]), deps=[g2])
        bc2 = [psmm.next(), psmm.next()]
        g4 = em.t(lambda t: [
            t.matmul(psmm.tile(bc2[0])[:, 0:512], c_onesr[:], rrow2[:, 0:512],
                     start=True, stop=True),
            t.matmul(psmm.tile(bc2[1])[:, 0:512], c_onesr[:], rrow2[:, 512:1024],
                     start=True, stop=True)], deps=[g3, consts])
        g5 = em.s(lambda sc: [
            sc.copy(out=rstd2_b[:, 0:512], in_=psmm.tile(bc2[0])[:, 0:512]),
            sc.copy(out=rstd2_b[:, 512:1024], in_=psmm.tile(bc2[1])[:, 0:512])],
            deps=[g4])
        for s_ in bc2:
            psmm.reader[s_] = g5
        norm_done = [None] * 32
        for dc in range(32):
            norm_done[dc] = em.v(
                (lambda dc: lambda v: v.scalar_tensor_tensor(
                    out=g16[:, dc, :], in0=g16[:, dc, :],
                    scalar=c_nw[:, dc:dc + 1], in1=rstd2_b[:],
                    op0=ALU.mult, op1=ALU.mult))(dc),
                deps=[g5, gat[31]])

        ldw = [None] * 16
        stat_units = []
        ev_units = []
        for mc in range(16):
            ldw[mc] = em.gp_dma(
                (lambda mc: lambda g: g.dma_start(
                    out=wbuf[mc % 2][:].rearrange("p (kc j) -> p kc j", j=128),
                    in_=wout_h.ap()[mc].rearrange("kc p j -> p kc j")))(mc),
                deps=[ev_units[-3] if mc >= 2 else None])
            for i, (o, w) in enumerate(NCP_TC):
                s_ = psmm.next()
                mm = em.t(
                    (lambda mc, o, w, s_: lambda t: [
                        t.matmul(psmm.tile(s_)[:, 0:w],
                                 wbuf[mc % 2][:, kc * 128:(kc + 1) * 128],
                                 g16[:, kc, o:o + w],
                                 start=(kc == 0), stop=(kc == 31))
                        for kc in range(32)])(mc, o, w, s_),
                    deps=[ldw[mc], psmm.reader[s_]] + (norm_done if mc == 0 else []))
                ev = em.v(
                    (lambda mc, o, w, s_: lambda v: v.tensor_tensor(
                        out=xs_[:, mc, o:o + w], in0=xs_[:, mc, o:o + w],
                        in1=psmm.tile(s_)[:, 0:w], op=ALU.add))(mc, o, w, s_),
                    deps=[mm, ldx])
                psmm.reader[s_] = ev
                ev_units.append(ev)
            sqd = em.s(
                (lambda mc: lambda sc: sc.activation(
                    out=sq[:], in_=xs_[:, mc, :], func=AF.Square))(mc),
                deps=[ev_units[-1], ev_units[-2],
                      stat_units[-1] if stat_units else None])
            stat_units.append(em.t(
                (lambda mc: lambda t: [
                    t.matmul(ps_st[2][:, 0:512], c_onesc[:], xs_[:, mc, 0:512],
                             start=(mc == 0), stop=(mc == 15)),
                    t.matmul(ps_st[3][:, 0:512], c_onesc[:], xs_[:, mc, 512:1024],
                             start=(mc == 0), stop=(mc == 15)),
                    t.matmul(ps_st[0][:, 0:512], c_onesc[:], sq[:, 0:512],
                             start=(mc == 0), stop=(mc == 15)),
                    t.matmul(ps_st[1][:, 0:512], c_onesc[:], sq[:, 512:1024],
                             start=(mc == 0), stop=(mc == 15))])(mc),
                deps=[sqd, consts, sq_st[31] if mc == 0 else None]))

        rrc = em.s(lambda sc: [
            sc.copy(out=rows[:, 0:512], in_=ps_st[2][:, 0:512]),
            sc.copy(out=rows[:, 512:1024], in_=ps_st[3][:, 0:512]),
            sc.copy(out=rows[:, 1024:1536], in_=ps_st[0][:, 0:512]),
            sc.copy(out=rows[:, 1536:2048], in_=ps_st[1][:, 0:512])],
            deps=[stat_units[15]])
        rr = em.v(lambda v: [
            v.tensor_scalar(out=rows[:, 2048:3072], in0=rows[:, 0:1024],
                            scalar1=1.0 / DM, scalar2=None, op0=ALU.mult),
            v.tensor_scalar(out=rows[:, 1024:2048], in0=rows[:, 1024:2048],
                            scalar1=1.0 / DM, scalar2=None, op0=ALU.mult),
            v.tensor_tensor(out=rows[:, 0:1024], in0=rows[:, 2048:3072],
                            in1=rows[:, 2048:3072], op=ALU.mult),
            v.tensor_tensor(out=rows[:, 3072:4096], in0=rows[:, 1024:2048],
                            in1=rows[:, 0:1024], op=ALU.subtract),
            v.tensor_scalar(out=rows[:, 3072:4096], in0=rows[:, 3072:4096],
                            scalar1=EPS5, scalar2=None, op0=ALU.add)],
            deps=[rrc])
        r2 = em.s(lambda sc: sc.activation(out=rows[:, 3072:4096],
                                           in_=rows[:, 3072:4096], func=AF.Sqrt),
                  deps=[rr])
        r3 = em.v(lambda v: v.reciprocal(out=rows[:, 3072:4096],
                                         in_=rows[:, 3072:4096]), deps=[r2])
        bcs = [psmm.next() for _ in range(4)]
        r4 = em.t(lambda t: [
            t.matmul(psmm.tile(bcs[0])[:, 0:512], c_onesr[:], rows[:, 2048:2560],
                     start=True, stop=True),
            t.matmul(psmm.tile(bcs[1])[:, 0:512], c_onesr[:], rows[:, 2560:3072],
                     start=True, stop=True),
            t.matmul(psmm.tile(bcs[2])[:, 0:512], c_onesr[:], rows[:, 3072:3584],
                     start=True, stop=True),
            t.matmul(psmm.tile(bcs[3])[:, 0:512], c_onesr[:], rows[:, 3584:4096],
                     start=True, stop=True)],
            deps=[r3] + [psmm.reader[s_] for s_ in bcs])
        r5 = em.s(lambda sc: [
            sc.copy(out=mean_b[:, 0:512], in_=psmm.tile(bcs[0])[:, 0:512]),
            sc.copy(out=mean_b[:, 512:1024], in_=psmm.tile(bcs[1])[:, 0:512]),
            sc.copy(out=rstd_b[:, 0:512], in_=psmm.tile(bcs[2])[:, 0:512]),
            sc.copy(out=rstd_b[:, 512:1024], in_=psmm.tile(bcs[3])[:, 0:512])],
            deps=[r4])
        for s_ in bcs:
            psmm.reader[s_] = r5
        outs = []
        for mc in range(16):
            ap1 = em.v(
                (lambda mc: lambda v: [
                    v.tensor_tensor(out=xs_[:, mc, :], in0=xs_[:, mc, :],
                                    in1=mean_b[:], op=ALU.subtract),
                    v.tensor_tensor(out=xs_[:, mc, :], in0=xs_[:, mc, :],
                                    in1=rstd_b[:], op=ALU.mult)])(mc),
                deps=[r5, stat_units[15]])
            cp = em.s(
                (lambda mc: lambda sc: sc.copy(out=xn3b[:], in_=xs_[:, mc, :]))(mc),
                deps=[ap1, outs[-1] if mc >= 1 else None])
            o1 = em.sync_dma(
                (lambda mc: lambda sy: [
                    sy.dma_start(out=x3_h.ap()[mc], in_=xs_[:, mc, :]),
                    sy.dma_start(out=xn3_h.ap()[mc], in_=xn3b[:])])(mc),
                deps=[cp], n=2)
            outs.append(o1)
        em.g(lambda g: g.engine_nop(), deps=outs)
        em.realize(block, sems, dma_sems)
    return nc


def host_inputs_F1(resD, resE, resC2, inputs):
    y_all = np.concatenate(
        [np.asarray(r["yg16"]).view(BF).reshape(512, S) for r in resE], axis=0)
    wout = lhsT_tiles(np.asarray(inputs["L1_attn_Wout"]))
    nw = percol(np.asarray(inputs["L1_attn_norm_w"]))
    maps = []
    for c in range(NC):
        cols = slice(c * TCK, (c + 1) * TCK)
        maps.append({
            "so16": np.ascontiguousarray(y_all[:, cols]).reshape(32, 128, TCK),
            "gate16": np.asarray(resD[c]["gate16"]).view(BF).reshape(32, 128, TCK),
            "x2T": np.asarray(resC2[c]["x2"]).reshape(16, 128, TCK),
            "nw": nw,
            "wout2_t": wout,
            "onesc": np.ones((128, 1), np.float32),
            "onesr": np.ones((1, 128), np.float32),
        })
    return maps


def run_F1(resD, resE, resC2, inputs):
    nc = build_F1()
    maps = host_inputs_F1(resD, resE, resC2, inputs)
    res = run_bass_kernel_spmd(nc, maps, core_ids=list(range(NC)))
    return res.results



def build_F2():
    nc = bass.Bass("TRN2", target_bir_lowering=False, debug=False)
    xn3_h = nc.dram_tensor("xn3", [16, 128, TCK], BF16, kind="ExternalInput")
    x3_h = nc.dram_tensor("x3", [16, 128, TCK], F32, kind="ExternalInput")
    wgu_h = nc.dram_tensor("wgu_t", [88, 16, 128, 128], BF16, kind="ExternalInput")
    wdn_h = nc.dram_tensor("wdn_t", [16, 44, 128, 128], BF16, kind="ExternalInput")
    onesc_h = nc.dram_tensor("onesc", [128, 1], F32, kind="ExternalInput")
    onesr_h = nc.dram_tensor("onesr", [1, 128], F32, kind="ExternalInput")
    out_h = nc.dram_tensor("outp", [16, 128, TCK], F32, kind="ExternalOutput")

    TT = 512
    st = ExitStack()
    with st:
        def SB(name, shape, dt):
            return st.enter_context(nc.sbuf_tensor(name, shape, dt))

        def PSUM(name, shape):
            return st.enter_context(nc.psum_tensor(name, shape, F32))
        c_onesc = SB("c_onesc", [128, 1], F32)
        c_onesr = SB("c_onesr", [1, 128], F32)
        xn3 = SB("xn3s", [128, 16, TCK], BF16)
        hT = SB("hT", [128, 44, TT], BF16)
        gs = SB("gs", [128, TT], BF16)
        wbuf = [SB("wbuf0", [128, 16 * 128], BF16), SB("wbuf1", [128, 16 * 128], BF16),
                SB("wbuf2", [128, 16 * 128], BF16), SB("wbuf3", [128, 16 * 128], BF16)]
        wdb = [SB("wdb0", [128, 44 * 128], BF16), SB("wdb1", [128, 44 * 128], BF16)]
        yf = SB("yf", [128, 16, TT], F32)
        x3h = SB("x3h", [128, 16, TT], F32)
        sq = SB("sq", [128, TT], F32)
        rows = SB("rows", [1, 4 * TT], F32)
        rrow2 = SB("rrow2", [1, TT], F32)
        mean_b = SB("mean_b", [128, TT], F32)
        rstd_b = SB("rstd_b", [128, TT], F32)
        rms_b = SB("rms_b", [128, TT], F32)
        psmm = Slots([PSUM(f"ps_m{i}", [128, 512]) for i in range(4)])
        ps_st = [PSUM(f"ps_s{i}", [1, 512]) for i in range(4)]
        sems = {e: st.enter_context(nc.semaphore("sem_" + e)) for e in Emitter.ENGS}
        dma_sems = [st.enter_context(nc.semaphore(f"dsem{i}")) for i in range(Emitter.NPOOL)]
        block = st.enter_context(nc.Block())
        em = Emitter(nc)

        consts = em.sync_dma(lambda sy: [
            sy.dma_start(out=c_onesc[:], in_=onesc_h.ap()),
            sy.dma_start(out=c_onesr[:], in_=onesr_h.ap())], n=2)
        ldxn = em.sync_dma(lambda sy: sy.dma_start(
            out=xn3[:], in_=xn3_h.ap().rearrange("kc p t -> p kc t")))

        outs_all = []
        prev = {}
        for th in range(2):
            ldx3 = em.sync_dma(
                (lambda th: lambda sy: sy.dma_start(
                    out=x3h[:],
                    in_=x3_h.ap().rearrange("kc p t -> p kc t")
                    [:, :, th * TT:(th + 1) * TT]))(th),
                deps=prev.get('outs', []))
            hdone = []
            for g in range(44):
                ldg = em.gp_dma(
                    (lambda g: lambda gg: gg.dma_start(
                        out=wbuf[g % 2][:].rearrange("p (kc j) -> p kc j", j=128),
                        in_=wgu_h.ap()[g].rearrange("kc p j -> p kc j")))(g),
                    deps=[prev.get(('gmm', g % 2))])
                s_ = psmm.next()
                gmm = em.t(
                    (lambda g, th, s_: lambda t: [
                        t.matmul(psmm.tile(s_)[:, 0:TT],
                                 wbuf[g % 2][:, kc * 128:(kc + 1) * 128],
                                 xn3[:, kc, th * TT:(th + 1) * TT],
                                 start=(kc == 0), stop=(kc == 15))
                        for kc in range(16)])(g, th, s_),
                    deps=[ldg, ldxn, psmm.reader[s_]])
                prev[('gmm', g % 2)] = gmm
                gev = em.s(
                    (lambda s_: lambda sc: sc.activation(
                        out=gs[:], in_=psmm.tile(s_)[:, 0:TT], func=AF.Silu))(s_),
                    deps=[gmm, prev.get('gsr')])
                psmm.reader[s_] = gev
                ldu = em.gp_dma(
                    (lambda g: lambda gg: gg.dma_start(
                        out=wbuf[2 + g % 2][:].rearrange("p (kc j) -> p kc j", j=128),
                        in_=wgu_h.ap()[44 + g].rearrange("kc p j -> p kc j")))(g),
                    deps=[prev.get(('umm', g % 2))])
                s2 = psmm.next()
                umm = em.t(
                    (lambda g, th, s2: lambda t: [
                        t.matmul(psmm.tile(s2)[:, 0:TT],
                                 wbuf[2 + g % 2][:, kc * 128:(kc + 1) * 128],
                                 xn3[:, kc, th * TT:(th + 1) * TT],
                                 start=(kc == 0), stop=(kc == 15))
                        for kc in range(16)])(g, th, s2),
                    deps=[ldu, ldxn, psmm.reader[s2]])
                prev[('umm', g % 2)] = umm
                hu = em.v(
                    (lambda g, s2: lambda v: v.tensor_tensor(
                        out=hT[:, g, :], in0=gs[:], in1=psmm.tile(s2)[:, 0:TT],
                        op=ALU.mult))(g, s2),
                    deps=[umm, gev, prev.get('hTr'), prev.get(('dmm', 0))])
                psmm.reader[s2] = hu
                prev['gsr'] = hu
                hdone.append(hu)
            dev_units = []
            rms_st = []
            for mc in range(16):
                ldd = em.gp_dma(
                    (lambda mc: lambda gg: gg.dma_start(
                        out=wdb[mc % 2][:].rearrange("p (kc j) -> p kc j", j=128),
                        in_=wdn_h.ap()[mc].rearrange("kc p j -> p kc j")))(mc),
                    deps=[prev.get(('dmm', mc % 2))])
                s_ = psmm.next()
                dmm = em.t(
                    (lambda mc, s_: lambda t: [
                        t.matmul(psmm.tile(s_)[:, 0:TT], wdb[mc % 2][:, kc * 128:(kc + 1) * 128],
                                 hT[:, kc, :], start=(kc == 0), stop=(kc == 43))
                        for kc in range(44)])(mc, s_),
                    deps=[ldd, psmm.reader[s_]] + (hdone if mc == 0 else []))
                prev[('dmm', mc % 2)] = dmm
                dev = em.s(
                    (lambda mc, s_: lambda sc: sc.copy(
                        out=yf[:, mc, :], in_=psmm.tile(s_)[:, 0:TT]))(mc, s_),
                    deps=[dmm] + (prev.get('outs', []) if mc == 0 else []))
                psmm.reader[s_] = dev
                dev_units.append(dev)
                sqd = em.v(
                    (lambda mc: lambda v: v.tensor_tensor(
                        out=sq[:], in0=yf[:, mc, :], in1=yf[:, mc, :],
                        op=ALU.mult))(mc),
                    deps=[dev, rms_st[-1] if rms_st else None])
                rms_st.append(em.t(
                    (lambda mc: lambda t: t.matmul(
                        ps_st[0][:, 0:512], c_onesc[:], sq[:],
                        start=(mc == 0), stop=(mc == 15)))(mc),
                    deps=[sqd, consts]))
            q1c = em.s(lambda sc: sc.copy(out=rrow2[:], in_=ps_st[0][:, 0:512]),
                       deps=[rms_st[15]])
            q1 = em.v(lambda v: v.tensor_scalar(out=rrow2[:], in0=rrow2[:],
                                                scalar1=1.0 / DM, scalar2=EPS6,
                                                op0=ALU.mult, op1=ALU.add), deps=[q1c])
            q2 = em.s(lambda sc: sc.activation(out=rrow2[:], in_=rrow2[:],
                                               func=AF.Sqrt), deps=[q1])
            q3 = em.v(lambda v: v.reciprocal(out=rrow2[:], in_=rrow2[:]), deps=[q2])
            bq = psmm.next()
            q4 = em.t(lambda t: t.matmul(psmm.tile(bq)[:, 0:512], c_onesr[:],
                                         rrow2[:], start=True, stop=True),
                      deps=[q3, consts, psmm.reader[bq]])
            q5 = em.s(lambda sc: sc.copy(out=rms_b[:], in_=psmm.tile(bq)[:, 0:512]),
                      deps=[q4])
            psmm.reader[bq] = q5
            st_units = []
            for mc in range(16):
                a1 = em.v(
                    (lambda mc: lambda v: [
                        v.tensor_tensor(out=yf[:, mc, :], in0=yf[:, mc, :],
                                        in1=rms_b[:], op=ALU.mult),
                        v.tensor_tensor(out=yf[:, mc, :], in0=yf[:, mc, :],
                                        in1=x3h[:, mc, :], op=ALU.add)])(mc),
                    deps=[q5, ldx3, rms_st[15]])
                sqd = em.s(
                    (lambda mc: lambda sc: sc.activation(
                        out=sq[:], in_=yf[:, mc, :], func=AF.Square))(mc),
                    deps=[a1, st_units[-1] if st_units else None])
                st_units.append(em.t(
                    (lambda mc: lambda t: [
                        t.matmul(ps_st[1][:, 0:512], c_onesc[:], yf[:, mc, :],
                                 start=(mc == 0), stop=(mc == 15)),
                        t.matmul(ps_st[2][:, 0:512], c_onesc[:], sq[:],
                                 start=(mc == 0), stop=(mc == 15))])(mc),
                    deps=[sqd, consts]))
            rrc = em.s(lambda sc: [
                sc.copy(out=rows[:, 0:512], in_=ps_st[1][:, 0:512]),
                sc.copy(out=rows[:, 512:1024], in_=ps_st[2][:, 0:512])],
                deps=[st_units[15]])
            rr = em.v(lambda v: [
                v.tensor_scalar(out=rows[:, 1024:1536], in0=rows[:, 0:512],
                                scalar1=1.0 / DM, scalar2=None, op0=ALU.mult),
                v.tensor_scalar(out=rows[:, 512:1024], in0=rows[:, 512:1024],
                                scalar1=1.0 / DM, scalar2=None, op0=ALU.mult),
                v.tensor_tensor(out=rows[:, 0:512], in0=rows[:, 1024:1536],
                                in1=rows[:, 1024:1536], op=ALU.mult),
                v.tensor_tensor(out=rows[:, 1536:2048], in0=rows[:, 512:1024],
                                in1=rows[:, 0:512], op=ALU.subtract),
                v.tensor_scalar(out=rows[:, 1536:2048], in0=rows[:, 1536:2048],
                                scalar1=EPS5, scalar2=None, op0=ALU.add)],
                deps=[rrc])
            r2 = em.s(lambda sc: sc.activation(out=rows[:, 1536:2048],
                                               in_=rows[:, 1536:2048], func=AF.Sqrt),
                      deps=[rr])
            r3 = em.v(lambda v: v.reciprocal(out=rows[:, 1536:2048],
                                             in_=rows[:, 1536:2048]), deps=[r2])
            bcs = [psmm.next(), psmm.next()]
            r4 = em.t(lambda t: [
                t.matmul(psmm.tile(bcs[0])[:, 0:512], c_onesr[:], rows[:, 1024:1536],
                         start=True, stop=True),
                t.matmul(psmm.tile(bcs[1])[:, 0:512], c_onesr[:], rows[:, 1536:2048],
                         start=True, stop=True)],
                deps=[r3] + [psmm.reader[s_] for s_ in bcs])
            r5 = em.s(lambda sc: [
                sc.copy(out=mean_b[:], in_=psmm.tile(bcs[0])[:, 0:512]),
                sc.copy(out=rstd_b[:], in_=psmm.tile(bcs[1])[:, 0:512])],
                deps=[r4])
            for s_ in bcs:
                psmm.reader[s_] = r5
            outs = []
            for mc in range(16):
                a2 = em.v(
                    (lambda mc: lambda v: [
                        v.tensor_tensor(out=yf[:, mc, :], in0=yf[:, mc, :],
                                        in1=mean_b[:], op=ALU.subtract),
                        v.tensor_tensor(out=yf[:, mc, :], in0=yf[:, mc, :],
                                        in1=rstd_b[:], op=ALU.mult)])(mc),
                    deps=[r5, st_units[15]])
                o1 = em.sync_dma(
                    (lambda mc, th: lambda sy: sy.dma_start(
                        out=out_h.ap()[mc][:, th * TT:(th + 1) * TT],
                        in_=yf[:, mc, :]))(mc, th),
                    deps=[a2])
                outs.append(o1)
            outs_all += outs
            prev['outs'] = list(outs)
            prev['hTr'] = prev[('dmm', 1)]
        em.g(lambda g: g.engine_nop(), deps=outs_all)
        em.realize(block, sems, dma_sems)
    return nc


def host_inputs_F2(resF1, resG, inputs):
    wgu = np.concatenate(
        [np.asarray(rr["wguT"]).view(BF) for rr in resG], axis=2).reshape(2048, 11264)
    wdn = np.concatenate(
        [np.asarray(rr["wdnT"]).view(BF) for rr in resG], axis=2).reshape(5632, 2048)
    wgu_t = np.ascontiguousarray(
        wgu.reshape(16, 128, 88, 128).transpose(2, 0, 1, 3))
    wdn_t = np.ascontiguousarray(
        wdn.reshape(44, 128, 16, 128).transpose(2, 0, 1, 3))
    maps = []
    for c in range(NC):
        maps.append({
            "xn3": np.asarray(resF1[c]["xn3"]).view(BF).reshape(16, 128, TCK),
            "x3": np.asarray(resF1[c]["x3"]).reshape(16, 128, TCK),
            "wgu_t": wgu_t,
            "wdn_t": wdn_t,
            "onesc": np.ones((128, 1), np.float32),
            "onesr": np.ones((1, 128), np.float32),
        })
    return maps


def run_F2(resF1, resG, inputs):
    nc = build_F2()
    maps = host_inputs_F2(resF1, resG, inputs)
    res = run_bass_kernel_spmd(nc, maps, core_ids=list(range(NC)))
    return res.results
